# version 03
# baseline (speedup 1.0000x reference)
import numpy as np
import ml_dtypes
from contextlib import ExitStack

import concourse.bass as bass
import concourse.mybir as mybir
from concourse.bass_utils import run_bass_kernel_spmd

F32 = mybir.dt.float32
BF16 = mybir.dt.bfloat16
AF = mybir.ActivationFunctionType
ALU = mybir.AluOpType
AX = mybir.AxisListType

NB = 2
S = 2048
DM = 2048
HEADS, KVH, G, DH = 32, 8, 4, 64
CBLK = SBLK = 16
NSEL = 8
WIN = 64
SCALE = DH ** -0.5
CDIM = CBLK * DH
NBLK = S // CBLK
DI = HEADS * DH
DKV = KVH * DH
DFF = 8192
BIGF = 1600.0
NEG30 = -1e30
EPS = 1e-5

ALL_STAGES = ("ln1", "qkv", "rope", "cmp", "attn", "wout", "ffna", "ffnb")


def make_consts():
    c = {}
    ident = np.eye(128, dtype=np.float32)
    c["identF"] = ident
    c["identB"] = ident
    R = np.zeros((DH, DH), np.float32)
    for i in range(DH // 2):
        R[2 * i, 2 * i + 1] = -1.0
        R[2 * i + 1, 2 * i] = 1.0
    R2T = np.zeros((128, 128), np.float32)
    R2T[:64, :64] = R.T
    R2T[64:, 64:] = R.T
    c["R2T"] = R2T
    inv = 1.0 / 10000.0 ** (np.arange(0, DH, 2, dtype=np.float32) / DH)
    ang = np.arange(S, dtype=np.float32)[None, :] * inv[:, None]
    cos = np.repeat(np.cos(ang), 2, axis=0)
    sin = np.repeat(np.sin(ang), 2, axis=0)
    c["C2"] = np.ascontiguousarray(np.concatenate([cos, cos], 0))
    c["S2"] = np.ascontiguousarray(np.concatenate([sin, sin], 0))
    blk = np.arange(NBLK)[:, None]
    qi = np.arange(S)[None, :]
    valid = (16 * (blk + 1) - 1) < qi
    c["cmaskADD"] = np.where(valid, 0.0, -BIGF).astype(np.float32)
    c["cmaskNEG"] = np.where(valid, 0.0, NEG30).astype(np.float32)
    key = np.arange(S)[None, :]
    c["EBIG"] = np.where(np.arange(NBLK)[:, None] == (key // SBLK), BIGF, 0.0).astype(np.float32)
    kk = np.arange(128)[:, None]
    qq = np.arange(128)[None, :]
    own01 = ((qq // 16) == (kk // 16)) & ((qq % 16) >= (kk % 16))
    c["ownADD"] = np.where(own01, BIGF, 0.0).astype(np.float32)
    j = np.arange(WIN)[:, None]
    q = np.arange(WIN)[None, :]
    c["sADDp"] = np.where(q <= j, 0.0, -BIGF).astype(np.float32)
    c["sADDo"] = np.where(j <= q, 0.0, -BIGF).astype(np.float32)
    return c


class Ctr:
    __slots__ = ("sem", "val")

    def __init__(self, sem):
        self.sem = sem
        self.val = 0


def build(dbg=(), stages=ALL_STAGES):
    nc = bass.Bass("TRN2", target_bir_lowering=False)

    def din(name, shape, dtype=F32):
        return nc.dram_tensor(name, list(shape), dtype, kind="ExternalInput").ap()

    def dscr(name, shape, dtype):
        kind = "ExternalOutput" if name in dbg else "Internal"
        return nc.dram_tensor(name, list(shape), dtype, kind=kind).ap()

    x_in = din("x", (NB, S, DM))
    w_qkv = din("w_qkv", (DM, DI + 2 * DKV))
    w_gate = din("w_gate", (DM, 96))
    b_gate = din("b_gate", (96, 1))
    w_out = din("w_out", (DI, DM))
    k_cw1 = din("k_cw1", (CDIM, CDIM))
    k_cb1 = din("k_cb1", (CDIM, 1))
    k_cw2 = din("k_cw2", (CDIM, DH))
    k_cb2 = din("k_cb2", (DH, 1))
    v_cw1 = din("v_cw1", (CDIM, CDIM))
    v_cb1 = din("v_cb1", (CDIM, 1))
    v_cw2 = din("v_cw2", (CDIM, DH))
    v_cb2 = din("v_cb2", (DH, 1))
    kposT = din("kposT", (KVH, DH, CBLK))
    vposT = din("vposT", (KVH, DH, CBLK))
    memckT = din("memckT", (DH, KVH))
    memcv = din("memcv", (KVH, DH))
    ffn_wgT = din("ffn_wgT", (DM, DFF))
    ffn_wuT = din("ffn_wuT", (DM, DFF))
    ffn_wdT = din("ffn_wdT", (DFF, DM))
    identF = din("identF", (128, 128))
    identB_in = din("identB", (128, 128))
    R2T_in = din("R2T", (128, 128))
    C2_in = din("C2", (128, S))
    S2_in = din("S2", (128, S))
    cmaskADD_in = din("cmaskADD", (128, S))
    cmaskNEG_in = din("cmaskNEG", (128, S))
    EBIG_in = din("EBIG", (128, S))
    ownADD_in = din("ownADD", (128, 128))
    sADDp_in = din("sADDp", (64, 64))
    sADDo_in = din("sADDo", (64, 64))

    out = nc.dram_tensor("out", [NB, S, DM], F32, kind="ExternalOutput").ap()
    xlnT = dscr("xlnT", (NB, 16, 128, S), BF16)
    qF = dscr("qF", (NB, 16, 128, S), F32)
    kF = dscr("kF", (NB, 4, 128, S), F32)
    vF = dscr("vF", (NB, 4, 128, S), F32)
    gatesT = dscr("gatesT", (NB, 16, 128, 96), F32)
    rqT = dscr("rqT", (NB, 16, 128, S), BF16)
    rkT = dscr("rkT", (NB, 4, 128, S), BF16)
    kbT = dscr("kbT", (NB, 4, 128, S), BF16)
    vbT = dscr("vbT", (NB, 4, 128, S), BF16)
    ckD = dscr("ckD", (DH, NB * KVH * NBLK), F32)
    cvD = dscr("cvD", (DH, NB * KVH * NBLK), F32)
    attnFM = dscr("attnFM", (NB, 32, 64, S), BF16)
    x1D = dscr("x1D", (NB, S, DM), F32)
    xln2T = dscr("xln2T", (NB, 16, 128, S), BF16)
    gatedD = dscr("gatedD", (64, 128, NB * S), BF16)
    bmD = dscr("bmD", (NB, KVH, 16, 128, 128), BF16)

    ctx = ExitStack()
    blk = ctx.enter_context(nc.Block())
    _sems = {}

    def sem(name):
        if name not in _sems:
            _sems[name] = Ctr(ctx.enter_context(nc.semaphore(name)))
        return _sems[name]

    class RS:
        __slots__ = ("c", "base")

        def __init__(self, c):
            self.c = c
            self.base = c.val

        @property
        def sem(self):
            return self.c.sem

        @property
        def val(self):
            return self.c.val

    def rsem(pool):
        return RS(sem(pool))

    def bump(x, n=1):
        (x.c if isinstance(x, RS) else x).val += n

    def dma(eng, dst, src, ctr):
        tgt = ctr.c if isinstance(ctr, RS) else ctr
        eng.dma_start(dst, src).then_inc(tgt.sem, 16)
        tgt.val += 16

    def full_barrier():
        bar = sem("bar")
        bar.val += 5
        v = bar.val

        @blk.tensor
        def _(t):
            t.sem_inc(bar.sem, 1)
            t.wait_ge(bar.sem, v)

        @blk.vector
        def _(ve):
            ve.sem_inc(bar.sem, 1)
            ve.wait_ge(bar.sem, v)

        @blk.scalar
        def _(s):
            s.sem_inc(bar.sem, 1)
            s.wait_ge(bar.sem, v)

        @blk.gpsimd
        def _(g):
            g.sem_inc(bar.sem, 1)
            g.wait_ge(bar.sem, v)

        @blk.sync
        def _(sy):
            sy.sem_inc(bar.sem, 1)
            sy.wait_ge(bar.sem, v)

    ckT = ctx.enter_context(nc.sbuf_tensor("ckT", [DH, NB * KVH * NBLK], BF16))
    cvT = ctx.enter_context(nc.sbuf_tensor("cvT", [DH, NB * KVH * NBLK], F32))
    cvKM = ctx.enter_context(nc.sbuf_tensor("cvKM", [128, NB * KVH * 65], BF16))
    memcv_st = ctx.enter_context(nc.sbuf_tensor("memcv_st", [1, KVH * 65], BF16))
    memck_sb = ctx.enter_context(nc.sbuf_tensor("memck_sb", [DH, KVH], BF16))
    identB = ctx.enter_context(nc.sbuf_tensor("identB_sb", [128, 128], BF16))
    identF_sb = ctx.enter_context(nc.sbuf_tensor("identF_sb", [128, 128], F32))
    smallc = ctx.enter_context(nc.sbuf_tensor("smallc", [128, 4], F32))
    P0 = ctx.enter_context(nc.psum_tensor("P0", [128, 512], F32))
    P1 = ctx.enter_context(nc.psum_tensor("P1", [128, 512], F32))
    P2 = ctx.enter_context(nc.psum_tensor("P2", [128, 512], F32))
    P3 = ctx.enter_context(nc.psum_tensor("P3", [128, 512], F32))
    P4 = ctx.enter_context(nc.psum_tensor("P4", [128, 512], F32))
    PTR = ctx.enter_context(nc.psum_tensor("PTR", [128, 390], F32))
    PIM = ctx.enter_context(nc.psum_tensor("PIM", [128, 128], F32))
    PMEM = ctx.enter_context(nc.psum_tensor("PMEM", [1, 512], F32))
    cst = {}
    for nm, src, shp, dt in [("R2T", R2T_in, (128, 128), F32), ("C2", C2_in, (128, S), F32),
                             ("S2", S2_in, (128, S), F32), ("cmaskADD", cmaskADD_in, (128, S), BF16),
                             ("cmaskNEG", cmaskNEG_in, (128, S), F32), ("EBIG", EBIG_in, (128, S), BF16),
                             ("ownADD", ownADD_in, (128, 128), BF16), ("sADDp", sADDp_in, (64, 64), BF16),
                             ("sADDo", sADDo_in, (64, 64), BF16)]:
        cst[nm] = ctx.enter_context(nc.sbuf_tensor("c_" + nm, list(shp), dt))

    ld0 = sem("ld0")

    @blk.gpsimd
    def _(g):
        dma(g, identB[:, :], identB_in[:, :], ld0)
        dma(g, identF_sb[:, :], identF[:, :], ld0)
        for nm, src in [("R2T", R2T_in), ("C2", C2_in), ("S2", S2_in), ("cmaskADD", cmaskADD_in),
                        ("cmaskNEG", cmaskNEG_in), ("EBIG", EBIG_in), ("ownADD", ownADD_in),
                        ("sADDp", sADDp_in), ("sADDo", sADDo_in)]:
            dma(g, cst[nm][:, :], src[:, :], ld0)
        dma(g, memck_sb[:, :], memckT[:, :], ld0)
        for h in range(KVH):
            dma(g, memcv_st[0:1, h * 65:h * 65 + 64], memcv[h:h + 1, :], ld0)

    @blk.vector
    def _(ve):
        ve.wait_ge(ld0.sem, ld0.val)
        ve.memset(memcv_st[0:1, :].rearrange("p (h o) -> p h o", h=KVH)[:, :, 64:65], 1.0)
        ve.memset(smallc[:, 0:1], EPS)
        ve.memset(smallc[:, 1:2], -1000.0)

    full_barrier()

    if "ln1" in stages:
        with ExitStack() as sc:
            NT = NB * 16
            xt = [sc.enter_context(nc.sbuf_tensor(f"ln_x{i}", [128, DM], F32)) for i in range(2)]
            xl = [sc.enter_context(nc.sbuf_tensor(f"ln_o{i}", [128, DM], F32)) for i in range(2)]
            tx = [sc.enter_context(nc.sbuf_tensor(f"ln_t{i}", [128, DM], BF16)) for i in range(2)]
            st = sc.enter_context(nc.sbuf_tensor("ln_st", [128, 32], F32))
            sq = sc.enter_context(nc.sbuf_tensor("ln_sq", [128, DM], F32))
            pt = [PIM, PTR]
            s_ld = [sem("ln_ld0"), sem("ln_ld1")]
            s_m = sem("ln_m")
            s_var = sem("ln_var")
            s_xl = sem("ln_xl")
            s_tp = sem("ln_tp")
            s_ev = sem("ln_ev")
            s_st = [sem("ln_st0"), sem("ln_st1")]

            @blk.sync
            def _(sy):
                for i in range(min(2, NT)):
                    seq, tc = divmod(i, 16)
                    dma(sy, xt[i % 2][:, :], x_in[seq, tc * 128:(tc + 1) * 128, :], s_ld[i % 2])
                for i in range(NT):
                    sy.wait_ge(s_ev.sem, 16 * (i + 1))
                    seq, tc = divmod(i, 16)
                    dma(sy, xlnT[seq, :, :, tc * 128:(tc + 1) * 128].transpose([1, 0, 2]),
                        tx[i % 2][:, :].rearrange("p (dc t) -> p dc t", dc=16), s_st[i % 2])
                    if i + 2 < NT:
                        sy.wait_ge(s_xl.sem, i + 1)
                        seq2, tc2 = divmod(i + 2, 16)
                        dma(sy, xt[i % 2][:, :], x_in[seq2, tc2 * 128:(tc2 + 1) * 128, :], s_ld[i % 2])
                sy.wait_ge(s_st[0].sem, s_st[0].val)
                sy.wait_ge(s_st[1].sem, s_st[1].val)

            s_e = sem("ln_e")
            s_sq = sem("ln_sq2")
            s_v2 = sem("ln_v2")

            @blk.vector
            def _(ve):
                for i in range(NT):
                    bi = i % 2
                    ve.wait_ge(s_ld[i % 2].sem, 16 * (i // 2 + 1))
                    if i >= 1:
                        ve.wait_ge(s_xl.sem, i)
                    ve.tensor_reduce(st[:, 0:1], xt[bi][:, :], AX.X, ALU.add).then_inc(s_e.sem, 1); bump(s_e, 1)
                    ve.wait_ge(s_e.sem, 5 * i + 1)
                    ve.tensor_scalar(st[:, 2:3], st[:, 0:1], 1.0 / DM, None, ALU.mult).then_inc(s_e.sem, 1); bump(s_e, 1)
                    ve.wait_ge(s_sq.sem, i + 1)
                    ve.tensor_reduce(st[:, 1:2], sq[:, :], AX.X, ALU.add).then_inc(s_v2.sem, 1); bump(s_v2, 1)
                    ve.wait_ge(s_v2.sem, i + 1)
                    ve.tensor_scalar(st[:, 3:4], st[:, 2:3], st[:, 2:3], None, ALU.mult).then_inc(s_e.sem, 1); bump(s_e, 1)
                    ve.wait_ge(s_e.sem, 5 * i + 3)
                    ve.tensor_scalar(st[:, 4:5], st[:, 1:2], 1.0 / DM, st[:, 3:4],
                                     ALU.mult, ALU.subtract).then_inc(s_m.sem, 1); bump(s_m, 1)
                    ve.wait_ge(s_var.sem, i + 1)
                    ve.reciprocal(st[:, 27:28], st[:, 26:27]).then_inc(s_e.sem, 1); bump(s_e, 1)
                    ve.wait_ge(s_e.sem, 5 * i + 4)
                    ve.tensor_scalar(xl[bi][:, :], xt[bi][:, :], st[:, 2:3], st[:, 27:28],
                                     ALU.subtract, ALU.mult).then_inc(s_xl.sem, 1); bump(s_xl, 1)
                    ve.sem_inc(s_e.sem, 1); bump(s_e, 1)
                    for dc in range(16):
                        n = 16 * i + dc
                        ve.wait_ge(s_tp.sem, n + 1)
                        if i >= 2 and dc == 0:
                            ve.wait_ge(s_st[i % 2].sem, 16 * (i // 2))
                        ve.tensor_copy(tx[bi][:, dc * 128:(dc + 1) * 128], pt[n % 2][:, 0:128]).then_inc(s_ev.sem, 1); bump(s_ev, 1)

            @blk.scalar
            def _(s):
                for i in range(NT):
                    bi = i % 2
                    s.wait_ge(s_ld[i % 2].sem, 16 * (i // 2 + 1))
                    if i >= 1:
                        s.wait_ge(s_v2.sem, i)
                    s.activation(sq[:, :], xt[bi][:, :], AF.Square).then_inc(s_sq.sem, 1); bump(s_sq, 1)
                    s.wait_ge(s_m.sem, i + 1)
                    s.activation(st[:, 26:27], st[:, 4:5], AF.Sqrt, bias=smallc[:, 0:1],
                                 scale=1.0).then_inc(s_var.sem, 1); bump(s_var, 1)

            @blk.tensor
            def _(t):
                for i in range(NT):
                    bi = i % 2
                    t.wait_ge(s_xl.sem, i + 1)
                    for dc in range(16):
                        n = 16 * i + dc
                        if n >= 2:
                            t.wait_ge(s_ev.sem, n - 1)
                        t.transpose(pt[n % 2][:, 0:128], xl[bi][:, dc * 128:(dc + 1) * 128],
                                    identF_sb[:, :]).then_inc(s_tp.sem, 1); bump(s_tp, 1)

        full_barrier()

    if "qkv" in stages:
        with ExitStack() as sc:
            OC = 24
            wq = sc.enter_context(nc.sbuf_tensor("qk_w", [128, OC * 16 * 128], BF16))
            wg = sc.enter_context(nc.sbuf_tensor("qk_wg", [128, 16 * 96], BF16))
            bg = sc.enter_context(nc.sbuf_tensor("qk_bg", [96, 1], F32))
            xs = [sc.enter_context(nc.sbuf_tensor(f"qk_x{i}", [128, 16 * 512], BF16)) for i in range(2)]
            stg = [sc.enter_context(nc.sbuf_tensor(f"qk_s{i}", [128, 512], F32)) for i in range(4)]
            gst = sc.enter_context(nc.sbuf_tensor("qk_gs", [96, 512], F32))
            gtt = sc.enter_context(nc.sbuf_tensor("qk_gt", [128, 4 * 96], F32))
            pq = [P0, P1]
            pg = P2
            pgt = P3
            s_w = sem("qk_w")
            s_x = [sem("qk_x0"), sem("qk_x1")]
            s_pe = sem("qk_pe")
            s_ev = sem("qk_ev")
            s_pg = sem("qk_pg")
            s_sg = sem("qk_sg")
            s_tg = sem("qk_tg")
            s_eg = sem("qk_eg")
            s_st = [sem(f"qk_st{i}") for i in range(4)]
            s_gs = sem("qk_gs2")
            NT = NB * 4

            @blk.gpsimd
            def _(g):
                for oc in range(OC):
                    dma(g, wq[:, oc * 2048:(oc + 1) * 2048].rearrange("p (dc m) -> p dc m", dc=16),
                        w_qkv[:, oc * 128:(oc + 1) * 128].rearrange("(dc p) m -> p dc m", p=128), s_w)
                dma(g, wg[:, :].rearrange("p (dc m) -> p dc m", dc=16),
                    w_gate[:, :].rearrange("(dc p) m -> p dc m", p=128), s_w)
                dma(g, bg[:, :], b_gate[:, :], s_w)

            @blk.sync
            def _(sy):
                for it in range(min(2, NT)):
                    seq, tq = divmod(it, 4)
                    dma(sy, xs[it % 2][:, :].rearrange("p (dc t) -> p dc t", dc=16),
                        xlnT[seq, :, :, tq * 512:(tq + 1) * 512].transpose([1, 0, 2]), s_x[it % 2])
                for it in range(NT):
                    seq, tq = divmod(it, 4)
                    for oc in range(OC):
                        n = it * OC + oc
                        sy.wait_ge(s_ev.sem, n + 1)
                        dstf = (qF[seq, oc, :, tq * 512:(tq + 1) * 512] if oc < 16 else
                                kF[seq, oc - 16, :, tq * 512:(tq + 1) * 512] if oc < 20 else
                                vF[seq, oc - 20, :, tq * 512:(tq + 1) * 512])
                        dma(sy, dstf, stg[n % 4][:, :], s_st[n % 4])
                    sy.wait_ge(s_eg.sem, 4 * (it + 1))
                    dma(sy, gatesT[seq, tq * 4:(tq + 1) * 4, :, :].transpose([1, 0, 2]),
                        gtt[:, :].rearrange("p (tt g) -> p tt g", tt=4), s_gs)
                    if it + 2 < NT:
                        seq2, tq2 = divmod(it + 2, 4)
                        dma(sy, xs[it % 2][:, :].rearrange("p (dc t) -> p dc t", dc=16),
                            xlnT[seq2, :, :, tq2 * 512:(tq2 + 1) * 512].transpose([1, 0, 2]), s_x[it % 2])
                for i in range(4):
                    sy.wait_ge(s_st[i].sem, s_st[i].val)
                sy.wait_ge(s_gs.sem, s_gs.val)

            @blk.tensor
            def _(t):
                t.wait_ge(s_w.sem, 16 * (OC + 2))
                for it in range(NT):
                    xv = xs[it % 2][:, :].rearrange("p (dc t) -> p dc t", dc=16)
                    t.wait_ge(s_x[it % 2].sem, 16 * (it // 2 + 1))
                    for oc in range(OC):
                        n = it * OC + oc
                        if n >= 2:
                            t.wait_ge(s_ev.sem, n - 1)
                        for dc in range(16):
                            mm = t.matmul(pq[n % 2][:, :], wq[:, (oc * 16 + dc) * 128:(oc * 16 + dc + 1) * 128],
                                          xv[:, dc, :], start=(dc == 0), stop=(dc == 15))
                        mm.then_inc(s_pe.sem, 1); bump(s_pe, 1)
                    if it >= 1:
                        t.wait_ge(s_sg.sem, it)
                    for dc in range(16):
                        mm = t.matmul(pg[0:96, :], wg[:, dc * 96:(dc + 1) * 96], xv[:, dc, :],
                                      start=(dc == 0), stop=(dc == 15))
                    mm.then_inc(s_pg.sem, 1); bump(s_pg, 1)
                    t.wait_ge(s_sg.sem, it + 1)
                    for tt in range(4):
                        n = 4 * it + tt
                        if n >= 1:
                            t.wait_ge(s_eg.sem, n)
                        t.transpose(pgt[:, 0:96], gst[0:96, tt * 128:(tt + 1) * 128],
                                    identF_sb[0:96, 0:96]).then_inc(s_tg.sem, 1); bump(s_tg, 1)

            @blk.vector
            def _(ve):
                for it in range(NT):
                    for oc in range(OC):
                        n = it * OC + oc
                        ve.wait_ge(s_pe.sem, n + 1)
                        if n >= 4:
                            ve.wait_ge(s_st[n % 4].sem, 16 * (n // 4))
                        ve.tensor_copy(stg[n % 4][:, :], pq[n % 2][:, :]).then_inc(s_ev.sem, 1); bump(s_ev, 1)
                    for tt in range(4):
                        n = 4 * it + tt
                        ve.wait_ge(s_tg.sem, n + 1)
                        if it >= 1 and tt == 0:
                            ve.wait_ge(s_gs.sem, 16 * it)
                        ve.tensor_copy(gtt[:, tt * 96:(tt + 1) * 96], pgt[:, 0:96]).then_inc(s_eg.sem, 1); bump(s_eg, 1)

            @blk.scalar
            def _(s):
                for it in range(NT):
                    s.wait_ge(s_pg.sem, it + 1)
                    if it >= 1:
                        s.wait_ge(s_tg.sem, 4 * it)
                    s.activation(gst[0:96, :], pg[0:96, :], AF.Sigmoid,
                                 bias=bg[0:96, 0:1]).then_inc(s_sg.sem, 1); bump(s_sg, 1)

        full_barrier()

    if "rope" in stages:
        with ExitStack() as sc:
            bf = [sc.enter_context(nc.sbuf_tensor(f"ro_x{i}", [128, S], F32)) for i in range(2)]
            t1 = sc.enter_context(nc.sbuf_tensor("ro_t1", [128, S], F32))
            t2 = sc.enter_context(nc.sbuf_tensor("ro_t2", [128, S], F32))
            ob = [sc.enter_context(nc.sbuf_tensor(f"ro_o{i}", [128, S], BF16)) for i in range(2)]
            kpb = sc.enter_context(nc.sbuf_tensor("ro_kp", [128, CBLK], F32))
            pr = [P0, P1]
            s_ld = [sem("ro_ld0"), sem("ro_ld1")]
            s_pe = sem("ro_pe")
            s_t2 = sem("ro_t2")
            s_dv = sem("ro_dv")
            s_st = [sem("ro_st0"), sem("ro_st1")]
            s_kp = sem("ro_kp")

            bands = []
            for seq in range(NB):
                for bd in range(16):
                    bands.append((0, seq, bd))
                for bd in range(4):
                    bands.append((1, seq, bd))
                for bd in range(4):
                    bands.append((2, seq, bd))
                for bd in range(4):
                    bands.append((3, seq, bd))

            def src_of(kind, seq, bd):
                return (qF if kind == 0 else kF if kind in (1, 2) else vF)[seq, bd, :, :]

            @blk.sync
            def _(sy):
                for i in range(min(2, len(bands))):
                    kind, seq, bd = bands[i]
                    dma(sy, bf[i % 2][:, :], src_of(kind, seq, bd), s_ld[i % 2])
                for i, (kind, seq, bd) in enumerate(bands):
                    sy.wait_ge(s_dv.sem, i + 1)
                    dst = (rqT, rkT, kbT, vbT)[kind]
                    dma(sy, dst[seq, bd, :, :], ob[i % 2][:, :], s_st[i % 2])
                    if i + 2 < len(bands):
                        kind2, seq2, bd2 = bands[i + 2]
                        dma(sy, bf[i % 2][:, :], src_of(kind2, seq2, bd2), s_ld[i % 2])
                sy.wait_ge(s_st[0].sem, s_st[0].val)
                sy.wait_ge(s_st[1].sem, s_st[1].val)

            @blk.gpsimd
            def _(g):
                kpi = 0
                for i, (kind, seq, bd) in enumerate(bands):
                    if kind in (2, 3):
                        if kpi >= 1:
                            g.wait_ge(s_dv.sem, prev_kp + 1)
                        src = kposT if kind == 2 else vposT
                        dma(g, kpb[0:64, :], src[2 * bd, :, :], s_kp)
                        dma(g, kpb[64:128, :], src[2 * bd + 1, :, :], s_kp)
                        kpi += 1
                        prev_kp = i

            @blk.tensor
            def _(t):
                nrot = 0
                for i, (kind, seq, bd) in enumerate(bands):
                    if kind in (0, 1):
                        t.wait_ge(s_ld[i % 2].sem, 16 * (i // 2 + 1))
                        for c in range(4):
                            if nrot >= 2:
                                t.wait_ge(s_t2.sem, nrot - 1)
                            t.matmul(pr[nrot % 2][:, :], cst["R2T"][:, :],
                                     bf[i % 2][:, c * 512:(c + 1) * 512], start=True,
                                     stop=True).then_inc(s_pe.sem, 1); bump(s_pe, 1)
                            nrot += 1

            @blk.vector
            def _(ve):
                nrot = 0
                nkp = 0
                for i, (kind, seq, bd) in enumerate(bands):
                    bi = i % 2
                    ve.wait_ge(s_ld[i % 2].sem, 16 * (i // 2 + 1))
                    if i >= 1:
                        ve.wait_ge(s_dv.sem, i)
                    if i >= 2:
                        ve.wait_ge(s_st[i % 2].sem, 16 * (i // 2))
                    if kind in (0, 1):
                        ve.tensor_mul(t1[:, :], bf[bi][:, :], cst["C2"][:, :])
                        for c in range(4):
                            ve.wait_ge(s_pe.sem, nrot + 1)
                            ve.tensor_mul(t2[:, c * 512:(c + 1) * 512], pr[nrot % 2][:, :],
                                          cst["S2"][:, c * 512:(c + 1) * 512]).then_inc(s_t2.sem, 1); bump(s_t2, 1)
                            nrot += 1
                        ve.wait_ge(s_t2.sem, nrot)
                        ve.tensor_add(ob[bi][:, :], t1[:, :], t2[:, :]).then_inc(s_dv.sem, 1); bump(s_dv, 1)
                    else:
                        nkp += 32
                        ve.wait_ge(s_kp.sem, nkp)
                        ve.tensor_add(ob[bi][:, :].rearrange("p (blk c) -> p blk c", c=CBLK),
                                      bf[bi][:, :].rearrange("p (blk c) -> p blk c", c=CBLK),
                                      kpb[:, :].unsqueeze(1).to_broadcast([128, NBLK, CBLK])
                                      ).then_inc(s_dv.sem, 1); bump(s_dv, 1)

        full_barrier()

    s_ck = sem("cm_ck")
    s_km = sem("cm_km")
    if "cmp" in stages:
        with ExitStack() as sc:
            cw1 = {}
            cw2 = {}
            cb1 = {}
            cb2 = {}
            for t_, w1, w2 in [("k", k_cw1, k_cw2), ("v", v_cw1, v_cw2)]:
                cw1[t_] = sc.enter_context(nc.sbuf_tensor(f"cm_w1{t_}", [64, 16 * 1024], BF16))
                cw2[t_] = sc.enter_context(nc.sbuf_tensor(f"cm_w2{t_}", [128, 8 * 64], BF16))
                cb1[t_] = sc.enter_context(nc.sbuf_tensor(f"cm_b1{t_}", [128, 8], F32))
                cb2[t_] = sc.enter_context(nc.sbuf_tensor(f"cm_b2{t_}", [64, 1], F32))
            kbt = sc.enter_context(nc.sbuf_tensor("cm_kb", [64, 4 * S], BF16))
            vbt = sc.enter_context(nc.sbuf_tensor("cm_vb", [64, 4 * S], BF16))
            h1 = sc.enter_context(nc.sbuf_tensor("cm_h1", [128, 8 * 512], BF16))
            ph = [P0, P1]
            pc = P2
            ptk = [PIM, PTR]
            s_w = sem("cm_w")
            s_ldg = sem("cm_ld")
            s_mm = sem("cm_mm")
            s_h1 = sem("cm_h1")
            s_pc = sem("cm_pc")
            s_tp = sem("cm_tp")
            NG = 4

            @blk.gpsimd
            def _(g):
                for t_, w1, w2, b1, b2 in [("k", k_cw1, k_cw2, k_cb1, k_cb2),
                                           ("v", v_cw1, v_cw2, v_cb1, v_cb2)]:
                    for kk in range(16):
                        dma(g, cw1[t_][:, kk * 1024:(kk + 1) * 1024],
                            w1[kk * 64:(kk + 1) * 64, :], s_w)
                    dma(g, cw2[t_][:, :].rearrange("p (mc m) -> p mc m", mc=8),
                        w2[:, :].rearrange("(mc p) m -> p mc m", p=128), s_w)
                    g.dma_start(cb1[t_][:, :].unsqueeze(2),
                                b1[:, :].rearrange("(mc p) o -> p mc o", p=128),
                                allow_slow_non_contiguous=True).then_inc(s_w.sem, 16); bump(s_w, 16)
                    dma(g, cb2[t_][:, :], b2[:, :], s_w)

            @blk.sync
            def _(sy):
                for g in range(NG):
                    if g >= 1:
                        sy.wait_ge(s_h1.sem, 16 * g)
                    b = (4 * g) // 8
                    band0 = ((4 * g) % 8) // 2
                    for j4 in range(4):
                        band = band0 + j4 // 2
                        r0 = (j4 % 2) * 64
                        dma(sy, kbt[:, j4 * S:(j4 + 1) * S], kbT[b, band, r0:r0 + 64, :], s_ldg)
                        dma(sy, vbt[:, j4 * S:(j4 + 1) * S], vbT[b, band, r0:r0 + 64, :], s_ldg)

            def kb_mov(t_, kk):
                tile = kbt if t_ == "k" else vbt
                return tile[:, :].rearrange("p (j4 blk c) -> p c j4 blk", c=CBLK, j4=4)[:, kk, :, :]

            @blk.tensor
            def _(t):
                t.wait_ge(s_w.sem, s_w.val)
                for g in range(NG):
                    t.wait_ge(s_ldg.sem, 128 * (g + 1))
                    for ti, t_ in enumerate(("k", "v")):
                        for mc in range(8):
                            n = 16 * g + 8 * ti + mc
                            if n >= 2:
                                t.wait_ge(s_h1.sem, n - 1)
                            for kk in range(16):
                                w1ap = cw1[t_][:, kk * 1024 + mc * 128:kk * 1024 + (mc + 1) * 128]
                                mm = t.matmul(ph[n % 2][:, :], w1ap, kb_mov(t_, kk),
                                              start=(kk == 0), stop=(kk == 15))
                            mm.then_inc(s_mm.sem, 1); bump(s_mm, 1)
                        t.wait_ge(s_h1.sem, 16 * g + 8 * ti + 8)
                        for mc in range(8):
                            mm = t.matmul(pc[0:64, :], cw2[t_][:, mc * 64:(mc + 1) * 64],
                                          h1[:, mc * 512:(mc + 1) * 512],
                                          start=(mc == 0), stop=(mc == 7))
                        mm.then_inc(s_pc.sem, 1); bump(s_pc, 1)
                        if not (g == NG - 1 and ti == 1):
                            t.wait_ge(s_ck.sem, 2 * g + ti + 1)
                    t.wait_ge(s_ck.sem, 2 * g + 2)
                    for j4 in range(4):
                        n = 4 * g + j4
                        if n >= 2:
                            t.wait_ge(s_km.sem, n - 1)
                        bh = 4 * g + j4
                        t.transpose(ptk[n % 2][:, 0:64], cvT[:, bh * 128:(bh + 1) * 128],
                                    identF_sb[0:64, 0:64]).then_inc(s_tp.sem, 1); bump(s_tp, 1)

            @blk.scalar
            def _(s):
                for g in range(NG):
                    for ti, t_ in enumerate(("k", "v")):
                        for mc in range(8):
                            n = 16 * g + 8 * ti + mc
                            s.wait_ge(s_mm.sem, n + 1)
                            s.activation(h1[:, mc * 512:(mc + 1) * 512], ph[n % 2][:, :],
                                         AF.Relu, bias=cb1[t_][:, mc:mc + 1]).then_inc(s_h1.sem, 1); bump(s_h1, 1)
                        s.wait_ge(s_pc.sem, 2 * g + ti + 1)
                        dstT = ckT if t_ == "k" else cvT
                        s.activation(dstT[:, g * 512:(g + 1) * 512], pc[0:64, :], AF.Identity,
                                     bias=cb2[t_][:, 0:1]).then_inc(s_ck.sem, 1); bump(s_ck, 1)

            @blk.vector
            def _(ve):
                ve.memset(cvKM[:, :].rearrange("p (bh o) -> p bh o", bh=NB * KVH)[:, :, 64:65], 1.0)
                for g in range(NG):
                    for j4 in range(4):
                        n = 4 * g + j4
                        ve.wait_ge(s_tp.sem, n + 1)
                        bh = 4 * g + j4
                        ve.tensor_copy(cvKM[:, bh * 65:bh * 65 + 64], ptk[n % 2][:, 0:64]
                                       ).then_inc(s_km.sem, 1); bump(s_km, 1)

            if "ckD" in dbg or "cvD" in dbg:
                @blk.gpsimd
                def _(g):
                    g.wait_ge(s_ck.sem, 8)
                    dma(g, ckD[:, :], ckT[:, :], s_w)
                    dma(g, cvD[:, :], cvT[:, :], s_w)
                    g.wait_ge(s_w.sem, s_w.val)

        full_barrier()

    if "attn" in stages:
        with ExitStack() as sc:
            rq4 = [sc.enter_context(nc.sbuf_tensor(f"at_rq{i}", [64, 4 * S], BF16)) for i in range(2)]
            q4 = [sc.enter_context(nc.sbuf_tensor(f"at_q{i}", [64, 4 * S], BF16)) for i in range(2)]
            rkh = [sc.enter_context(nc.sbuf_tensor(f"at_rk{i}", [64, S], BF16)) for i in range(2)]
            vth = [sc.enter_context(nc.sbuf_tensor("at_vt0", [64, S], F32))] * 2
            vKM = sc.enter_context(nc.sbuf_tensor("at_vkm", [128, 16 * 65], BF16))
            vKM2 = sc.enter_context(nc.sbuf_tensor("at_vkm2", [64, 32 * 65], BF16))
            impsum = sc.enter_context(nc.sbuf_tensor("at_imps", [128, S], F32))
            esh = impsum
            impq = sc.enter_context(nc.sbuf_tensor("at_impq", [128, S], F32))
            top8 = sc.enter_context(nc.sbuf_tensor("at_top8", [128, 16 * 8], F32))
            zr = sc.enter_context(nc.sbuf_tensor("at_zr", [128, 6 * 16], F32))
            BMq = sc.enter_context(nc.sbuf_tensor("at_bmq", [128, S], F32))
            BMT1 = sc.enter_context(nc.sbuf_tensor("at_bmt1", [128, S], BF16))
            pf = sc.enter_context(nc.sbuf_tensor("at_pf", [128, 8 * 512], BF16))
            pc_sb = sc.enter_context(nc.sbuf_tensor("at_pc", [128, 512], BF16))
            pm_sb = sc.enter_context(nc.sbuf_tensor("at_pm", [1, 512], BF16))
            psl = sc.enter_context(nc.sbuf_tensor("at_psl", [64, 512], BF16))
            pso = sc.enter_context(nc.sbuf_tensor("at_pso", [64, 512], BF16))
            cnum = sc.enter_context(nc.sbuf_tensor("at_cn", [65, 512], F32))
            fnum = sc.enter_context(nc.sbuf_tensor("at_fn", [65, 512], F32))
            snum = sc.enter_context(nc.sbuf_tensor("at_sn", [65, 512], F32))
            zrec = sc.enter_context(nc.sbuf_tensor("at_zrec", [128, 12], F32))
            mbr = sc.enter_context(nc.sbuf_tensor("at_mbr", [128, 12], F32))
            acc = sc.enter_context(nc.sbuf_tensor("at_acc", [128, 256], F32))
            tmpc = sc.enter_context(nc.sbuf_tensor("at_tmpc", [128, 256], F32))
            gtile = sc.enter_context(nc.sbuf_tensor("at_gt", [128, 16 * 96], F32))
            asm = [sc.enter_context(nc.sbuf_tensor(f"at_asm{i}", [64, S], BF16)) for i in range(4)]
            pcs = [P0, P1]
            pfs = [P2, P3]
            pbr = P4
            ptr = PTR
            pim = PIM
            pmem = PMEM

            names = ["a_ld0", "a_ld1", "a_lq0", "a_lq1", "a_gt", "a_tpv", "a_evv", "a_cs",
                     "a_imp", "a_msk", "a_imtp", "a_imev", "a_bm", "a_bmtp", "a_bmev", "a_e",
                     "a_c1", "a_cm", "a_xc", "a_xm", "a_co", "a_bev", "a_fs", "a_xf", "a_fv",
                     "a_fo", "a_ss", "a_xs", "a_so", "a_tr", "a_cb", "a_tb", "a_te", "a_st",
                     "a_ms", "a_lsh", "a_mpr", "a_es", "a_es2", "a_zv", "a_es3", "a_bd"]
            sm = {n: sem(n) for n in names}
            NBH = NB * KVH

            def bh_of(i):
                return i // 8, i % 8

            def ldw(bh):
                return 16 * 6 * (bh // 2 + 1)

            def ss_base(c):
                full, j = divmod(c, 16)
                tot = full * 252
                for jj in range(j):
                    tot += 15 if jj // 4 == 0 else 16
                return tot

            def ss_nprev(c):
                return 7 if (c % 16) // 4 == 0 else 8

            @blk.sync
            def _(sy):
                for bh in range(NBH):
                    b, h = bh_of(bh)
                    p = bh % 2
                    if bh >= 2:
                        sy.wait_ge(sm["a_te"].sem, 16 * (bh - 1))
                    for g in range(G):
                        band = 2 * h + g // 2
                        r0 = (g % 2) * 64
                        dma(sy, rq4[p][:, g * S:(g + 1) * S], rqT[b, band, r0:r0 + 64, :], sm[f"a_ld{p}"])
                    dma(sy, rkh[p][:, :], rkT[b, h // 2, (h % 2) * 64:(h % 2) * 64 + 64, :], sm[f"a_ld{p}"])
                    if bh >= 1:
                        sy.wait_ge(sm["a_tpv"].sem, 48 * bh)
                    dma(sy, vth[p][:, :], vF[b, h // 2, (h % 2) * 64:(h % 2) * 64 + 64, :], sm[f"a_ld{p}"])
                    if bh >= 2:
                        bp, hp = bh_of(bh - 2)
                        sy.wait_ge(sm["a_te"].sem, 16 * (bh - 1))
                        for g in range(G):
                            dma(sy, attnFM[bp, 4 * hp + g, :, :], asm[g][:, :], sm["a_st"])
                    if h == 0:
                        if bh >= 2:
                            sy.wait_ge(sm["a_cb"].sem, 32 * bh)
                        dma(sy, gtile[:, :].rearrange("p (qt g) -> p qt g", qt=16),
                            gatesT[b, :, :, :].transpose([1, 0, 2]), sm["a_gt"])
                for bh in (NBH - 2, NBH - 1):
                    bp, hp = bh_of(bh)
                    sy.wait_ge(sm["a_te"].sem, 16 * (bh + 1))
                    for g in range(G):
                        dma(sy, attnFM[bp, 4 * hp + g, :, :], asm[g][:, :], sm["a_st"])
                sy.wait_ge(sm["a_st"].sem, sm["a_st"].val)

            @blk.gpsimd
            def _(g_):
                for bh in range(NBH):
                    b, h = bh_of(bh)
                    p = bh % 2
                    if bh >= 2:
                        g_.wait_ge(sm["a_cs"].sem, 16 * (bh - 1))
                        g_.wait_ge(sm["a_c1"].sem, 16 * (bh - 1))
                        g_.wait_ge(sm["a_cm"].sem, 16 * (bh - 1))
                    for g in range(G):
                        band = 2 * h + g // 2
                        r0 = (g % 2) * 64
                        dma(g_, q4[p][:, g * S:(g + 1) * S], qF[b, band, r0:r0 + 64, :], sm[f"a_lq{p}"])
                    if "bmD" in dbg:
                        g_.wait_ge(sm["a_bm"].sem, bh + 1)
                        dma(g_, bmD[b, h, :, :, :].transpose([1, 0, 2]),
                            BMq[:, :].rearrange("p (qt blk) -> p qt blk", qt=16), sm["a_bd"])

            @blk.tensor
            def _(t):
                def flush_tr(cp):
                    t.wait_ge(sm["a_bev"].sem, 3 * (cp + 1))
                    for r in range(2):
                        t.wait_ge(sm["a_cb"].sem, 2 * cp + r)
                        for q2 in range(2):
                            qt2 = 2 * r + q2
                            for br, buf in enumerate((cnum, fnum, snum)):
                                t.transpose(ptr[:, (q2 * 3 + br) * 65:(q2 * 3 + br + 1) * 65],
                                            buf[0:65, qt2 * 128:(qt2 + 1) * 128],
                                            identF_sb[0:65, 0:65]).then_inc(sm["a_tr"].sem, 1); bump(sm["a_tr"], 1)

                def flush_tb(cp):
                    t.wait_ge(sm["a_cb"].sem, 2 * (cp + 1))
                    t.wait_ge(sm["a_xc"].sem, cp + 1)
                    for qt2 in range(4):
                        t.transpose(pcs[cp % 2][0:64, qt2 * 128:(qt2 + 1) * 128],
                                    acc[:, qt2 * 64:(qt2 + 1) * 64],
                                    identF_sb[:, :]).then_inc(sm["a_tb"].sem, 1); bump(sm["a_tb"], 1)

                def emit_T3(bh):
                    b, h = bh_of(bh)
                    p = bh % 2
                    t.wait_ge(sm["a_evv"].sem, 48 * (bh + 1))
                    t.wait_ge(sm["a_imp"].sem, min(16 * (bh + 2), 16 * NBH))
                    t.wait_ge(sm["a_bmev"].sem, 16 * (bh + 1))
                    for j in range(16):
                        qc, g = divmod(j, 4)
                        c = 16 * bh + j
                        if j > 0:
                            flush_tr(c - 1)
                        qsl = q4[p][:, g * S + qc * 512:g * S + (qc + 1) * 512]
                        rqsl = rq4[p][:, g * S + qc * 512:g * S + (qc + 1) * 512]
                        if j >= 2:
                            t.wait_ge(sm["a_xc"].sem, c - 1)
                            t.wait_ge(sm["a_te"].sem, c - 1)
                        t.matmul(pcs[c % 2][:, :], ckT[:, bh * 128:(bh + 1) * 128], qsl,
                                 start=True, stop=False)
                        t.matmul(pcs[c % 2][:, :], identB[:, :],
                                 cst["cmaskADD"][:, qc * 512:(qc + 1) * 512],
                                 start=False, stop=True).then_inc(sm["a_c1"].sem, 1); bump(sm["a_c1"], 1)
                        if c >= 1:
                            t.wait_ge(sm["a_xm"].sem, c)
                        t.matmul(pmem[0:1, :], memck_sb[:, h:h + 1], qsl,
                                 start=True, stop=True).then_inc(sm["a_cm"].sem, 1); bump(sm["a_cm"], 1)
                        for kc in range(16):
                            n = 16 * c + kc
                            if n >= 2:
                                t.wait_ge(sm["a_xf"].sem, n - 1)
                            if kc < 2 and c >= 1:
                                t.wait_ge(sm["a_so"].sem, 8 * c)
                            dg = kc - 4 * qc
                            diag = dg if 0 <= dg < 4 else None
                            t.matmul(pfs[kc % 2][:, :], rkh[p][:, kc * 128:(kc + 1) * 128],
                                     rqsl, start=True, stop=False)
                            mmE = t.matmul(pfs[kc % 2][:, :], cst["EBIG"][:, kc * 128:(kc + 1) * 128],
                                           BMT1[:, qc * 512:(qc + 1) * 512],
                                           start=False, stop=(diag is None))
                            if diag is not None:
                                mmE = t.matmul(pfs[kc % 2][:, diag * 128:(diag + 1) * 128],
                                               identB[:, :], cst["ownADD"][:, :],
                                               start=False, stop=True)
                            mmE.then_inc(sm["a_fs"].sem, 1); bump(sm["a_fs"], 1)
                            if kc == 1:
                                if c >= 1 and j > 0:
                                    flush_tb(c - 1)
                                if c >= 1:
                                    t.wait_ge(sm["a_bev"].sem, 3 * c)
                                t.wait_ge(sm["a_xc"].sem, c + 1)
                                t.matmul(pbr[0:65, :], cvKM[:, bh * 65:(bh + 1) * 65], pc_sb[:, :],
                                         start=True, stop=False)
                                t.wait_ge(sm["a_xm"].sem, c + 1)
                                t.matmul(pbr[0:65, :], memcv_st[0:1, h * 65:(h + 1) * 65],
                                         pm_sb[0:1, :],
                                         start=False, stop=True).then_inc(sm["a_co"].sem, 1); bump(sm["a_co"], 1)
                            if kc >= 2:
                                nf = kc - 2
                                if nf == 0:
                                    t.wait_ge(sm["a_bev"].sem, 3 * c + 1)
                                t.wait_ge(sm["a_xf"].sem, 16 * c + nf + 1)
                                t.matmul(pbr[0:65, :], vKM[:, nf * 65:(nf + 1) * 65],
                                         pf[:, (nf % 8) * 512:(nf % 8 + 1) * 512],
                                         start=(nf == 0), stop=False).then_inc(sm["a_fv"].sem, 1); bump(sm["a_fv"], 1)
                        for nf in (14, 15):
                            t.wait_ge(sm["a_xf"].sem, 16 * c + nf + 1)
                            t.matmul(pbr[0:65, :], vKM[:, nf * 65:(nf + 1) * 65],
                                     pf[:, (nf % 8) * 512:(nf % 8 + 1) * 512],
                                     start=False, stop=(nf == 15)).then_inc(sm["a_fv"].sem, 1); bump(sm["a_fv"], 1)
                        t.nop().then_inc(sm["a_fo"].sem, 1); bump(sm["a_fo"], 1)
                        t.wait_ge(sm["a_xf"].sem, 16 * (c + 1))
                        for w8 in range(8):
                            w = qc * 8 + w8
                            if w == 0:
                                continue
                            t.matmul(pfs[0][0:64, w8 * 64:(w8 + 1) * 64],
                                     rkh[p][:, (w - 1) * 64:w * 64],
                                     rq4[p][:, g * S + w * 64:g * S + (w + 1) * 64],
                                     start=True, stop=False)
                            t.matmul(pfs[0][0:64, w8 * 64:(w8 + 1) * 64],
                                     identB[0:64, 0:64], cst["sADDp"][:, :],
                                     start=False, stop=True).then_inc(sm["a_ss"].sem, 1); bump(sm["a_ss"], 1)
                        for w8 in range(8):
                            w = qc * 8 + w8
                            t.matmul(pfs[1][0:64, w8 * 64:(w8 + 1) * 64],
                                     rkh[p][:, w * 64:(w + 1) * 64],
                                     rq4[p][:, g * S + w * 64:g * S + (w + 1) * 64],
                                     start=True, stop=False)
                            t.matmul(pfs[1][0:64, w8 * 64:(w8 + 1) * 64],
                                     identB[0:64, 0:64], cst["sADDo"][:, :],
                                     start=False, stop=True).then_inc(sm["a_ss"].sem, 1); bump(sm["a_ss"], 1)
                        t.wait_ge(sm["a_bev"].sem, 3 * c + 2)
                        t.wait_ge(sm["a_xs"].sem, 2 * (c + 1))
                        for w8 in range(8):
                            w = qc * 8 + w8
                            if w > 0:
                                t.matmul(pbr[0:65, w8 * 64:(w8 + 1) * 64],
                                         vKM2[:, (w - 1) * 65:(w - 1) * 65 + 65],
                                         psl[:, w8 * 64:(w8 + 1) * 64], start=True, stop=False)
                            t.matmul(pbr[0:65, w8 * 64:(w8 + 1) * 64],
                                     vKM2[:, w * 65:w * 65 + 65],
                                     pso[:, w8 * 64:(w8 + 1) * 64],
                                     start=(w == 0), stop=True).then_inc(sm["a_so"].sem, 1); bump(sm["a_so"], 1)

                t.wait_ge(s_ck.sem, s_ck.val)
                t.wait_ge(s_km.sem, s_km.val)
                t.wait_ge(sm["a_ms"].sem, 2)
                for bh in range(NBH + 1):
                    b, h = bh_of(bh) if bh < NBH else (0, 0)
                    p = bh % 2
                    if bh < NBH:
                        t.wait_ge(sm[f"a_lq{p}"].sem, 64 * (bh // 2 + 1))
                        pA = [pcs[0], pcs[1], pfs[0], pfs[1]]
                        for j in range(16):
                            qc, g = divmod(j, 4)
                            n = 16 * bh + j
                            if n >= 4:
                                t.wait_ge(sm["a_imp"].sem, n - 3)
                            if bh >= 2 and j == 0:
                                t.wait_ge(sm["a_xc"].sem, 16 * (bh - 1))
                                t.wait_ge(sm["a_te"].sem, 16 * (bh - 1))
                                t.wait_ge(sm["a_xf"].sem, 256 * (bh - 1))
                                t.wait_ge(sm["a_xs"].sem, 32 * (bh - 1))
                            t.matmul(pA[j % 4][:, :], ckT[:, bh * 128:(bh + 1) * 128],
                                     q4[p][:, g * S + qc * 512:g * S + (qc + 1) * 512],
                                     start=True, stop=False)
                            t.matmul(pA[j % 4][:, :], identB[:, :],
                                     cst["cmaskADD"][:, qc * 512:(qc + 1) * 512],
                                     start=False, stop=True).then_inc(sm["a_cs"].sem, 1); bump(sm["a_cs"], 1)
                    if bh >= 1:
                        emit_T3(bh - 1)
                        flush_tr(16 * bh - 1)
                        flush_tb(16 * bh - 1)
                    if bh < NBH:
                        t.wait_ge(sm[f"a_ld{p}"].sem, ldw(bh))
                        if bh >= 1:
                            t.wait_ge(sm["a_cb"].sem, 32 * bh)
                            t.wait_ge(sm["a_bmev"].sem, 16 * bh)
                        for n in range(48):
                            dst = pim if n % 2 == 0 else ptr
                            if n >= 2:
                                t.wait_ge(sm["a_evv"].sem, 48 * bh + n - 1)
                            if n < 16:
                                t.transpose(dst[:, 0:64], vth[p][:, n * 128:(n + 1) * 128],
                                            identF_sb[0:64, 0:64]).then_inc(sm["a_tpv"].sem, 1); bump(sm["a_tpv"], 1)
                            else:
                                w = n - 16
                                t.transpose(dst[0:64, 0:64], vth[p][:, w * 64:(w + 1) * 64],
                                            identF_sb[0:64, 0:64]).then_inc(sm["a_tpv"].sem, 1); bump(sm["a_tpv"], 1)
                        t.wait_ge(sm["a_msk"].sem, bh + 1)
                        t.wait_ge(sm["a_evv"].sem, 48 * (bh + 1))
                        for qt in range(16):
                            n = 16 * bh + qt
                            dst = pim[:, 0:128] if qt % 2 == 0 else ptr[:, 0:128]
                            if qt > 1:
                                t.wait_ge(sm["a_imev"].sem, n - 1)
                            t.transpose(dst, impsum[:, qt * 128:(qt + 1) * 128],
                                        identF_sb[:, :]).then_inc(sm["a_imtp"].sem, 1); bump(sm["a_imtp"], 1)
                        t.wait_ge(sm["a_bm"].sem, bh + 1)
                        t.wait_ge(sm["a_imev"].sem, 16 * (bh + 1))
                        for qt in range(16):
                            n = 16 * bh + qt
                            dst = pim[:, 0:128] if qt % 2 == 0 else ptr[:, 0:128]
                            if qt > 1:
                                t.wait_ge(sm["a_bmev"].sem, n - 1)
                            t.transpose(dst, BMq[:, qt * 128:(qt + 1) * 128],
                                        identF_sb[:, :]).then_inc(sm["a_bmtp"].sem, 1); bump(sm["a_bmtp"], 1)

            @blk.scalar
            def _(s):
                zrv = zr[:, :].rearrange("p (k qt) -> p k qt", k=6)
                for bh in range(NBH + 1):
                    if bh >= 1:
                        bhp = bh - 1
                        hh = bhp % 8
                        for j in range(16):
                            qc, g = divmod(j, 4)
                            c = 16 * bhp + j
                            s.wait_ge(sm["a_c1"].sem, c + 1)
                            if c >= 1:
                                s.wait_ge(sm["a_co"].sem, c)
                            s.activation(pc_sb[:, :], pcs[c % 2][:, :], AF.Exp,
                                         scale=SCALE).then_inc(sm["a_xc"].sem, 1); bump(sm["a_xc"], 1)
                            s.wait_ge(sm["a_cm"].sem, c + 1)
                            s.activation(pm_sb[0:1, :], pmem[0:1, :], AF.Exp,
                                         scale=SCALE).then_inc(sm["a_xm"].sem, 1); bump(sm["a_xm"], 1)
                            for kc in range(16):
                                n = 16 * c + kc
                                s.wait_ge(sm["a_fs"].sem, n + 1)
                                if c >= 1 or kc >= 8:
                                    s.wait_ge(sm["a_fv"].sem, 16 * c + kc - 7)
                                s.activation(pf[:, (kc % 8) * 512:(kc % 8 + 1) * 512],
                                             pfs[kc % 2][:, :],
                                             AF.Exp, scale=SCALE).then_inc(sm["a_xf"].sem, 1); bump(sm["a_xf"], 1)
                            s.wait_ge(sm["a_ss"].sem, ss_base(c) + ss_nprev(c))
                            if c >= 1:
                                s.wait_ge(sm["a_so"].sem, 8 * c)
                            if qc == 0:
                                s.activation(psl[:, 64:512], pfs[0][0:64, 64:512], AF.Exp,
                                             scale=SCALE).then_inc(sm["a_xs"].sem, 1); bump(sm["a_xs"], 1)
                            else:
                                s.activation(psl[:, :], pfs[0][0:64, :], AF.Exp,
                                             scale=SCALE).then_inc(sm["a_xs"].sem, 1); bump(sm["a_xs"], 1)
                            s.wait_ge(sm["a_ss"].sem, ss_base(c) + ss_nprev(c) + 8)
                            s.activation(pso[:, :], pfs[1][0:64, :], AF.Exp,
                                         scale=SCALE).then_inc(sm["a_xs"].sem, 1); bump(sm["a_xs"], 1)
                    if bh < NBH:
                        s.wait_ge(sm["a_lsh"].sem, bh + 1)
                        s.activation(esh[:, :], esh[:, :], AF.Exp).then_inc(sm["a_es"].sem, 1); bump(sm["a_es"], 1)
                        s.wait_ge(sm["a_mpr"].sem, bh + 1)
                        s.activation(zrv[:, 2, :], zrv[:, 1, :], AF.Exp,
                                     bias=smallc[:, 1:2]).then_inc(sm["a_es2"].sem, 1); bump(sm["a_es2"], 1)
                        s.wait_ge(sm["a_zv"].sem, bh + 1)
                        s.activation(zrv[:, 4, :], zrv[:, 3, :], AF.Ln).then_inc(sm["a_es3"].sem, 1); bump(sm["a_es3"], 1)

            @blk.vector
            def _(ve):
                ve.memset(vKM[:, :].rearrange("p (kc o) -> p kc o", kc=16)[:, :, 64:65],
                          1.0).then_inc(sm["a_ms"].sem, 1); bump(sm["a_ms"], 1)
                ve.memset(vKM2[:, :].rearrange("p (w o) -> p w o", w=32)[:, :, 64:65],
                          1.0).then_inc(sm["a_ms"].sem, 1); bump(sm["a_ms"], 1)
                ec = [0]

                def edge(ins):
                    ins.then_inc(sm["a_e"].sem, 1)
                    ec[0] += 1
                    ve.wait_ge(sm["a_e"].sem, ec[0])
                    return ins

                def tback_evac(cp):
                    bhp, jp = divmod(cp, 16)
                    qcp, gp = divmod(jp, 4)
                    ve.wait_ge(sm["a_tb"].sem, 4 * (cp + 1))
                    if jp == 0 and bhp >= 1:
                        ve.wait_ge(sm["a_st"].sem, 64 * bhp)
                    ve.tensor_copy(asm[gp][:, qcp * 512:(qcp + 1) * 512],
                                   pcs[cp % 2][0:64, :]).then_inc(sm["a_te"].sem, 1); bump(sm["a_te"], 1)

                for bh in range(NBH + 1):
                    b, h = bh_of(bh) if bh < NBH else (0, 0)
                    if bh >= 2:
                        tback_evac(16 * (bh - 1) - 1)
                    if bh < NBH:
                        for j in range(16):
                            qc, g = divmod(j, 4)
                            n = 16 * bh + j
                            if j == 0 and bh >= 1:
                                ve.wait_ge(sm["a_imtp"].sem, 16 * bh)
                                ve.wait_ge(sm["a_es"].sem, bh)
                            ve.wait_ge(sm["a_cs"].sem, n + 1)
                            if g >= 1:
                                ve.wait_ge(sm["a_imp"].sem, n)
                            sl = impsum[:, qc * 512:(qc + 1) * 512]
                            pAv = [pcs[0], pcs[1], pfs[0], pfs[1]][j % 4]
                            if g == 0:
                                ve.tensor_copy(sl, pAv[:, :]).then_inc(sm["a_imp"].sem, 1); bump(sm["a_imp"], 1)
                            else:
                                ve.tensor_add(sl, sl, pAv[:, :]).then_inc(sm["a_imp"].sem, 1); bump(sm["a_imp"], 1)
                        ve.wait_ge(sm["a_imp"].sem, 16 * (bh + 1))
                        edge(ve.scalar_tensor_tensor(impsum[:, :], impsum[:, :], SCALE / G,
                                                     cst["cmaskNEG"][:, :], ALU.mult, ALU.add))
                        ve.sem_inc(sm["a_msk"].sem, 1); bump(sm["a_msk"], 1)
                    if bh >= 1:
                        bhp = bh - 1
                        hh = bhp % 8
                        ve.wait_ge(sm["a_gt"].sem, 16 * (bhp // 8 + 1))
                        for j in range(16):
                            qc, g = divmod(j, 4)
                            c = 16 * bhp + j
                            if j > 0:
                                tback_evac(c - 1)
                            if c >= 1:
                                ve.wait_ge(sm["a_tr"].sem, 12 * c)
                            ve.wait_ge(sm["a_co"].sem, c + 1)
                            ve.tensor_copy(cnum[0:65, :], pbr[0:65, :]).then_inc(sm["a_bev"].sem, 1); bump(sm["a_bev"], 1)
                            ve.wait_ge(sm["a_fo"].sem, c + 1)
                            ve.wait_ge(sm["a_fv"].sem, 16 * (c + 1))
                            ve.tensor_copy(fnum[0:65, :], pbr[0:65, :]).then_inc(sm["a_bev"].sem, 1); bump(sm["a_bev"], 1)
                            ve.wait_ge(sm["a_so"].sem, 8 * (c + 1))
                            ve.tensor_copy(snum[0:65, :], pbr[0:65, :]).then_inc(sm["a_bev"].sem, 1); bump(sm["a_bev"], 1)
                            ptrv = ptr[:, 0:390].rearrange("p (qt br o) -> p qt br o", qt=2, br=3)
                            gv = gtile[:, :].rearrange("p (qt gg) -> p qt gg", qt=16)
                            mv = mbr[:, 0:6].rearrange("p (qt br) -> p qt br", qt=2)
                            for r in range(2):
                                ve.wait_ge(sm["a_tr"].sem, 12 * c + 6 * (r + 1))
                                if r == 0 and c >= 1:
                                    ve.wait_ge(sm["a_tb"].sem, 4 * c)
                                edge(ve.reciprocal(zrec[:, 0:6].rearrange("p (qt br) -> p qt br", qt=2),
                                                   ptrv[:, :, :, 64]))
                                edge(ve.tensor_mul(mv,
                                                   gv[:, 4 * qc + 2 * r:4 * qc + 2 * r + 2,
                                                      12 * hh + 3 * g:12 * hh + 3 * g + 3],
                                                   zrec[:, 0:6].rearrange("p (qt br) -> p qt br", qt=2)))
                                av = acc[:, 128 * r:128 * (r + 1)].rearrange("p (qt d) -> p qt d", qt=2)
                                tv = tmpc[:, 0:128].rearrange("p (qt d) -> p qt d", qt=2)
                                edge(ve.tensor_tensor(av, ptrv[:, :, 0, 0:64],
                                                      mv[:, :, 0].unsqueeze(2).to_broadcast([128, 2, 64]),
                                                      ALU.mult))
                                edge(ve.tensor_tensor(tv, ptrv[:, :, 1, 0:64],
                                                      mv[:, :, 1].unsqueeze(2).to_broadcast([128, 2, 64]),
                                                      ALU.mult))
                                edge(ve.tensor_add(av, av, tv))
                                edge(ve.tensor_tensor(tv, ptrv[:, :, 2, 0:64],
                                                      mv[:, :, 2].unsqueeze(2).to_broadcast([128, 2, 64]),
                                                      ALU.mult))
                                ve.tensor_add(av, av, tv).then_inc(sm["a_cb"].sem, 1); bump(sm["a_cb"], 1)
                    if bh < NBH:
                        for n in range(48):
                            src = pim if n % 2 == 0 else ptr
                            ve.wait_ge(sm["a_tpv"].sem, 48 * bh + n + 1)
                            if n < 16:
                                ve.tensor_copy(vKM[:, n * 65:n * 65 + 64],
                                               src[:, 0:64]).then_inc(sm["a_evv"].sem, 1); bump(sm["a_evv"], 1)
                            else:
                                w = n - 16
                                ve.tensor_copy(vKM2[:, w * 65:w * 65 + 64],
                                               src[0:64, 0:64]).then_inc(sm["a_evv"].sem, 1); bump(sm["a_evv"], 1)
                        for qt in range(16):
                            n = 16 * bh + qt
                            srcp = pim[:, 0:128] if qt % 2 == 0 else ptr[:, 0:128]
                            ve.wait_ge(sm["a_imtp"].sem, n + 1)
                            ve.tensor_copy(impq[:, qt * 128:(qt + 1) * 128],
                                           srcp).then_inc(sm["a_imev"].sem, 1); bump(sm["a_imev"], 1)
                        ve.wait_ge(sm["a_imev"].sem, 16 * (bh + 1))
                        for qt in range(16):
                            edge(ve.max(top8[:, qt * 8:(qt + 1) * 8], impq[:, qt * 128:(qt + 1) * 128]))
                        t8v = top8[:, :].rearrange("p (qt e) -> p qt e", qt=16)
                        zrv = zr[:, :].rearrange("p (k qt) -> p k qt", k=6)
                        iv = impq[:, :].rearrange("p (qt q) -> p qt q", qt=16)
                        ev = esh[:, :].rearrange("p (qt q) -> p qt q", qt=16)
                        edge(ve.tensor_scalar(zrv[:, 0, :], t8v[:, :, 0], -1000.0, None, ALU.max))
                        ve.tensor_scalar(zrv[:, 1, :], zrv[:, 0, :], -1.0, None,
                                         ALU.mult).then_inc(sm["a_mpr"].sem, 1); bump(sm["a_mpr"], 1)
                        ve.wait_ge(sm["a_mpr"].sem, bh + 1)
                        ve.tensor_add(ev, iv, zrv[:, 1, :].unsqueeze(2).to_broadcast([128, 16, 128])
                                      ).then_inc(sm["a_lsh"].sem, 1); bump(sm["a_lsh"], 1)
                        ve.wait_ge(sm["a_es"].sem, bh + 1)
                        edge(ve.tensor_reduce(zrv[:, 3, :].unsqueeze(2), ev, AX.X, ALU.add))
                        ve.wait_ge(sm["a_es2"].sem, bh + 1)
                        ve.tensor_add(zrv[:, 3, :], zrv[:, 3, :], zrv[:, 2, :]).then_inc(sm["a_zv"].sem, 1); bump(sm["a_zv"], 1)
                        ve.wait_ge(sm["a_es3"].sem, bh + 1)
                        edge(ve.tensor_scalar(zrv[:, 5, :], zrv[:, 4, :], float(np.log(1e-10)),
                                              None, ALU.add))
                        edge(ve.tensor_add(zrv[:, 5, :], zrv[:, 5, :], zrv[:, 0, :]))
                        if "bmD" in dbg and bh >= 1:
                            ve.wait_ge(sm["a_bd"].sem, 16 * bh)
                        bv = BMq[:, :].rearrange("p (qt q) -> p qt q", qt=16)
                        sv = ev
                        edge(ve.tensor_tensor(bv, iv, t8v[:, :, 7].unsqueeze(2).to_broadcast(
                            [128, 16, 128]), ALU.is_ge))
                        edge(ve.tensor_tensor(sv, iv, zrv[:, 5, :].unsqueeze(2).to_broadcast(
                            [128, 16, 128]), ALU.is_gt))
                        ve.tensor_mul(BMq[:, :], BMq[:, :], esh[:, :]).then_inc(sm["a_bm"].sem, 1); bump(sm["a_bm"], 1)
                        for qt in range(16):
                            n = 16 * bh + qt
                            srcp = pim[:, 0:128] if qt % 2 == 0 else ptr[:, 0:128]
                            ve.wait_ge(sm["a_bmtp"].sem, n + 1)
                            if bh >= 1 and qt == 0:
                                ve.wait_ge(sm["a_fs"].sem, 256 * bh)
                            ve.tensor_scalar(BMT1[:, qt * 128:(qt + 1) * 128], srcp, 1.0,
                                             None, ALU.subtract).then_inc(sm["a_bmev"].sem, 1); bump(sm["a_bmev"], 1)
                tback_evac(16 * NBH - 1)

        full_barrier()

    if "wout" in stages:
        with ExitStack() as sc:
            wo = sc.enter_context(nc.sbuf_tensor("wo_w", [128, 16 * 16 * 128], BF16))
            am = [sc.enter_context(nc.sbuf_tensor("wo_a0", [128, 16 * 512], BF16))] * 2
            wb = sc.enter_context(nc.sbuf_tensor("wo_wb", [128, 16 * 512], F32))
            xt = [sc.enter_context(nc.sbuf_tensor(f"wo_x{i}", [128, DM], F32)) for i in range(2)]
            x1r = sc.enter_context(nc.sbuf_tensor("wo_xr", [128, DM], F32))
            x1l = sc.enter_context(nc.sbuf_tensor("wo_xl", [128, DM], F32))
            stl = sc.enter_context(nc.sbuf_tensor("wo_st", [128, 32], F32))
            txb = [sc.enter_context(nc.sbuf_tensor("wo_t0", [128, DM], BF16))] * 2
            pw = [P0, P1]
            pt = [PIM, PTR]
            pl = [P2, P3]
            w_w = rsem("wo_w2")
            w_am = [rsem("qk_x0"), rsem("qk_x1")]
            w_xl = [rsem("ln_ld0"), rsem("ln_ld1")]
            w_pe = rsem("qk_pe")
            w_ev = rsem("qk_ev")
            w_tp = rsem("ln_tp")
            w_te = rsem("ln_ev")
            w_m = rsem("ln_m")
            w_var = rsem("ln_var")
            w_xlr = rsem("ln_xl")
            w_ltp = rsem("qk_tg")
            w_lev = rsem("qk_eg")
            w_st = [rsem("ln_st0"), rsem("ln_st1")]
            w_e = rsem("ln_e")
            NI = NB * 4

            @blk.gpsimd
            def _(g_):
                for oc in range(16):
                    dma(g_, wo[:, oc * 2048:(oc + 1) * 2048].rearrange("p (ic m) -> p ic m", ic=16),
                        w_out[:, oc * 128:(oc + 1) * 128].rearrange("(ic p) m -> p ic m", p=128), w_w)

            @blk.sync
            def _(sy):
                for it in range(NI):
                    seq, tq = divmod(it, 4)
                    if it >= 1:
                        sy.wait_ge(w_pe.sem, w_pe.base + 16 * it)
                    for hf in range(2):
                        dma(sy, am[it % 2][hf * 64:hf * 64 + 64, :].rearrange("p (ic t) -> p ic t", ic=16),
                            attnFM[seq, hf:32:2, :, tq * 512:(tq + 1) * 512].transpose([1, 0, 2]),
                            w_am[0])
                    for tt in range(4):
                        n = 4 * it + tt
                        if n >= 2:
                            sy.wait_ge(w_xlr.sem, w_xlr.base + n - 1)
                        dma(sy, xt[n % 2][:, :],
                            x_in[seq, tq * 512 + tt * 128:tq * 512 + (tt + 1) * 128, :],
                            w_xl[n % 2])
                        sy.wait_ge(w_lev.sem, w_lev.base + 16 * (n + 1))
                        sy.wait_ge(w_xlr.sem, w_xlr.base + n + 1)
                        tg = tq * 512 + tt * 128
                        dma(sy, xln2T[seq, :, :, tg:tg + 128].transpose([1, 0, 2]),
                            txb[n % 2][:, :].rearrange("p (dc t) -> p dc t", dc=16), w_st[n % 2])
                        dma(sy, x1D[seq, tg:tg + 128, :], x1r[:, :], w_st[n % 2])
                for i in range(2):
                    sy.wait_ge(w_st[i].sem, w_st[i].val)

            @blk.tensor
            def _(t):
                t.wait_ge(w_w.sem, w_w.val)
                for it in range(NI):
                    seq, tq = divmod(it, 4)
                    t.wait_ge(w_am[0].sem, w_am[0].base + 32 * (it + 1))
                    amv = am[it % 2][:, :].rearrange("p (ic t) -> p ic t", ic=16)
                    for oc in range(16):
                        n = 16 * it + oc
                        if n >= 2:
                            t.wait_ge(w_ev.sem, w_ev.base + n - 1)
                        for ic in range(16):
                            mm = t.matmul(pw[n % 2][:, :],
                                          wo[:, (oc * 16 + ic) * 128:(oc * 16 + ic + 1) * 128],
                                          amv[:, ic, :], start=(ic == 0), stop=(ic == 15))
                        mm.then_inc(w_pe.sem, 1); bump(w_pe, 1)
                    for tt in range(4):
                        n = 4 * it + tt
                        t.wait_ge(w_ev.sem, w_ev.base + 16 * (it + 1))
                        for oc in range(16):
                            k = 16 * n + oc
                            if k >= 2:
                                t.wait_ge(w_te.sem, w_te.base + k - 1)
                            t.transpose(pt[k % 2][:, 0:128],
                                        wb[:, (oc * 4 + tt) * 128:(oc * 4 + tt + 1) * 128],
                                        identF_sb[:, :]).then_inc(w_tp.sem, 1); bump(w_tp, 1)
                        t.wait_ge(w_xlr.sem, w_xlr.base + n + 1)
                        for dc in range(16):
                            k = 16 * n + dc
                            if k >= 2:
                                t.wait_ge(w_lev.sem, w_lev.base + k - 1)
                            t.transpose(pl[k % 2][:, 0:128], x1l[:, dc * 128:(dc + 1) * 128],
                                        identF_sb[:, :]).then_inc(w_ltp.sem, 1); bump(w_ltp, 1)

            @blk.vector
            def _(ve):
                ec = [0]

                def edge(ins):
                    ins.then_inc(w_e.sem, 1); bump(w_e, 1)
                    ec[0] += 1
                    ve.wait_ge(w_e.sem, w_e.base + ec[0])

                edge(ve.memset(stl[:, :], 0.0))
                for it in range(NI):
                    for oc in range(16):
                        n = 16 * it + oc
                        ve.wait_ge(w_pe.sem, w_pe.base + n + 1)
                        ve.tensor_copy(wb[:, oc * 512:(oc + 1) * 512],
                                       pw[n % 2][:, :]).then_inc(w_ev.sem, 1); bump(w_ev, 1)
                    for tt in range(4):
                        n = 4 * it + tt
                        ve.wait_ge(w_xl[n % 2].sem, w_xl[n % 2].base + 16 * (n // 2 + 1))
                        if n >= 1:
                            ve.wait_ge(w_xlr.sem, w_xlr.base + n)
                        for oc in range(16):
                            k = 16 * n + oc
                            ve.wait_ge(w_tp.sem, w_tp.base + k + 1)
                            if n >= 1 and oc == 0:
                                ve.wait_ge(w_st[(n - 1) % 2].sem, w_st[(n - 1) % 2].base + 32 * ((n - 1) // 2 + 1))
                            ve.tensor_add(x1r[:, oc * 128:(oc + 1) * 128], pt[k % 2][:, 0:128],
                                          xt[n % 2][:, oc * 128:(oc + 1) * 128]
                                          ).then_inc(w_te.sem, 1); bump(w_te, 1)
                        for cch in range(4):
                            edge(ve.bn_stats(stl[:, 6 * cch:6 * cch + 6],
                                             x1r[:, 512 * cch:512 * (cch + 1)]))
                        edge(ve.bn_aggr(stl[:, 24:26], stl[:, 0:12]))
                        edge(ve.bn_aggr(stl[:, 26:28], stl[:, 12:24]))
                        edge(ve.tensor_tensor(stl[:, 2:3], stl[:, 24:25], stl[:, 26:27], ALU.add))
                        edge(ve.tensor_scalar(stl[:, 2:3], stl[:, 2:3], 0.5, None, ALU.mult))
                        edge(ve.tensor_tensor(stl[:, 3:4], stl[:, 24:25], stl[:, 26:27],
                                              ALU.subtract))
                        edge(ve.tensor_scalar(stl[:, 3:4], stl[:, 3:4], 0.5, None, ALU.mult))
                        edge(ve.tensor_scalar(stl[:, 3:4], stl[:, 3:4], stl[:, 3:4], None, ALU.mult))
                        edge(ve.tensor_tensor(stl[:, 4:5], stl[:, 25:26], stl[:, 27:28], ALU.add))
                        ve.tensor_scalar(stl[:, 4:5], stl[:, 4:5], 0.5, stl[:, 3:4],
                                          ALU.mult, ALU.add).then_inc(w_m.sem, 1); bump(w_m, 1)
                        ve.wait_ge(w_var.sem, w_var.base + n + 1)
                        edge(ve.reciprocal(stl[:, 6:7], stl[:, 5:6]))
                        ve.tensor_scalar(x1l[:, :], x1r[:, :], stl[:, 2:3], stl[:, 6:7],
                                         ALU.subtract, ALU.mult).then_inc(w_xlr.sem, 1); bump(w_xlr, 1)
                        for dc in range(16):
                            k = 16 * n + dc
                            ve.wait_ge(w_ltp.sem, w_ltp.base + k + 1)
                            if n >= 1 and dc == 0:
                                ve.wait_ge(w_st[(n - 1) % 2].sem, w_st[(n - 1) % 2].base + 32 * ((n - 1) // 2 + 1))
                            ve.tensor_copy(txb[n % 2][:, dc * 128:(dc + 1) * 128],
                                           pl[k % 2][:, 0:128]).then_inc(w_lev.sem, 1); bump(w_lev, 1)

            @blk.scalar
            def _(s):
                s.wait_ge(w_e.sem, w_e.base + 1)
                for it in range(NI):
                    for tt in range(4):
                        n = 4 * it + tt
                        s.wait_ge(w_m.sem, w_m.base + n + 1)
                        s.activation(stl[:, 5:6], stl[:, 4:5], AF.Sqrt, bias=smallc[:, 0:1],
                                     scale=1.0).then_inc(w_var.sem, 1); bump(w_var, 1)

        full_barrier()

    if "ffna" in stages:
        with ExitStack() as sc:
            xr = sc.enter_context(nc.sbuf_tensor("fa_x", [128, 16 * NB * S], BF16))
            wgs = [sc.enter_context(nc.sbuf_tensor(f"fa_wg{i}", [128, 16 * 128], BF16)) for i in range(2)]
            wus = [sc.enter_context(nc.sbuf_tensor(f"fa_wu{i}", [128, 16 * 128], BF16)) for i in range(2)]
            gsil = sc.enter_context(nc.sbuf_tensor("fa_gs", [128, 512], F32))
            gbuf = [sc.enter_context(nc.sbuf_tensor("fa_gb0", [128, NB * S], BF16))] * 2
            pg = [P0, P1]
            pu = [P2, P3]
            f_x = rsem("cm_ld")
            f_w = [rsem("fa_w0"), rsem("fa_w1")]
            f_pg = rsem("ro_pe")
            f_pu = rsem("ro_t2")
            f_si = rsem("ro_dv")
            f_gt = rsem("cm_mm")
            f_st = [rsem("fa_st0"), rsem("fa_st1")]

            @blk.sync
            def _(sy):
                for seq in range(NB):
                    for dc in range(16):
                        dma(sy, xr[:, (dc * NB + seq) * S:(dc * NB + seq + 1) * S],
                            xln2T[seq, dc, :, :], f_x)

            @blk.gpsimd
            def _(g_):
                for hc in range(65):
                    if hc < 64:
                        if hc >= 2:
                            g_.wait_ge(f_pu.sem, f_pu.base + 8 * (hc - 1))
                        dma(g_, wgs[hc % 2][:, :].rearrange("p (ic m) -> p ic m", ic=16),
                            ffn_wgT[:, hc * 128:(hc + 1) * 128].rearrange("(ic p) m -> p ic m", p=128),
                            f_w[hc % 2])
                        dma(g_, wus[hc % 2][:, :].rearrange("p (ic m) -> p ic m", ic=16),
                            ffn_wuT[:, hc * 128:(hc + 1) * 128].rearrange("(ic p) m -> p ic m", p=128),
                            f_w[hc % 2])
                    if hc >= 1:
                        g_.wait_ge(f_gt.sem, f_gt.base + 8 * hc)
                        dma(g_, gatedD[hc - 1, :, :], gbuf[(hc - 1) % 2][:, :], f_st[(hc - 1) % 2])
                for i in range(2):
                    g_.wait_ge(f_st[i].sem, f_st[i].val)

            @blk.tensor
            def _(t):
                t.wait_ge(f_x.sem, f_x.val)
                for hc in range(64):
                    t.wait_ge(f_w[hc % 2].sem, f_w[hc % 2].base + 32 * (hc // 2 + 1))
                    for tb in range(8):
                        n = 8 * hc + tb
                        if n >= 2:
                            t.wait_ge(f_si.sem, f_si.base + n - 1)
                        for ic in range(16):
                            mm = t.matmul(pg[n % 2][:, :], wgs[hc % 2][:, ic * 128:(ic + 1) * 128],
                                          xr[:, :].rearrange("p (ic t) -> p ic t", ic=16)[:, ic,
                                          tb * 512:(tb + 1) * 512],
                                          start=(ic == 0), stop=(ic == 15))
                        mm.then_inc(f_pg.sem, 1); bump(f_pg, 1)
                        if n >= 2:
                            t.wait_ge(f_gt.sem, f_gt.base + n - 1)
                        for ic in range(16):
                            mm = t.matmul(pu[n % 2][:, :], wus[hc % 2][:, ic * 128:(ic + 1) * 128],
                                          xr[:, :].rearrange("p (ic t) -> p ic t", ic=16)[:, ic,
                                          tb * 512:(tb + 1) * 512],
                                          start=(ic == 0), stop=(ic == 15))
                        mm.then_inc(f_pu.sem, 1); bump(f_pu, 1)

            @blk.scalar
            def _(s):
                for hc in range(64):
                    for tb in range(8):
                        n = 8 * hc + tb
                        s.wait_ge(f_pg.sem, f_pg.base + n + 1)
                        if n >= 1:
                            s.wait_ge(f_gt.sem, f_gt.base + n)
                        s.activation(gsil[:, :], pg[n % 2][:, :], AF.Silu).then_inc(f_si.sem, 1); bump(f_si, 1)

            @blk.vector
            def _(ve):
                for hc in range(64):
                    for tb in range(8):
                        n = 8 * hc + tb
                        ve.wait_ge(f_si.sem, f_si.base + n + 1)
                        ve.wait_ge(f_pu.sem, f_pu.base + n + 1)
                        if hc >= 1 and tb == 0:
                            ve.wait_ge(f_st[(hc - 1) % 2].sem,
                                       f_st[(hc - 1) % 2].base + 16 * ((hc - 1) // 2 + 1))
                        ve.tensor_mul(gbuf[hc % 2][:, tb * 512:(tb + 1) * 512], gsil[:, :],
                                      pu[n % 2][:, :]).then_inc(f_gt.sem, 1); bump(f_gt, 1)

        full_barrier()

    if "ffnb" in stages:
        with ExitStack() as sc:
            gr = sc.enter_context(nc.sbuf_tensor("fb_g", [128, 64 * 512], BF16))
            wds = [sc.enter_context(nc.sbuf_tensor(f"fb_w{i}", [128, 64 * 128], BF16)) for i in range(2)]
            ob = sc.enter_context(nc.sbuf_tensor("fb_o", [128, 16 * 512], F32))
            x1t = [sc.enter_context(nc.sbuf_tensor("fb_x0", [128, DM], F32))] * 2
            orow = [sc.enter_context(nc.sbuf_tensor(f"fb_or{i}", [128, DM], F32)) for i in range(2)]
            pd = [P0, P1]
            pt2 = [PIM, PTR]
            b_g = rsem("cm_ld")
            b_w = [rsem("fb_w0"), rsem("fb_w1")]
            b_x = [rsem("ln_ld0"), rsem("ln_ld1")]
            b_pe = rsem("qk_pe")
            b_ev = rsem("qk_ev")
            b_tp = rsem("ln_tp")
            b_te = rsem("ln_ev")
            b_st = [rsem("ln_st0"), rsem("ln_st1")]
            NSH = 8

            @blk.sync
            def _(sy):
                for sh in range(NSH):
                    if sh >= 1:
                        sy.wait_ge(b_pe.sem, b_pe.base + 16 * sh)
                    for hc in range(64):
                        dma(sy, gr[:, hc * 512:(hc + 1) * 512],
                            gatedD[hc, :, sh * 512:(sh + 1) * 512], b_g)
                    for tt in range(4):
                        n = 4 * sh + tt
                        tg = sh * 512 + tt * 128
                        seq, trow = divmod(tg, S)
                        if n >= 1:
                            sy.wait_ge(b_te.sem, b_te.base + 16 * n)
                        dma(sy, x1t[n % 2][:, :], x1D[seq, trow:trow + 128, :], b_x[n % 2])
                        sy.wait_ge(b_te.sem, b_te.base + 16 * (n + 1))
                        dma(sy, out[seq, trow:trow + 128, :], orow[n % 2][:, :], b_st[n % 2])
                for i in range(2):
                    sy.wait_ge(b_st[i].sem, b_st[i].val)

            @blk.gpsimd
            def _(g_):
                for sh in range(NSH):
                    for oc in range(16):
                        n = 16 * sh + oc
                        if n >= 2:
                            g_.wait_ge(b_pe.sem, b_pe.base + n - 1)
                        dma(g_, wds[n % 2][:, :].rearrange("p (hc m) -> p hc m", hc=64),
                            ffn_wdT[:, oc * 128:(oc + 1) * 128].rearrange("(hc p) m -> p hc m", p=128),
                            b_w[n % 2])

            @blk.tensor
            def _(t):
                for sh in range(NSH):
                    t.wait_ge(b_g.sem, b_g.base + 64 * 16 * (sh + 1))
                    for oc in range(16):
                        n = 16 * sh + oc
                        t.wait_ge(b_w[n % 2].sem, b_w[n % 2].base + 16 * (n // 2 + 1))
                        if n >= 2:
                            t.wait_ge(b_ev.sem, b_ev.base + n - 1)
                        for hc in range(64):
                            mm = t.matmul(pd[n % 2][:, :],
                                          wds[n % 2][:, hc * 128:(hc + 1) * 128],
                                          gr[:, :].rearrange("p (hc t) -> p hc t", hc=64)[:, hc, :],
                                          start=(hc == 0), stop=(hc == 63))
                        mm.then_inc(b_pe.sem, 1); bump(b_pe, 1)
                    for tt in range(4):
                        n = 4 * sh + tt
                        t.wait_ge(b_ev.sem, b_ev.base + 16 * (sh + 1))
                        for oc in range(16):
                            k = 16 * n + oc
                            if k >= 2:
                                t.wait_ge(b_te.sem, b_te.base + k - 1)
                            t.transpose(pt2[k % 2][:, 0:128],
                                        ob[:, (oc * 4 + tt) * 128:(oc * 4 + tt) * 128 + 128],
                                        identF_sb[:, :]).then_inc(b_tp.sem, 1); bump(b_tp, 1)

            @blk.vector
            def _(ve):
                for sh in range(NSH):
                    for oc in range(16):
                        n = 16 * sh + oc
                        ve.wait_ge(b_pe.sem, b_pe.base + n + 1)
                        if sh >= 1 and oc == 0:
                            ve.wait_ge(b_tp.sem, b_tp.base + 64 * sh)
                        ve.tensor_copy(ob[:, oc * 512:(oc + 1) * 512],
                                       pd[n % 2][:, :]).then_inc(b_ev.sem, 1); bump(b_ev, 1)
                    for tt in range(4):
                        n = 4 * sh + tt
                        ve.wait_ge(b_x[n % 2].sem, b_x[n % 2].base + 16 * (n // 2 + 1))
                        for oc in range(16):
                            k = 16 * n + oc
                            ve.wait_ge(b_tp.sem, b_tp.base + k + 1)
                            if n >= 2 and oc == 0:
                                ve.wait_ge(b_st[n % 2].sem, b_st[n % 2].base + 16 * (n // 2))
                            ve.tensor_add(orow[n % 2][:, oc * 128:(oc + 1) * 128], pt2[k % 2][:, 0:128],
                                          x1t[n % 2][:, oc * 128:(oc + 1) * 128]
                                          ).then_inc(b_te.sem, 1); bump(b_te, 1)

        full_barrier()

    if "ffnb" not in stages:
        s_fin = sem("fin")
        zb = ctx.enter_context(nc.sbuf_tensor("zb", [128, DM], F32))

        @blk.vector
        def _(ve):
            ve.memset(zb[:, :], 0.0).then_inc(s_fin.sem, 1)
        s_fin.val += 1

        @blk.sync
        def _(sy):
            sy.wait_ge(s_fin.sem, s_fin.val)
            for seq in range(NB):
                for tc in range(16):
                    dma(sy, out[seq, tc * 128:(tc + 1) * 128, :], zb[:, :], s_fin)
            sy.wait_ge(s_fin.sem, s_fin.val)

    ctx.close()
    nc.finalize()
    return nc


def prep_inputs(inputs):
    c = make_consts()
    w = {
        "w_qkv": inputs["L0_attn_w_qkv"],
        "w_gate": inputs["L0_attn_w_gate"],
        "b_gate": np.ascontiguousarray(inputs["L0_attn_b_gate"].reshape(96, 1)),
        "w_out": inputs["L0_attn_w_out"],
        "k_cw1": inputs["L0_attn_k_cw1"],
        "k_cb1": np.ascontiguousarray(inputs["L0_attn_k_cb1"].reshape(CDIM, 1)),
        "k_cw2": inputs["L0_attn_k_cw2"],
        "k_cb2": np.ascontiguousarray(inputs["L0_attn_k_cb2"].reshape(DH, 1)),
        "v_cw1": inputs["L0_attn_v_cw1"],
        "v_cb1": np.ascontiguousarray(inputs["L0_attn_v_cb1"].reshape(CDIM, 1)),
        "v_cw2": inputs["L0_attn_v_cw2"],
        "v_cb2": np.ascontiguousarray(inputs["L0_attn_v_cb2"].reshape(DH, 1)),
        "kposT": np.ascontiguousarray(inputs["L0_attn_k_pos"].transpose(0, 2, 1)),
        "vposT": np.ascontiguousarray(inputs["L0_attn_v_pos"].transpose(0, 2, 1)),
        "memckT": np.ascontiguousarray(inputs["L0_attn_mem_ck"].reshape(KVH, DH).T),
        "memcv": np.ascontiguousarray(inputs["L0_attn_mem_cv"].reshape(KVH, DH)),
        "ffn_wgT": np.ascontiguousarray(inputs["L0_ffn_w_gate"].T),
        "ffn_wuT": np.ascontiguousarray(inputs["L0_ffn_w_up"].T),
        "ffn_wdT": np.ascontiguousarray(inputs["L0_ffn_w_down"].T),
    }
    w.update(c)
    return w


def kernel(**inputs):
    x = inputs["L0_attn_hidden_states"]
    shared = prep_inputs(inputs)
    nc = build()
    in_maps = []
    for core in range(8):
        m = dict(shared)
        m["x"] = np.ascontiguousarray(x[2 * core:2 * core + 2])
        in_maps.append(m)
    res = run_bass_kernel_spmd(nc, in_maps, core_ids=list(range(8)))
    return np.concatenate([r["out"] for r in res.results], axis=0)


# version 04
# speedup vs baseline: 1.2065x; 1.2065x over previous
import numpy as np
import ml_dtypes
from contextlib import ExitStack

import concourse.bass as bass
import concourse.mybir as mybir
from concourse.bass_utils import run_bass_kernel_spmd

F32 = mybir.dt.float32
BF16 = mybir.dt.bfloat16
AF = mybir.ActivationFunctionType
ALU = mybir.AluOpType
AX = mybir.AxisListType

NB = 2
S = 2048
DM = 2048
HEADS, KVH, G, DH = 32, 8, 4, 64
CBLK = SBLK = 16
NSEL = 8
WIN = 64
SCALE = DH ** -0.5
CDIM = CBLK * DH
NBLK = S // CBLK
DI = HEADS * DH
DKV = KVH * DH
DFF = 8192
BIGF = 1600.0
NEG30 = -1e30
EPS = 1e-5

ALL_STAGES = ("ln1", "qkv", "rope", "cmp", "attn", "wout", "ffna", "ffnb")


def make_consts():
    c = {}
    ident = np.eye(128, dtype=np.float32)
    c["identF"] = ident
    c["identB"] = ident
    R = np.zeros((DH, DH), np.float32)
    for i in range(DH // 2):
        R[2 * i, 2 * i + 1] = -1.0
        R[2 * i + 1, 2 * i] = 1.0
    R2T = np.zeros((128, 128), np.float32)
    R2T[:64, :64] = R.T
    R2T[64:, 64:] = R.T
    c["R2T"] = R2T
    inv = 1.0 / 10000.0 ** (np.arange(0, DH, 2, dtype=np.float32) / DH)
    ang = np.arange(S, dtype=np.float32)[None, :] * inv[:, None]
    cos = np.repeat(np.cos(ang), 2, axis=0)
    sin = np.repeat(np.sin(ang), 2, axis=0)
    c["C2"] = np.ascontiguousarray(np.concatenate([cos, cos], 0))
    c["S2"] = np.ascontiguousarray(np.concatenate([sin, sin], 0))
    blk = np.arange(NBLK)[:, None]
    qi = np.arange(S)[None, :]
    valid = (16 * (blk + 1) - 1) < qi
    c["cmaskADD"] = np.where(valid, 0.0, -BIGF).astype(np.float32)
    c["cmaskNEG"] = np.where(valid, 0.0, NEG30).astype(np.float32)
    key = np.arange(S)[None, :]
    c["EBIG"] = np.where(np.arange(NBLK)[:, None] == (key // SBLK), BIGF, 0.0).astype(np.float32)
    kk = np.arange(128)[:, None]
    qq = np.arange(128)[None, :]
    own01 = ((qq // 16) == (kk // 16)) & ((qq % 16) >= (kk % 16))
    c["ownADD"] = np.where(own01, BIGF, 0.0).astype(np.float32)
    j = np.arange(WIN)[:, None]
    q = np.arange(WIN)[None, :]
    c["sADDp"] = np.where(q <= j, 0.0, -BIGF).astype(np.float32)
    c["sADDo"] = np.where(j <= q, 0.0, -BIGF).astype(np.float32)
    return c


class Ctr:
    __slots__ = ("sem", "val")

    def __init__(self, sem):
        self.sem = sem
        self.val = 0


def build(dbg=(), stages=ALL_STAGES):
    nc = bass.Bass("TRN2", target_bir_lowering=False)

    def din(name, shape, dtype=F32):
        return nc.dram_tensor(name, list(shape), dtype, kind="ExternalInput").ap()

    def dscr(name, shape, dtype):
        kind = "ExternalOutput" if name in dbg else "Internal"
        return nc.dram_tensor(name, list(shape), dtype, kind=kind).ap()

    x_in = din("x", (NB, S, DM))
    w_qkv = din("w_qkv", (DM, DI + 2 * DKV))
    w_gate = din("w_gate", (DM, 96))
    b_gate = din("b_gate", (96, 1))
    w_out = din("w_out", (DI, DM))
    k_cw1 = din("k_cw1", (CDIM, CDIM))
    k_cb1 = din("k_cb1", (CDIM, 1))
    k_cw2 = din("k_cw2", (CDIM, DH))
    k_cb2 = din("k_cb2", (DH, 1))
    v_cw1 = din("v_cw1", (CDIM, CDIM))
    v_cb1 = din("v_cb1", (CDIM, 1))
    v_cw2 = din("v_cw2", (CDIM, DH))
    v_cb2 = din("v_cb2", (DH, 1))
    kposT = din("kposT", (KVH, DH, CBLK))
    vposT = din("vposT", (KVH, DH, CBLK))
    memckT = din("memckT", (DH, KVH))
    memcv = din("memcv", (KVH, DH))
    ffn_wgT = din("ffn_wgT", (DM, DFF))
    ffn_wuT = din("ffn_wuT", (DM, DFF))
    ffn_wdT = din("ffn_wdT", (DFF, DM))
    identF = din("identF", (128, 128))
    identB_in = din("identB", (128, 128))
    R2T_in = din("R2T", (128, 128))
    C2_in = din("C2", (128, S))
    S2_in = din("S2", (128, S))
    cmaskADD_in = din("cmaskADD", (128, S))
    cmaskNEG_in = din("cmaskNEG", (128, S))
    EBIG_in = din("EBIG", (128, S))
    ownADD_in = din("ownADD", (128, 128))
    sADDp_in = din("sADDp", (64, 64))
    sADDo_in = din("sADDo", (64, 64))

    out = nc.dram_tensor("out", [NB, S, DM], F32, kind="ExternalOutput").ap()
    xlnT = dscr("xlnT", (NB, 16, 128, S), BF16)
    qF = dscr("qF", (NB, 16, 128, S), F32)
    kF = dscr("kF", (NB, 4, 128, S), F32)
    vF = dscr("vF", (NB, 4, 128, S), F32)
    gatesT = dscr("gatesT", (NB, 16, 128, 96), F32)
    rqT = dscr("rqT", (NB, 16, 128, S), BF16)
    rkT = dscr("rkT", (NB, 4, 128, S), BF16)
    kbT = dscr("kbT", (NB, 4, 128, S), BF16)
    vbT = dscr("vbT", (NB, 4, 128, S), BF16)
    ckD = dscr("ckD", (DH, NB * KVH * NBLK), F32)
    cvD = dscr("cvD", (DH, NB * KVH * NBLK), F32)
    attnFM = dscr("attnFM", (NB, 32, 64, S), BF16)
    x1D = dscr("x1D", (NB, S, DM), F32)
    xln2T = dscr("xln2T", (NB, 16, 128, S), BF16)
    gatedD = dscr("gatedD", (64, 128, NB * S), BF16)
    bmD = dscr("bmD", (NB, KVH, 16, 128, 128), BF16)

    ctx = ExitStack()
    blk = ctx.enter_context(nc.Block())
    _sems = {}

    def sem(name):
        if name not in _sems:
            _sems[name] = Ctr(ctx.enter_context(nc.semaphore(name)))
        return _sems[name]

    class RS:
        __slots__ = ("c", "base")

        def __init__(self, c):
            self.c = c
            self.base = c.val

        @property
        def sem(self):
            return self.c.sem

        @property
        def val(self):
            return self.c.val

    def rsem(pool):
        return RS(sem(pool))

    def bump(x, n=1):
        (x.c if isinstance(x, RS) else x).val += n

    def dma(eng, dst, src, ctr):
        tgt = ctr.c if isinstance(ctr, RS) else ctr
        eng.dma_start(dst, src).then_inc(tgt.sem, 16)
        tgt.val += 16

    def full_barrier():
        bar = sem("bar")
        bar.val += 5
        v = bar.val

        @blk.tensor
        def _(t):
            t.sem_inc(bar.sem, 1)
            t.wait_ge(bar.sem, v)

        @blk.vector
        def _(ve):
            ve.sem_inc(bar.sem, 1)
            ve.wait_ge(bar.sem, v)

        @blk.scalar
        def _(s):
            s.sem_inc(bar.sem, 1)
            s.wait_ge(bar.sem, v)

        @blk.gpsimd
        def _(g):
            g.sem_inc(bar.sem, 1)
            g.wait_ge(bar.sem, v)

        @blk.sync
        def _(sy):
            sy.sem_inc(bar.sem, 1)
            sy.wait_ge(bar.sem, v)

    ckT = ctx.enter_context(nc.sbuf_tensor("ckT", [DH, NB * KVH * NBLK], BF16))
    cvT = ctx.enter_context(nc.sbuf_tensor("cvT", [DH, NB * KVH * NBLK], F32))
    cvKM = ctx.enter_context(nc.sbuf_tensor("cvKM", [128, NB * KVH * 65], BF16))
    memcv_st = ctx.enter_context(nc.sbuf_tensor("memcv_st", [1, KVH * 65], BF16))
    memck_sb = ctx.enter_context(nc.sbuf_tensor("memck_sb", [DH, KVH], BF16))
    identB = ctx.enter_context(nc.sbuf_tensor("identB_sb", [128, 128], BF16))
    identF_sb = ctx.enter_context(nc.sbuf_tensor("identF_sb", [128, 128], F32))
    smallc = ctx.enter_context(nc.sbuf_tensor("smallc", [128, 4], F32))
    P0 = ctx.enter_context(nc.psum_tensor("P0", [128, 512], F32))
    P1 = ctx.enter_context(nc.psum_tensor("P1", [128, 512], F32))
    P2 = ctx.enter_context(nc.psum_tensor("P2", [128, 512], F32))
    P3 = ctx.enter_context(nc.psum_tensor("P3", [128, 512], F32))
    P4 = ctx.enter_context(nc.psum_tensor("P4", [128, 512], F32))
    PTR = ctx.enter_context(nc.psum_tensor("PTR", [128, 390], F32))
    PIM = ctx.enter_context(nc.psum_tensor("PIM", [128, 128], F32))
    PMEM = ctx.enter_context(nc.psum_tensor("PMEM", [1, 512], F32))
    cst = {}
    for nm, src, shp, dt in [("R2T", R2T_in, (128, 128), F32), ("C2", C2_in, (128, S), F32),
                             ("S2", S2_in, (128, S), F32), ("cmaskADD", cmaskADD_in, (128, S), BF16),
                             ("cmaskNEG", cmaskNEG_in, (128, S), F32), ("EBIG", EBIG_in, (128, S), BF16),
                             ("ownADD", ownADD_in, (128, 128), BF16), ("sADDp", sADDp_in, (64, 64), BF16),
                             ("sADDo", sADDo_in, (64, 64), BF16)]:
        cst[nm] = ctx.enter_context(nc.sbuf_tensor("c_" + nm, list(shp), dt))

    ld0 = sem("ld0")

    @blk.gpsimd
    def _(g):
        dma(g, identB[:, :], identB_in[:, :], ld0)
        dma(g, identF_sb[:, :], identF[:, :], ld0)
        for nm, src in [("R2T", R2T_in), ("C2", C2_in), ("S2", S2_in), ("cmaskADD", cmaskADD_in),
                        ("cmaskNEG", cmaskNEG_in), ("EBIG", EBIG_in), ("ownADD", ownADD_in),
                        ("sADDp", sADDp_in), ("sADDo", sADDo_in)]:
            dma(g, cst[nm][:, :], src[:, :], ld0)
        dma(g, memck_sb[:, :], memckT[:, :], ld0)
        for h in range(KVH):
            dma(g, memcv_st[0:1, h * 65:h * 65 + 64], memcv[h:h + 1, :], ld0)

    @blk.vector
    def _(ve):
        ve.wait_ge(ld0.sem, ld0.val)
        ve.memset(memcv_st[0:1, :].rearrange("p (h o) -> p h o", h=KVH)[:, :, 64:65], 1.0)
        ve.memset(smallc[:, 0:1], EPS)
        ve.memset(smallc[:, 1:2], -1000.0)

    full_barrier()

    if "ln1" in stages:
        with ExitStack() as sc:
            NT = NB * 16
            xt = [sc.enter_context(nc.sbuf_tensor(f"ln_x{i}", [128, DM], F32)) for i in range(2)]
            xl = [sc.enter_context(nc.sbuf_tensor(f"ln_o{i}", [128, DM], F32)) for i in range(2)]
            tx = [sc.enter_context(nc.sbuf_tensor(f"ln_t{i}", [128, DM], BF16)) for i in range(2)]
            st = sc.enter_context(nc.sbuf_tensor("ln_st", [128, 32], F32))
            sq = sc.enter_context(nc.sbuf_tensor("ln_sq", [128, DM], F32))
            pt = [PIM, PTR]
            s_ld = [sem("ln_ld0"), sem("ln_ld1")]
            s_m = sem("ln_m")
            s_var = sem("ln_var")
            s_xl = sem("ln_xl")
            s_tp = sem("ln_tp")
            s_ev = sem("ln_ev")
            s_st = [sem("ln_st0"), sem("ln_st1")]

            @blk.sync
            def _(sy):
                for i in range(min(2, NT)):
                    seq, tc = divmod(i, 16)
                    dma(sy, xt[i % 2][:, :], x_in[seq, tc * 128:(tc + 1) * 128, :], s_ld[i % 2])
                for i in range(NT):
                    sy.wait_ge(s_ev.sem, 16 * (i + 1))
                    seq, tc = divmod(i, 16)
                    dma(sy, xlnT[seq, :, :, tc * 128:(tc + 1) * 128].transpose([1, 0, 2]),
                        tx[i % 2][:, :].rearrange("p (dc t) -> p dc t", dc=16), s_st[i % 2])
                    if i + 2 < NT:
                        sy.wait_ge(s_xl.sem, i + 1)
                        seq2, tc2 = divmod(i + 2, 16)
                        dma(sy, xt[i % 2][:, :], x_in[seq2, tc2 * 128:(tc2 + 1) * 128, :], s_ld[i % 2])
                sy.wait_ge(s_st[0].sem, s_st[0].val)
                sy.wait_ge(s_st[1].sem, s_st[1].val)

            s_e = sem("ln_e")
            s_sq = sem("ln_sq2")
            s_v2 = sem("ln_v2")

            @blk.vector
            def _(ve):
                for i in range(NT):
                    bi = i % 2
                    ve.wait_ge(s_ld[i % 2].sem, 16 * (i // 2 + 1))
                    if i >= 1:
                        ve.wait_ge(s_xl.sem, i)
                    ve.tensor_reduce(st[:, 0:1], xt[bi][:, :], AX.X, ALU.add).then_inc(s_e.sem, 1); bump(s_e, 1)
                    ve.wait_ge(s_e.sem, 5 * i + 1)
                    ve.tensor_scalar(st[:, 2:3], st[:, 0:1], 1.0 / DM, None, ALU.mult).then_inc(s_e.sem, 1); bump(s_e, 1)
                    ve.wait_ge(s_sq.sem, i + 1)
                    ve.tensor_reduce(st[:, 1:2], sq[:, :], AX.X, ALU.add).then_inc(s_v2.sem, 1); bump(s_v2, 1)
                    ve.wait_ge(s_v2.sem, i + 1)
                    ve.tensor_scalar(st[:, 3:4], st[:, 2:3], st[:, 2:3], None, ALU.mult).then_inc(s_e.sem, 1); bump(s_e, 1)
                    ve.wait_ge(s_e.sem, 5 * i + 3)
                    ve.tensor_scalar(st[:, 4:5], st[:, 1:2], 1.0 / DM, st[:, 3:4],
                                     ALU.mult, ALU.subtract).then_inc(s_m.sem, 1); bump(s_m, 1)
                    ve.wait_ge(s_var.sem, i + 1)
                    ve.reciprocal(st[:, 27:28], st[:, 26:27]).then_inc(s_e.sem, 1); bump(s_e, 1)
                    ve.wait_ge(s_e.sem, 5 * i + 4)
                    ve.tensor_scalar(xl[bi][:, :], xt[bi][:, :], st[:, 2:3], st[:, 27:28],
                                     ALU.subtract, ALU.mult).then_inc(s_xl.sem, 1); bump(s_xl, 1)
                    ve.sem_inc(s_e.sem, 1); bump(s_e, 1)
                    for dc in range(16):
                        n = 16 * i + dc
                        ve.wait_ge(s_tp.sem, n + 1)
                        if i >= 2 and dc == 0:
                            ve.wait_ge(s_st[i % 2].sem, 16 * (i // 2))
                        ve.tensor_copy(tx[bi][:, dc * 128:(dc + 1) * 128], pt[n % 2][:, 0:128]).then_inc(s_ev.sem, 1); bump(s_ev, 1)

            @blk.scalar
            def _(s):
                for i in range(NT):
                    bi = i % 2
                    s.wait_ge(s_ld[i % 2].sem, 16 * (i // 2 + 1))
                    if i >= 1:
                        s.wait_ge(s_v2.sem, i)
                    s.activation(sq[:, :], xt[bi][:, :], AF.Square).then_inc(s_sq.sem, 1); bump(s_sq, 1)
                    s.wait_ge(s_m.sem, i + 1)
                    s.activation(st[:, 26:27], st[:, 4:5], AF.Sqrt, bias=smallc[:, 0:1],
                                 scale=1.0).then_inc(s_var.sem, 1); bump(s_var, 1)

            @blk.tensor
            def _(t):
                for i in range(NT):
                    bi = i % 2
                    t.wait_ge(s_xl.sem, i + 1)
                    for dc in range(16):
                        n = 16 * i + dc
                        if n >= 2:
                            t.wait_ge(s_ev.sem, n - 1)
                        t.transpose(pt[n % 2][:, 0:128], xl[bi][:, dc * 128:(dc + 1) * 128],
                                    identF_sb[:, :]).then_inc(s_tp.sem, 1); bump(s_tp, 1)

        full_barrier()

    if "qkv" in stages:
        with ExitStack() as sc:
            OC = 24
            wq = sc.enter_context(nc.sbuf_tensor("qk_w", [128, OC * 16 * 128], BF16))
            wg = sc.enter_context(nc.sbuf_tensor("qk_wg", [128, 16 * 96], BF16))
            bg = sc.enter_context(nc.sbuf_tensor("qk_bg", [96, 1], F32))
            xs = [sc.enter_context(nc.sbuf_tensor(f"qk_x{i}", [128, 16 * 512], BF16)) for i in range(2)]
            stg = [sc.enter_context(nc.sbuf_tensor(f"qk_s{i}", [128, 512], F32)) for i in range(4)]
            gst = sc.enter_context(nc.sbuf_tensor("qk_gs", [96, 512], F32))
            gtt = sc.enter_context(nc.sbuf_tensor("qk_gt", [128, 4 * 96], F32))
            pq = [P0, P1]
            pg = P2
            pgt = P3
            s_w = sem("qk_w")
            s_x = [sem("qk_x0"), sem("qk_x1")]
            s_pe = sem("qk_pe")
            s_ev = sem("qk_ev")
            s_pg = sem("qk_pg")
            s_sg = sem("qk_sg")
            s_tg = sem("qk_tg")
            s_eg = sem("qk_eg")
            s_st = [sem(f"qk_st{i}") for i in range(4)]
            s_gs = sem("qk_gs2")
            NT = NB * 4

            @blk.gpsimd
            def _(g):
                for oc in range(OC):
                    dma(g, wq[:, oc * 2048:(oc + 1) * 2048].rearrange("p (dc m) -> p dc m", dc=16),
                        w_qkv[:, oc * 128:(oc + 1) * 128].rearrange("(dc p) m -> p dc m", p=128), s_w)
                dma(g, wg[:, :].rearrange("p (dc m) -> p dc m", dc=16),
                    w_gate[:, :].rearrange("(dc p) m -> p dc m", p=128), s_w)
                dma(g, bg[:, :], b_gate[:, :], s_w)

            @blk.sync
            def _(sy):
                for it in range(min(2, NT)):
                    seq, tq = divmod(it, 4)
                    dma(sy, xs[it % 2][:, :].rearrange("p (dc t) -> p dc t", dc=16),
                        xlnT[seq, :, :, tq * 512:(tq + 1) * 512].transpose([1, 0, 2]), s_x[it % 2])
                for it in range(NT):
                    seq, tq = divmod(it, 4)
                    for oc in range(OC):
                        n = it * OC + oc
                        sy.wait_ge(s_ev.sem, n + 1)
                        dstf = (qF[seq, oc, :, tq * 512:(tq + 1) * 512] if oc < 16 else
                                kF[seq, oc - 16, :, tq * 512:(tq + 1) * 512] if oc < 20 else
                                vF[seq, oc - 20, :, tq * 512:(tq + 1) * 512])
                        dma(sy, dstf, stg[n % 4][:, :], s_st[n % 4])
                    sy.wait_ge(s_eg.sem, 4 * (it + 1))
                    dma(sy, gatesT[seq, tq * 4:(tq + 1) * 4, :, :].transpose([1, 0, 2]),
                        gtt[:, :].rearrange("p (tt g) -> p tt g", tt=4), s_gs)
                    if it + 2 < NT:
                        seq2, tq2 = divmod(it + 2, 4)
                        dma(sy, xs[it % 2][:, :].rearrange("p (dc t) -> p dc t", dc=16),
                            xlnT[seq2, :, :, tq2 * 512:(tq2 + 1) * 512].transpose([1, 0, 2]), s_x[it % 2])
                for i in range(4):
                    sy.wait_ge(s_st[i].sem, s_st[i].val)
                sy.wait_ge(s_gs.sem, s_gs.val)

            @blk.tensor
            def _(t):
                t.wait_ge(s_w.sem, 16 * (OC + 2))
                for it in range(NT):
                    xv = xs[it % 2][:, :].rearrange("p (dc t) -> p dc t", dc=16)
                    t.wait_ge(s_x[it % 2].sem, 16 * (it // 2 + 1))
                    for oc in range(OC):
                        n = it * OC + oc
                        if n >= 2:
                            t.wait_ge(s_ev.sem, n - 1)
                        for dc in range(16):
                            mm = t.matmul(pq[n % 2][:, :], wq[:, (oc * 16 + dc) * 128:(oc * 16 + dc + 1) * 128],
                                          xv[:, dc, :], start=(dc == 0), stop=(dc == 15))
                        mm.then_inc(s_pe.sem, 1); bump(s_pe, 1)
                    if it >= 1:
                        t.wait_ge(s_sg.sem, it)
                    for dc in range(16):
                        mm = t.matmul(pg[0:96, :], wg[:, dc * 96:(dc + 1) * 96], xv[:, dc, :],
                                      start=(dc == 0), stop=(dc == 15))
                    mm.then_inc(s_pg.sem, 1); bump(s_pg, 1)
                    t.wait_ge(s_sg.sem, it + 1)
                    for tt in range(4):
                        n = 4 * it + tt
                        if n >= 1:
                            t.wait_ge(s_eg.sem, n)
                        t.transpose(pgt[:, 0:96], gst[0:96, tt * 128:(tt + 1) * 128],
                                    identF_sb[0:96, 0:96]).then_inc(s_tg.sem, 1); bump(s_tg, 1)

            @blk.vector
            def _(ve):
                for it in range(NT):
                    for oc in range(OC):
                        n = it * OC + oc
                        ve.wait_ge(s_pe.sem, n + 1)
                        if n >= 4:
                            ve.wait_ge(s_st[n % 4].sem, 16 * (n // 4))
                        ve.tensor_copy(stg[n % 4][:, :], pq[n % 2][:, :]).then_inc(s_ev.sem, 1); bump(s_ev, 1)
                    for tt in range(4):
                        n = 4 * it + tt
                        ve.wait_ge(s_tg.sem, n + 1)
                        if it >= 1 and tt == 0:
                            ve.wait_ge(s_gs.sem, 16 * it)
                        ve.tensor_copy(gtt[:, tt * 96:(tt + 1) * 96], pgt[:, 0:96]).then_inc(s_eg.sem, 1); bump(s_eg, 1)

            @blk.scalar
            def _(s):
                for it in range(NT):
                    s.wait_ge(s_pg.sem, it + 1)
                    if it >= 1:
                        s.wait_ge(s_tg.sem, 4 * it)
                    s.activation(gst[0:96, :], pg[0:96, :], AF.Sigmoid,
                                 bias=bg[0:96, 0:1]).then_inc(s_sg.sem, 1); bump(s_sg, 1)

        full_barrier()

    if "rope" in stages:
        with ExitStack() as sc:
            bf = [sc.enter_context(nc.sbuf_tensor(f"ro_x{i}", [128, S], F32)) for i in range(2)]
            t1 = sc.enter_context(nc.sbuf_tensor("ro_t1", [128, S], F32))
            t2 = sc.enter_context(nc.sbuf_tensor("ro_t2", [128, S], F32))
            ob = [sc.enter_context(nc.sbuf_tensor(f"ro_o{i}", [128, S], BF16)) for i in range(2)]
            kpb = sc.enter_context(nc.sbuf_tensor("ro_kp", [128, CBLK], F32))
            pr = [P0, P1]
            s_ld = [sem("ro_ld0"), sem("ro_ld1")]
            s_pe = sem("ro_pe")
            s_t2 = sem("ro_t2")
            s_dv = sem("ro_dv")
            s_st = [sem("ro_st0"), sem("ro_st1")]
            s_kp = sem("ro_kp")

            bands = []
            for seq in range(NB):
                for bd in range(16):
                    bands.append((0, seq, bd))
                for bd in range(4):
                    bands.append((1, seq, bd))
                for bd in range(4):
                    bands.append((2, seq, bd))
                for bd in range(4):
                    bands.append((3, seq, bd))

            def src_of(kind, seq, bd):
                return (qF if kind == 0 else kF if kind in (1, 2) else vF)[seq, bd, :, :]

            @blk.sync
            def _(sy):
                for i in range(min(2, len(bands))):
                    kind, seq, bd = bands[i]
                    dma(sy, bf[i % 2][:, :], src_of(kind, seq, bd), s_ld[i % 2])
                for i, (kind, seq, bd) in enumerate(bands):
                    sy.wait_ge(s_dv.sem, i + 1)
                    dst = (rqT, rkT, kbT, vbT)[kind]
                    dma(sy, dst[seq, bd, :, :], ob[i % 2][:, :], s_st[i % 2])
                    if i + 2 < len(bands):
                        kind2, seq2, bd2 = bands[i + 2]
                        dma(sy, bf[i % 2][:, :], src_of(kind2, seq2, bd2), s_ld[i % 2])
                sy.wait_ge(s_st[0].sem, s_st[0].val)
                sy.wait_ge(s_st[1].sem, s_st[1].val)

            @blk.gpsimd
            def _(g):
                kpi = 0
                for i, (kind, seq, bd) in enumerate(bands):
                    if kind in (2, 3):
                        if kpi >= 1:
                            g.wait_ge(s_dv.sem, prev_kp + 1)
                        src = kposT if kind == 2 else vposT
                        dma(g, kpb[0:64, :], src[2 * bd, :, :], s_kp)
                        dma(g, kpb[64:128, :], src[2 * bd + 1, :, :], s_kp)
                        kpi += 1
                        prev_kp = i

            @blk.tensor
            def _(t):
                nrot = 0
                for i, (kind, seq, bd) in enumerate(bands):
                    if kind in (0, 1):
                        t.wait_ge(s_ld[i % 2].sem, 16 * (i // 2 + 1))
                        for c in range(4):
                            if nrot >= 2:
                                t.wait_ge(s_t2.sem, nrot - 1)
                            t.matmul(pr[nrot % 2][:, :], cst["R2T"][:, :],
                                     bf[i % 2][:, c * 512:(c + 1) * 512], start=True,
                                     stop=True).then_inc(s_pe.sem, 1); bump(s_pe, 1)
                            nrot += 1

            @blk.vector
            def _(ve):
                nrot = 0
                nkp = 0
                for i, (kind, seq, bd) in enumerate(bands):
                    bi = i % 2
                    ve.wait_ge(s_ld[i % 2].sem, 16 * (i // 2 + 1))
                    if i >= 1:
                        ve.wait_ge(s_dv.sem, i)
                    if i >= 2:
                        ve.wait_ge(s_st[i % 2].sem, 16 * (i // 2))
                    if kind in (0, 1):
                        ve.tensor_mul(t1[:, :], bf[bi][:, :], cst["C2"][:, :])
                        for c in range(4):
                            ve.wait_ge(s_pe.sem, nrot + 1)
                            ve.tensor_mul(t2[:, c * 512:(c + 1) * 512], pr[nrot % 2][:, :],
                                          cst["S2"][:, c * 512:(c + 1) * 512]).then_inc(s_t2.sem, 1); bump(s_t2, 1)
                            nrot += 1
                        ve.wait_ge(s_t2.sem, nrot)
                        ve.tensor_add(ob[bi][:, :], t1[:, :], t2[:, :]).then_inc(s_dv.sem, 1); bump(s_dv, 1)
                    else:
                        nkp += 32
                        ve.wait_ge(s_kp.sem, nkp)
                        ve.tensor_add(ob[bi][:, :].rearrange("p (blk c) -> p blk c", c=CBLK),
                                      bf[bi][:, :].rearrange("p (blk c) -> p blk c", c=CBLK),
                                      kpb[:, :].unsqueeze(1).to_broadcast([128, NBLK, CBLK])
                                      ).then_inc(s_dv.sem, 1); bump(s_dv, 1)

        full_barrier()

    s_ck = sem("cm_ck")
    s_km = sem("cm_km")
    if "cmp" in stages:
        with ExitStack() as sc:
            cw1 = {}
            cw2 = {}
            cb1 = {}
            cb2 = {}
            for t_, w1, w2 in [("k", k_cw1, k_cw2), ("v", v_cw1, v_cw2)]:
                cw1[t_] = sc.enter_context(nc.sbuf_tensor(f"cm_w1{t_}", [64, 16 * 1024], BF16))
                cw2[t_] = sc.enter_context(nc.sbuf_tensor(f"cm_w2{t_}", [128, 8 * 64], BF16))
                cb1[t_] = sc.enter_context(nc.sbuf_tensor(f"cm_b1{t_}", [128, 8], F32))
                cb2[t_] = sc.enter_context(nc.sbuf_tensor(f"cm_b2{t_}", [64, 1], F32))
            kbt = sc.enter_context(nc.sbuf_tensor("cm_kb", [64, 4 * S], BF16))
            vbt = sc.enter_context(nc.sbuf_tensor("cm_vb", [64, 4 * S], BF16))
            h1 = sc.enter_context(nc.sbuf_tensor("cm_h1", [128, 8 * 512], BF16))
            ph = [P0, P1]
            pc = P2
            ptk = [PIM, PTR]
            s_w = sem("cm_w")
            s_ldg = sem("cm_ld")
            s_mm = sem("cm_mm")
            s_h1 = sem("cm_h1")
            s_pc = sem("cm_pc")
            s_tp = sem("cm_tp")
            NG = 4

            @blk.gpsimd
            def _(g):
                for t_, w1, w2, b1, b2 in [("k", k_cw1, k_cw2, k_cb1, k_cb2),
                                           ("v", v_cw1, v_cw2, v_cb1, v_cb2)]:
                    for kk in range(16):
                        dma(g, cw1[t_][:, kk * 1024:(kk + 1) * 1024],
                            w1[kk * 64:(kk + 1) * 64, :], s_w)
                    dma(g, cw2[t_][:, :].rearrange("p (mc m) -> p mc m", mc=8),
                        w2[:, :].rearrange("(mc p) m -> p mc m", p=128), s_w)
                    g.dma_start(cb1[t_][:, :].unsqueeze(2),
                                b1[:, :].rearrange("(mc p) o -> p mc o", p=128),
                                allow_slow_non_contiguous=True).then_inc(s_w.sem, 16); bump(s_w, 16)
                    dma(g, cb2[t_][:, :], b2[:, :], s_w)

            @blk.sync
            def _(sy):
                for g in range(NG):
                    if g >= 1:
                        sy.wait_ge(s_h1.sem, 16 * g)
                    b = (4 * g) // 8
                    band0 = ((4 * g) % 8) // 2
                    for j4 in range(4):
                        band = band0 + j4 // 2
                        r0 = (j4 % 2) * 64
                        dma(sy, kbt[:, j4 * S:(j4 + 1) * S], kbT[b, band, r0:r0 + 64, :], s_ldg)
                        dma(sy, vbt[:, j4 * S:(j4 + 1) * S], vbT[b, band, r0:r0 + 64, :], s_ldg)

            def kb_mov(t_, kk):
                tile = kbt if t_ == "k" else vbt
                return tile[:, :].rearrange("p (j4 blk c) -> p c j4 blk", c=CBLK, j4=4)[:, kk, :, :]

            @blk.tensor
            def _(t):
                t.wait_ge(s_w.sem, s_w.val)
                for g in range(NG):
                    t.wait_ge(s_ldg.sem, 128 * (g + 1))
                    for ti, t_ in enumerate(("k", "v")):
                        for mc in range(8):
                            n = 16 * g + 8 * ti + mc
                            if n >= 2:
                                t.wait_ge(s_h1.sem, n - 1)
                            for kk in range(16):
                                w1ap = cw1[t_][:, kk * 1024 + mc * 128:kk * 1024 + (mc + 1) * 128]
                                mm = t.matmul(ph[n % 2][:, :], w1ap, kb_mov(t_, kk),
                                              start=(kk == 0), stop=(kk == 15))
                            mm.then_inc(s_mm.sem, 1); bump(s_mm, 1)
                        t.wait_ge(s_h1.sem, 16 * g + 8 * ti + 8)
                        for mc in range(8):
                            mm = t.matmul(pc[0:64, :], cw2[t_][:, mc * 64:(mc + 1) * 64],
                                          h1[:, mc * 512:(mc + 1) * 512],
                                          start=(mc == 0), stop=(mc == 7))
                        mm.then_inc(s_pc.sem, 1); bump(s_pc, 1)
                        if not (g == NG - 1 and ti == 1):
                            t.wait_ge(s_ck.sem, 2 * g + ti + 1)
                    t.wait_ge(s_ck.sem, 2 * g + 2)
                    for j4 in range(4):
                        n = 4 * g + j4
                        if n >= 2:
                            t.wait_ge(s_km.sem, n - 1)
                        bh = 4 * g + j4
                        t.transpose(ptk[n % 2][:, 0:64], cvT[:, bh * 128:(bh + 1) * 128],
                                    identF_sb[0:64, 0:64]).then_inc(s_tp.sem, 1); bump(s_tp, 1)

            @blk.scalar
            def _(s):
                for g in range(NG):
                    for ti, t_ in enumerate(("k", "v")):
                        for mc in range(8):
                            n = 16 * g + 8 * ti + mc
                            s.wait_ge(s_mm.sem, n + 1)
                            s.activation(h1[:, mc * 512:(mc + 1) * 512], ph[n % 2][:, :],
                                         AF.Relu, bias=cb1[t_][:, mc:mc + 1]).then_inc(s_h1.sem, 1); bump(s_h1, 1)
                        s.wait_ge(s_pc.sem, 2 * g + ti + 1)
                        dstT = ckT if t_ == "k" else cvT
                        s.activation(dstT[:, g * 512:(g + 1) * 512], pc[0:64, :], AF.Identity,
                                     bias=cb2[t_][:, 0:1]).then_inc(s_ck.sem, 1); bump(s_ck, 1)

            @blk.vector
            def _(ve):
                ve.memset(cvKM[:, :].rearrange("p (bh o) -> p bh o", bh=NB * KVH)[:, :, 64:65], 1.0)
                for g in range(NG):
                    for j4 in range(4):
                        n = 4 * g + j4
                        ve.wait_ge(s_tp.sem, n + 1)
                        bh = 4 * g + j4
                        ve.tensor_copy(cvKM[:, bh * 65:bh * 65 + 64], ptk[n % 2][:, 0:64]
                                       ).then_inc(s_km.sem, 1); bump(s_km, 1)

            if "ckD" in dbg or "cvD" in dbg:
                @blk.gpsimd
                def _(g):
                    g.wait_ge(s_ck.sem, 8)
                    dma(g, ckD[:, :], ckT[:, :], s_w)
                    dma(g, cvD[:, :], cvT[:, :], s_w)
                    g.wait_ge(s_w.sem, s_w.val)

        full_barrier()

    if "attn" in stages:
        with ExitStack() as sc:
            rq4 = [sc.enter_context(nc.sbuf_tensor(f"at_rq{i}", [64, 4 * S], BF16)) for i in range(2)]
            q4 = [sc.enter_context(nc.sbuf_tensor(f"at_q{i}", [64, 4 * S], BF16)) for i in range(2)]
            rkh = [sc.enter_context(nc.sbuf_tensor(f"at_rk{i}", [64, S], BF16)) for i in range(2)]
            vth = [sc.enter_context(nc.sbuf_tensor("at_vt0", [64, S], F32))] * 2
            vKM = sc.enter_context(nc.sbuf_tensor("at_vkm", [128, 16 * 65], BF16))
            vKM2 = sc.enter_context(nc.sbuf_tensor("at_vkm2", [64, 32 * 65], BF16))
            impsum = sc.enter_context(nc.sbuf_tensor("at_imps", [128, S], F32))
            esh = impsum
            impq = sc.enter_context(nc.sbuf_tensor("at_impq", [128, S], F32))
            top8 = sc.enter_context(nc.sbuf_tensor("at_top8", [128, 16 * 8], F32))
            zr = sc.enter_context(nc.sbuf_tensor("at_zr", [128, 6 * 16], F32))
            BMq = sc.enter_context(nc.sbuf_tensor("at_bmq", [128, S], F32))
            BMT1 = sc.enter_context(nc.sbuf_tensor("at_bmt1", [128, S], BF16))
            pf = sc.enter_context(nc.sbuf_tensor("at_pf", [128, 8 * 512], BF16))
            pc_sb = sc.enter_context(nc.sbuf_tensor("at_pc", [128, 512], BF16))
            pm_sb = sc.enter_context(nc.sbuf_tensor("at_pm", [1, 512], BF16))
            psl = sc.enter_context(nc.sbuf_tensor("at_psl", [64, 512], BF16))
            pso = sc.enter_context(nc.sbuf_tensor("at_pso", [64, 512], BF16))
            cnum = sc.enter_context(nc.sbuf_tensor("at_cn", [65, 512], F32))
            fnum = sc.enter_context(nc.sbuf_tensor("at_fn", [65, 512], F32))
            snum = sc.enter_context(nc.sbuf_tensor("at_sn", [65, 512], F32))
            zrec = sc.enter_context(nc.sbuf_tensor("at_zrec", [128, 12], F32))
            mbr = sc.enter_context(nc.sbuf_tensor("at_mbr", [128, 12], F32))
            acc = sc.enter_context(nc.sbuf_tensor("at_acc", [128, 256], F32))
            tmpc = sc.enter_context(nc.sbuf_tensor("at_tmpc", [128, 256], F32))
            gtile = sc.enter_context(nc.sbuf_tensor("at_gt", [128, 16 * 96], F32))
            asm = [sc.enter_context(nc.sbuf_tensor(f"at_asm{i}", [64, S], BF16)) for i in range(4)]
            pcs = [P0, P1]
            pfs = [P2, P3]
            pbr = P4
            ptr = PTR
            pim = PIM
            pmem = PMEM

            names = ["a_ld0", "a_ld1", "a_lq0", "a_lq1", "a_gt", "a_tpv", "a_evv", "a_cs",
                     "a_imp", "a_msk", "a_imtp", "a_imev", "a_bm", "a_bmtp", "a_bmev", "a_e",
                     "a_c1", "a_cm", "a_xc", "a_xm", "a_co", "a_bev", "a_fs", "a_xf", "a_fv",
                     "a_fo", "a_ss", "a_xs", "a_so", "a_tr", "a_cb", "a_tb", "a_te", "a_st",
                     "a_ms", "a_lsh", "a_mpr", "a_es", "a_es2", "a_zv", "a_es3", "a_bd"]
            sm = {n: sem(n) for n in names}
            NBH = NB * KVH

            def bh_of(i):
                return i // 8, i % 8

            def ldw(bh):
                return 16 * 6 * (bh // 2 + 1)

            def ss_base(c):
                full, j = divmod(c, 16)
                tot = full * 252
                for jj in range(j):
                    tot += 15 if jj // 4 == 0 else 16
                return tot

            def ss_nprev(c):
                return 7 if (c % 16) // 4 == 0 else 8

            @blk.sync
            def _(sy):
                for bh in range(NBH):
                    b, h = bh_of(bh)
                    p = bh % 2
                    if bh >= 2:
                        sy.wait_ge(sm["a_te"].sem, 16 * (bh - 1))
                    for g in range(G):
                        band = 2 * h + g // 2
                        r0 = (g % 2) * 64
                        dma(sy, rq4[p][:, g * S:(g + 1) * S], rqT[b, band, r0:r0 + 64, :], sm[f"a_ld{p}"])
                    dma(sy, rkh[p][:, :], rkT[b, h // 2, (h % 2) * 64:(h % 2) * 64 + 64, :], sm[f"a_ld{p}"])
                    if bh >= 1:
                        sy.wait_ge(sm["a_tpv"].sem, 48 * bh)
                    dma(sy, vth[p][:, :], vF[b, h // 2, (h % 2) * 64:(h % 2) * 64 + 64, :], sm[f"a_ld{p}"])
                    if bh >= 2:
                        bp, hp = bh_of(bh - 2)
                        sy.wait_ge(sm["a_te"].sem, 16 * (bh - 1))
                        for g in range(G):
                            dma(sy, attnFM[bp, 4 * hp + g, :, :], asm[g][:, :], sm["a_st"])
                    if h == 0:
                        if bh >= 2:
                            sy.wait_ge(sm["a_cb"].sem, 32 * bh)
                        dma(sy, gtile[:, :].rearrange("p (qt g) -> p qt g", qt=16),
                            gatesT[b, :, :, :].transpose([1, 0, 2]), sm["a_gt"])
                for bh in (NBH - 2, NBH - 1):
                    bp, hp = bh_of(bh)
                    sy.wait_ge(sm["a_te"].sem, 16 * (bh + 1))
                    for g in range(G):
                        dma(sy, attnFM[bp, 4 * hp + g, :, :], asm[g][:, :], sm["a_st"])
                sy.wait_ge(sm["a_st"].sem, sm["a_st"].val)

            @blk.gpsimd
            def _(g_):
                for bh in range(NBH):
                    b, h = bh_of(bh)
                    p = bh % 2
                    if bh >= 2:
                        g_.wait_ge(sm["a_cs"].sem, 16 * (bh - 1))
                        g_.wait_ge(sm["a_c1"].sem, 16 * (bh - 1))
                        g_.wait_ge(sm["a_cm"].sem, 16 * (bh - 1))
                    for g in range(G):
                        band = 2 * h + g // 2
                        r0 = (g % 2) * 64
                        dma(g_, q4[p][:, g * S:(g + 1) * S], qF[b, band, r0:r0 + 64, :], sm[f"a_lq{p}"])
                    if "bmD" in dbg:
                        g_.wait_ge(sm["a_bm"].sem, bh + 1)
                        dma(g_, bmD[b, h, :, :, :].transpose([1, 0, 2]),
                            BMq[:, :].rearrange("p (qt blk) -> p qt blk", qt=16), sm["a_bd"])

            @blk.tensor
            def _(t):
                def flush_tr_r(cp, r):
                    if r == 0:
                        t.wait_ge(sm["a_bev"].sem, 3 * (cp + 1))
                    t.wait_ge(sm["a_cb"].sem, 2 * cp + r)
                    for q2 in range(2):
                        qt2 = 2 * r + q2
                        for br, buf in enumerate((cnum, fnum, snum)):
                            t.transpose(ptr[:, (q2 * 3 + br) * 65:(q2 * 3 + br + 1) * 65],
                                        buf[0:65, qt2 * 128:(qt2 + 1) * 128],
                                        identF_sb[0:65, 0:65]).then_inc(sm["a_tr"].sem, 1); bump(sm["a_tr"], 1)

                def flush_tr(cp):
                    flush_tr_r(cp, 0)
                    flush_tr_r(cp, 1)

                def flush_tb(cp):
                    t.wait_ge(sm["a_cb"].sem, 2 * (cp + 1))
                    t.wait_ge(sm["a_xc"].sem, cp + 1)
                    for qt2 in range(4):
                        t.transpose(pcs[cp % 2][0:64, qt2 * 128:(qt2 + 1) * 128],
                                    acc[:, qt2 * 64:(qt2 + 1) * 64],
                                    identF_sb[:, :]).then_inc(sm["a_tb"].sem, 1); bump(sm["a_tb"], 1)

                def emit_T3(bh):
                    b, h = bh_of(bh)
                    p = bh % 2
                    t.wait_ge(sm["a_evv"].sem, 48 * (bh + 1))
                    t.wait_ge(sm["a_imp"].sem, min(16 * (bh + 2), 16 * NBH))
                    t.wait_ge(sm["a_bmev"].sem, 16 * (bh + 1))
                    for j in range(16):
                        qc, g = divmod(j, 4)
                        c = 16 * bh + j
                        if j > 0:
                            flush_tr_r(c - 1, 0)
                        qsl = q4[p][:, g * S + qc * 512:g * S + (qc + 1) * 512]
                        rqsl = rq4[p][:, g * S + qc * 512:g * S + (qc + 1) * 512]
                        if j >= 2:
                            t.wait_ge(sm["a_xc"].sem, c - 1)
                            t.wait_ge(sm["a_te"].sem, c - 1)
                        t.matmul(pcs[c % 2][:, :], ckT[:, bh * 128:(bh + 1) * 128], qsl,
                                 start=True, stop=False)
                        t.matmul(pcs[c % 2][:, :], identB[:, :],
                                 cst["cmaskADD"][:, qc * 512:(qc + 1) * 512],
                                 start=False, stop=True).then_inc(sm["a_c1"].sem, 1); bump(sm["a_c1"], 1)
                        if c >= 1:
                            t.wait_ge(sm["a_xm"].sem, c)
                        t.matmul(pmem[0:1, :], memck_sb[:, h:h + 1], qsl,
                                 start=True, stop=True).then_inc(sm["a_cm"].sem, 1); bump(sm["a_cm"], 1)
                        for kc in range(16):
                            n = 16 * c + kc
                            if n >= 2:
                                t.wait_ge(sm["a_xf"].sem, n - 1)
                            if kc < 2 and c >= 1:
                                t.wait_ge(sm["a_so"].sem, 8 * c)
                            dg = kc - 4 * qc
                            diag = dg if 0 <= dg < 4 else None
                            t.matmul(pfs[kc % 2][:, :], rkh[p][:, kc * 128:(kc + 1) * 128],
                                     rqsl, start=True, stop=False)
                            mmE = t.matmul(pfs[kc % 2][:, :], cst["EBIG"][:, kc * 128:(kc + 1) * 128],
                                           BMT1[:, qc * 512:(qc + 1) * 512],
                                           start=False, stop=(diag is None))
                            if diag is not None:
                                mmE = t.matmul(pfs[kc % 2][:, diag * 128:(diag + 1) * 128],
                                               identB[:, :], cst["ownADD"][:, :],
                                               start=False, stop=True)
                            mmE.then_inc(sm["a_fs"].sem, 1); bump(sm["a_fs"], 1)
                            if kc == 1:
                                if j > 0:
                                    flush_tr_r(c - 1, 1)
                                if c >= 1:
                                    t.wait_ge(sm["a_bev"].sem, 3 * c)
                                t.wait_ge(sm["a_xc"].sem, c + 1)
                                t.matmul(pbr[0:65, :], cvKM[:, bh * 65:(bh + 1) * 65], pc_sb[:, :],
                                         start=True, stop=False)
                                t.wait_ge(sm["a_xm"].sem, c + 1)
                                t.matmul(pbr[0:65, :], memcv_st[0:1, h * 65:(h + 1) * 65],
                                         pm_sb[0:1, :],
                                         start=False, stop=True).then_inc(sm["a_co"].sem, 1); bump(sm["a_co"], 1)
                            if kc >= 2:
                                nf = kc - 2
                                if nf == 0:
                                    t.wait_ge(sm["a_bev"].sem, 3 * c + 1)
                                t.wait_ge(sm["a_xf"].sem, 16 * c + nf + 1)
                                t.matmul(pbr[0:65, :], vKM[:, nf * 65:(nf + 1) * 65],
                                         pf[:, (nf % 8) * 512:(nf % 8 + 1) * 512],
                                         start=(nf == 0), stop=False).then_inc(sm["a_fv"].sem, 1); bump(sm["a_fv"], 1)
                        for nf in (14, 15):
                            t.wait_ge(sm["a_xf"].sem, 16 * c + nf + 1)
                            t.matmul(pbr[0:65, :], vKM[:, nf * 65:(nf + 1) * 65],
                                     pf[:, (nf % 8) * 512:(nf % 8 + 1) * 512],
                                     start=False, stop=(nf == 15)).then_inc(sm["a_fv"].sem, 1); bump(sm["a_fv"], 1)
                        t.nop().then_inc(sm["a_fo"].sem, 1); bump(sm["a_fo"], 1)
                        t.wait_ge(sm["a_xf"].sem, 16 * (c + 1))
                        for w8 in range(8):
                            w = qc * 8 + w8
                            if w == 0:
                                continue
                            t.matmul(pfs[0][0:64, w8 * 64:(w8 + 1) * 64],
                                     rkh[p][:, (w - 1) * 64:w * 64],
                                     rq4[p][:, g * S + w * 64:g * S + (w + 1) * 64],
                                     start=True, stop=False)
                            t.matmul(pfs[0][0:64, w8 * 64:(w8 + 1) * 64],
                                     identB[0:64, 0:64], cst["sADDp"][:, :],
                                     start=False, stop=True).then_inc(sm["a_ss"].sem, 1); bump(sm["a_ss"], 1)
                        for w8 in range(8):
                            w = qc * 8 + w8
                            t.matmul(pfs[1][0:64, w8 * 64:(w8 + 1) * 64],
                                     rkh[p][:, w * 64:(w + 1) * 64],
                                     rq4[p][:, g * S + w * 64:g * S + (w + 1) * 64],
                                     start=True, stop=False)
                            t.matmul(pfs[1][0:64, w8 * 64:(w8 + 1) * 64],
                                     identB[0:64, 0:64], cst["sADDo"][:, :],
                                     start=False, stop=True).then_inc(sm["a_ss"].sem, 1); bump(sm["a_ss"], 1)
                        t.wait_ge(sm["a_bev"].sem, 3 * c + 2)
                        t.wait_ge(sm["a_xs"].sem, 2 * (c + 1))
                        for w8 in range(8):
                            w = qc * 8 + w8
                            if w > 0:
                                t.matmul(pbr[0:65, w8 * 64:(w8 + 1) * 64],
                                         vKM2[:, (w - 1) * 65:(w - 1) * 65 + 65],
                                         psl[:, w8 * 64:(w8 + 1) * 64], start=True, stop=False)
                            t.matmul(pbr[0:65, w8 * 64:(w8 + 1) * 64],
                                     vKM2[:, w * 65:w * 65 + 65],
                                     pso[:, w8 * 64:(w8 + 1) * 64],
                                     start=(w == 0), stop=True).then_inc(sm["a_so"].sem, 1); bump(sm["a_so"], 1)
                        if c >= 1 and j > 0:
                            flush_tb(c - 1)

                t.wait_ge(s_ck.sem, s_ck.val)
                t.wait_ge(s_km.sem, s_km.val)
                t.wait_ge(sm["a_ms"].sem, 2)
                for bh in range(NBH + 1):
                    b, h = bh_of(bh) if bh < NBH else (0, 0)
                    p = bh % 2
                    if bh < NBH:
                        t.wait_ge(sm[f"a_lq{p}"].sem, 64 * (bh // 2 + 1))
                        pA = [pcs[0], pcs[1], pfs[0], pfs[1]]
                        for j in range(16):
                            qc, g = divmod(j, 4)
                            n = 16 * bh + j
                            if n >= 4:
                                t.wait_ge(sm["a_imp"].sem, n - 3)
                            if bh >= 2 and j == 0:
                                t.wait_ge(sm["a_xc"].sem, 16 * (bh - 1))
                                t.wait_ge(sm["a_te"].sem, 16 * (bh - 1))
                                t.wait_ge(sm["a_xf"].sem, 256 * (bh - 1))
                                t.wait_ge(sm["a_xs"].sem, 32 * (bh - 1))
                            t.matmul(pA[j % 4][:, :], ckT[:, bh * 128:(bh + 1) * 128],
                                     q4[p][:, g * S + qc * 512:g * S + (qc + 1) * 512],
                                     start=True, stop=False)
                            t.matmul(pA[j % 4][:, :], identB[:, :],
                                     cst["cmaskADD"][:, qc * 512:(qc + 1) * 512],
                                     start=False, stop=True).then_inc(sm["a_cs"].sem, 1); bump(sm["a_cs"], 1)
                    if bh >= 1:
                        emit_T3(bh - 1)
                        flush_tr(16 * bh - 1)
                        flush_tb(16 * bh - 1)
                    if bh < NBH:
                        t.wait_ge(sm[f"a_ld{p}"].sem, ldw(bh))
                        if bh >= 1:
                            t.wait_ge(sm["a_cb"].sem, 32 * bh)
                            t.wait_ge(sm["a_bmev"].sem, 16 * bh)
                        for n in range(48):
                            dst = pim if n % 2 == 0 else ptr
                            if n >= 2:
                                t.wait_ge(sm["a_evv"].sem, 48 * bh + n - 1)
                            if n < 16:
                                t.transpose(dst[:, 0:64], vth[p][:, n * 128:(n + 1) * 128],
                                            identF_sb[0:64, 0:64]).then_inc(sm["a_tpv"].sem, 1); bump(sm["a_tpv"], 1)
                            else:
                                w = n - 16
                                t.transpose(dst[0:64, 0:64], vth[p][:, w * 64:(w + 1) * 64],
                                            identF_sb[0:64, 0:64]).then_inc(sm["a_tpv"].sem, 1); bump(sm["a_tpv"], 1)
                        t.wait_ge(sm["a_msk"].sem, bh + 1)
                        t.wait_ge(sm["a_evv"].sem, 48 * (bh + 1))
                        for qt in range(16):
                            n = 16 * bh + qt
                            dst = pim[:, 0:128] if qt % 2 == 0 else ptr[:, 0:128]
                            if qt > 1:
                                t.wait_ge(sm["a_imev"].sem, n - 1)
                            t.transpose(dst, impsum[:, qt * 128:(qt + 1) * 128],
                                        identF_sb[:, :]).then_inc(sm["a_imtp"].sem, 1); bump(sm["a_imtp"], 1)
                        t.wait_ge(sm["a_bm"].sem, bh + 1)
                        t.wait_ge(sm["a_imev"].sem, 16 * (bh + 1))
                        for qt in range(16):
                            n = 16 * bh + qt
                            dst = pim[:, 0:128] if qt % 2 == 0 else ptr[:, 0:128]
                            if qt > 1:
                                t.wait_ge(sm["a_bmev"].sem, n - 1)
                            t.transpose(dst, BMq[:, qt * 128:(qt + 1) * 128],
                                        identF_sb[:, :]).then_inc(sm["a_bmtp"].sem, 1); bump(sm["a_bmtp"], 1)

            @blk.scalar
            def _(s):
                zrv = zr[:, :].rearrange("p (k qt) -> p k qt", k=6)
                for bh in range(NBH + 1):
                    if bh >= 1:
                        bhp = bh - 1
                        hh = bhp % 8
                        for j in range(16):
                            qc, g = divmod(j, 4)
                            c = 16 * bhp + j
                            s.wait_ge(sm["a_c1"].sem, c + 1)
                            if c >= 1:
                                s.wait_ge(sm["a_co"].sem, c)
                            s.activation(pc_sb[:, :], pcs[c % 2][:, :], AF.Exp,
                                         scale=SCALE).then_inc(sm["a_xc"].sem, 1); bump(sm["a_xc"], 1)
                            s.wait_ge(sm["a_cm"].sem, c + 1)
                            s.activation(pm_sb[0:1, :], pmem[0:1, :], AF.Exp,
                                         scale=SCALE).then_inc(sm["a_xm"].sem, 1); bump(sm["a_xm"], 1)
                            for kc in range(16):
                                n = 16 * c + kc
                                s.wait_ge(sm["a_fs"].sem, n + 1)
                                if c >= 1 or kc >= 8:
                                    s.wait_ge(sm["a_fv"].sem, 16 * c + kc - 7)
                                s.activation(pf[:, (kc % 8) * 512:(kc % 8 + 1) * 512],
                                             pfs[kc % 2][:, :],
                                             AF.Exp, scale=SCALE).then_inc(sm["a_xf"].sem, 1); bump(sm["a_xf"], 1)
                            s.wait_ge(sm["a_ss"].sem, ss_base(c) + ss_nprev(c))
                            if c >= 1:
                                s.wait_ge(sm["a_so"].sem, 8 * c)
                            if qc == 0:
                                s.activation(psl[:, 64:512], pfs[0][0:64, 64:512], AF.Exp,
                                             scale=SCALE).then_inc(sm["a_xs"].sem, 1); bump(sm["a_xs"], 1)
                            else:
                                s.activation(psl[:, :], pfs[0][0:64, :], AF.Exp,
                                             scale=SCALE).then_inc(sm["a_xs"].sem, 1); bump(sm["a_xs"], 1)
                            s.wait_ge(sm["a_ss"].sem, ss_base(c) + ss_nprev(c) + 8)
                            s.activation(pso[:, :], pfs[1][0:64, :], AF.Exp,
                                         scale=SCALE).then_inc(sm["a_xs"].sem, 1); bump(sm["a_xs"], 1)
                    if bh < NBH:
                        s.wait_ge(sm["a_lsh"].sem, bh + 1)
                        s.activation(esh[:, :], esh[:, :], AF.Exp).then_inc(sm["a_es"].sem, 1); bump(sm["a_es"], 1)
                        s.wait_ge(sm["a_mpr"].sem, bh + 1)
                        s.activation(zrv[:, 2, :], zrv[:, 1, :], AF.Exp,
                                     bias=smallc[:, 1:2]).then_inc(sm["a_es2"].sem, 1); bump(sm["a_es2"], 1)
                        s.wait_ge(sm["a_zv"].sem, bh + 1)
                        s.activation(zrv[:, 4, :], zrv[:, 3, :], AF.Ln).then_inc(sm["a_es3"].sem, 1); bump(sm["a_es3"], 1)

            @blk.vector
            def _(ve):
                ve.memset(vKM[:, :].rearrange("p (kc o) -> p kc o", kc=16)[:, :, 64:65],
                          1.0).then_inc(sm["a_ms"].sem, 1); bump(sm["a_ms"], 1)
                ve.memset(vKM2[:, :].rearrange("p (w o) -> p w o", w=32)[:, :, 64:65],
                          1.0).then_inc(sm["a_ms"].sem, 1); bump(sm["a_ms"], 1)
                ec = [0]

                def edge(ins):
                    ins.then_inc(sm["a_e"].sem, 1)
                    ec[0] += 1
                    ve.wait_ge(sm["a_e"].sem, ec[0])
                    return ins

                def tback_evac(cp):
                    bhp, jp = divmod(cp, 16)
                    qcp, gp = divmod(jp, 4)
                    ve.wait_ge(sm["a_tb"].sem, 4 * (cp + 1))
                    if jp == 0 and bhp >= 1:
                        ve.wait_ge(sm["a_st"].sem, 64 * bhp)
                    ve.tensor_copy(asm[gp][:, qcp * 512:(qcp + 1) * 512],
                                   pcs[cp % 2][0:64, :]).then_inc(sm["a_te"].sem, 1); bump(sm["a_te"], 1)

                for bh in range(NBH + 1):
                    b, h = bh_of(bh) if bh < NBH else (0, 0)
                    if bh >= 2:
                        tback_evac(16 * (bh - 1) - 1)
                    if bh < NBH:
                        for j in range(16):
                            qc, g = divmod(j, 4)
                            n = 16 * bh + j
                            if j == 0 and bh >= 1:
                                ve.wait_ge(sm["a_imtp"].sem, 16 * bh)
                                ve.wait_ge(sm["a_es"].sem, bh)
                            ve.wait_ge(sm["a_cs"].sem, n + 1)
                            if g >= 1:
                                ve.wait_ge(sm["a_imp"].sem, n)
                            sl = impsum[:, qc * 512:(qc + 1) * 512]
                            pAv = [pcs[0], pcs[1], pfs[0], pfs[1]][j % 4]
                            if g == 0:
                                ve.tensor_copy(sl, pAv[:, :]).then_inc(sm["a_imp"].sem, 1); bump(sm["a_imp"], 1)
                            else:
                                ve.tensor_add(sl, sl, pAv[:, :]).then_inc(sm["a_imp"].sem, 1); bump(sm["a_imp"], 1)
                        ve.wait_ge(sm["a_imp"].sem, 16 * (bh + 1))
                        edge(ve.scalar_tensor_tensor(impsum[:, :], impsum[:, :], SCALE / G,
                                                     cst["cmaskNEG"][:, :], ALU.mult, ALU.add))
                        ve.sem_inc(sm["a_msk"].sem, 1); bump(sm["a_msk"], 1)
                    if bh >= 1:
                        bhp = bh - 1
                        hh = bhp % 8
                        ve.wait_ge(sm["a_gt"].sem, 16 * (bhp // 8 + 1))
                        for j in range(16):
                            qc, g = divmod(j, 4)
                            c = 16 * bhp + j
                            if c >= 1:
                                ve.wait_ge(sm["a_tr"].sem, 12 * c)
                            ve.wait_ge(sm["a_co"].sem, c + 1)
                            ve.tensor_copy(cnum[0:65, :], pbr[0:65, :]).then_inc(sm["a_bev"].sem, 1); bump(sm["a_bev"], 1)
                            ve.wait_ge(sm["a_fo"].sem, c + 1)
                            ve.wait_ge(sm["a_fv"].sem, 16 * (c + 1))
                            ve.tensor_copy(fnum[0:65, :], pbr[0:65, :]).then_inc(sm["a_bev"].sem, 1); bump(sm["a_bev"], 1)
                            ve.wait_ge(sm["a_so"].sem, 8 * (c + 1))
                            ve.tensor_copy(snum[0:65, :], pbr[0:65, :]).then_inc(sm["a_bev"].sem, 1); bump(sm["a_bev"], 1)
                            if j > 0:
                                tback_evac(c - 1)
                            ptrv = ptr[:, 0:390].rearrange("p (qt br o) -> p qt br o", qt=2, br=3)
                            gv = gtile[:, :].rearrange("p (qt gg) -> p qt gg", qt=16)
                            mv = mbr[:, 0:6].rearrange("p (qt br) -> p qt br", qt=2)
                            for r in range(2):
                                ve.wait_ge(sm["a_tr"].sem, 12 * c + 6 * (r + 1))
                                if r == 0 and c >= 1:
                                    ve.wait_ge(sm["a_tb"].sem, 4 * c)
                                edge(ve.reciprocal(zrec[:, 0:6].rearrange("p (qt br) -> p qt br", qt=2),
                                                   ptrv[:, :, :, 64]))
                                edge(ve.tensor_mul(mv,
                                                   gv[:, 4 * qc + 2 * r:4 * qc + 2 * r + 2,
                                                      12 * hh + 3 * g:12 * hh + 3 * g + 3],
                                                   zrec[:, 0:6].rearrange("p (qt br) -> p qt br", qt=2)))
                                av = acc[:, 128 * r:128 * (r + 1)].rearrange("p (qt d) -> p qt d", qt=2)
                                tv = tmpc[:, 0:128].rearrange("p (qt d) -> p qt d", qt=2)
                                edge(ve.tensor_tensor(av, ptrv[:, :, 0, 0:64],
                                                      mv[:, :, 0].unsqueeze(2).to_broadcast([128, 2, 64]),
                                                      ALU.mult))
                                edge(ve.tensor_tensor(tv, ptrv[:, :, 1, 0:64],
                                                      mv[:, :, 1].unsqueeze(2).to_broadcast([128, 2, 64]),
                                                      ALU.mult))
                                edge(ve.tensor_add(av, av, tv))
                                edge(ve.tensor_tensor(tv, ptrv[:, :, 2, 0:64],
                                                      mv[:, :, 2].unsqueeze(2).to_broadcast([128, 2, 64]),
                                                      ALU.mult))
                                ve.tensor_add(av, av, tv).then_inc(sm["a_cb"].sem, 1); bump(sm["a_cb"], 1)
                    if bh < NBH:
                        for n in range(48):
                            src = pim if n % 2 == 0 else ptr
                            ve.wait_ge(sm["a_tpv"].sem, 48 * bh + n + 1)
                            if n < 16:
                                ve.tensor_copy(vKM[:, n * 65:n * 65 + 64],
                                               src[:, 0:64]).then_inc(sm["a_evv"].sem, 1); bump(sm["a_evv"], 1)
                            else:
                                w = n - 16
                                ve.tensor_copy(vKM2[:, w * 65:w * 65 + 64],
                                               src[0:64, 0:64]).then_inc(sm["a_evv"].sem, 1); bump(sm["a_evv"], 1)
                        for qt in range(16):
                            n = 16 * bh + qt
                            srcp = pim[:, 0:128] if qt % 2 == 0 else ptr[:, 0:128]
                            ve.wait_ge(sm["a_imtp"].sem, n + 1)
                            ve.tensor_copy(impq[:, qt * 128:(qt + 1) * 128],
                                           srcp).then_inc(sm["a_imev"].sem, 1); bump(sm["a_imev"], 1)
                        ve.wait_ge(sm["a_imev"].sem, 16 * (bh + 1))
                        for qt in range(16):
                            edge(ve.max(top8[:, qt * 8:(qt + 1) * 8], impq[:, qt * 128:(qt + 1) * 128]))
                        t8v = top8[:, :].rearrange("p (qt e) -> p qt e", qt=16)
                        zrv = zr[:, :].rearrange("p (k qt) -> p k qt", k=6)
                        iv = impq[:, :].rearrange("p (qt q) -> p qt q", qt=16)
                        ev = esh[:, :].rearrange("p (qt q) -> p qt q", qt=16)
                        edge(ve.tensor_scalar(zrv[:, 0, :], t8v[:, :, 0], -1000.0, None, ALU.max))
                        ve.tensor_scalar(zrv[:, 1, :], zrv[:, 0, :], -1.0, None,
                                         ALU.mult).then_inc(sm["a_mpr"].sem, 1); bump(sm["a_mpr"], 1)
                        ve.wait_ge(sm["a_mpr"].sem, bh + 1)
                        ve.tensor_add(ev, iv, zrv[:, 1, :].unsqueeze(2).to_broadcast([128, 16, 128])
                                      ).then_inc(sm["a_lsh"].sem, 1); bump(sm["a_lsh"], 1)
                        ve.wait_ge(sm["a_es"].sem, bh + 1)
                        edge(ve.tensor_reduce(zrv[:, 3, :].unsqueeze(2), ev, AX.X, ALU.add))
                        ve.wait_ge(sm["a_es2"].sem, bh + 1)
                        ve.tensor_add(zrv[:, 3, :], zrv[:, 3, :], zrv[:, 2, :]).then_inc(sm["a_zv"].sem, 1); bump(sm["a_zv"], 1)
                        ve.wait_ge(sm["a_es3"].sem, bh + 1)
                        edge(ve.tensor_scalar(zrv[:, 5, :], zrv[:, 4, :], float(np.log(1e-10)),
                                              None, ALU.add))
                        edge(ve.tensor_add(zrv[:, 5, :], zrv[:, 5, :], zrv[:, 0, :]))
                        if "bmD" in dbg and bh >= 1:
                            ve.wait_ge(sm["a_bd"].sem, 16 * bh)
                        bv = BMq[:, :].rearrange("p (qt q) -> p qt q", qt=16)
                        sv = ev
                        edge(ve.tensor_tensor(bv, iv, t8v[:, :, 7].unsqueeze(2).to_broadcast(
                            [128, 16, 128]), ALU.is_ge))
                        edge(ve.tensor_tensor(sv, iv, zrv[:, 5, :].unsqueeze(2).to_broadcast(
                            [128, 16, 128]), ALU.is_gt))
                        ve.tensor_mul(BMq[:, :], BMq[:, :], esh[:, :]).then_inc(sm["a_bm"].sem, 1); bump(sm["a_bm"], 1)
                        for qt in range(16):
                            n = 16 * bh + qt
                            srcp = pim[:, 0:128] if qt % 2 == 0 else ptr[:, 0:128]
                            ve.wait_ge(sm["a_bmtp"].sem, n + 1)
                            if bh >= 1 and qt == 0:
                                ve.wait_ge(sm["a_fs"].sem, 256 * bh)
                            ve.tensor_scalar(BMT1[:, qt * 128:(qt + 1) * 128], srcp, 1.0,
                                             None, ALU.subtract).then_inc(sm["a_bmev"].sem, 1); bump(sm["a_bmev"], 1)
                tback_evac(16 * NBH - 1)

        full_barrier()

    if "wout" in stages:
        with ExitStack() as sc:
            wo = sc.enter_context(nc.sbuf_tensor("wo_w", [128, 16 * 16 * 128], BF16))
            am = [sc.enter_context(nc.sbuf_tensor("wo_a0", [128, 16 * 512], BF16))] * 2
            wb = sc.enter_context(nc.sbuf_tensor("wo_wb", [128, 16 * 512], F32))
            xt = [sc.enter_context(nc.sbuf_tensor(f"wo_x{i}", [128, DM], F32)) for i in range(2)]
            x1r = sc.enter_context(nc.sbuf_tensor("wo_xr", [128, DM], F32))
            x1l = sc.enter_context(nc.sbuf_tensor("wo_xl", [128, DM], F32))
            stl = sc.enter_context(nc.sbuf_tensor("wo_st", [128, 32], F32))
            txb = [sc.enter_context(nc.sbuf_tensor("wo_t0", [128, DM], BF16))] * 2
            pw = [P0, P1]
            pt = [PIM, PTR]
            pl = [P2, P3]
            w_w = rsem("wo_w2")
            w_am = [rsem("qk_x0"), rsem("qk_x1")]
            w_xl = [rsem("ln_ld0"), rsem("ln_ld1")]
            w_pe = rsem("qk_pe")
            w_ev = rsem("qk_ev")
            w_tp = rsem("ln_tp")
            w_te = rsem("ln_ev")
            w_m = rsem("ln_m")
            w_var = rsem("ln_var")
            w_xlr = rsem("ln_xl")
            w_ltp = rsem("qk_tg")
            w_lev = rsem("qk_eg")
            w_st = [rsem("ln_st0"), rsem("ln_st1")]
            w_e = rsem("ln_e")
            NI = NB * 4

            @blk.gpsimd
            def _(g_):
                for oc in range(16):
                    dma(g_, wo[:, oc * 2048:(oc + 1) * 2048].rearrange("p (ic m) -> p ic m", ic=16),
                        w_out[:, oc * 128:(oc + 1) * 128].rearrange("(ic p) m -> p ic m", p=128), w_w)

            @blk.sync
            def _(sy):
                for it in range(NI):
                    seq, tq = divmod(it, 4)
                    if it >= 1:
                        sy.wait_ge(w_pe.sem, w_pe.base + 16 * it)
                    for hf in range(2):
                        dma(sy, am[it % 2][hf * 64:hf * 64 + 64, :].rearrange("p (ic t) -> p ic t", ic=16),
                            attnFM[seq, hf:32:2, :, tq * 512:(tq + 1) * 512].transpose([1, 0, 2]),
                            w_am[0])
                    for tt in range(4):
                        n = 4 * it + tt
                        if n >= 2:
                            sy.wait_ge(w_xlr.sem, w_xlr.base + n - 1)
                        dma(sy, xt[n % 2][:, :],
                            x_in[seq, tq * 512 + tt * 128:tq * 512 + (tt + 1) * 128, :],
                            w_xl[n % 2])
                        sy.wait_ge(w_lev.sem, w_lev.base + 16 * (n + 1))
                        sy.wait_ge(w_xlr.sem, w_xlr.base + n + 1)
                        tg = tq * 512 + tt * 128
                        dma(sy, xln2T[seq, :, :, tg:tg + 128].transpose([1, 0, 2]),
                            txb[n % 2][:, :].rearrange("p (dc t) -> p dc t", dc=16), w_st[n % 2])
                        dma(sy, x1D[seq, tg:tg + 128, :], x1r[:, :], w_st[n % 2])
                for i in range(2):
                    sy.wait_ge(w_st[i].sem, w_st[i].val)

            @blk.tensor
            def _(t):
                t.wait_ge(w_w.sem, w_w.val)
                for it in range(NI):
                    seq, tq = divmod(it, 4)
                    t.wait_ge(w_am[0].sem, w_am[0].base + 32 * (it + 1))
                    amv = am[it % 2][:, :].rearrange("p (ic t) -> p ic t", ic=16)
                    for oc in range(16):
                        n = 16 * it + oc
                        if n >= 2:
                            t.wait_ge(w_ev.sem, w_ev.base + n - 1)
                        for ic in range(16):
                            mm = t.matmul(pw[n % 2][:, :],
                                          wo[:, (oc * 16 + ic) * 128:(oc * 16 + ic + 1) * 128],
                                          amv[:, ic, :], start=(ic == 0), stop=(ic == 15))
                        mm.then_inc(w_pe.sem, 1); bump(w_pe, 1)
                    for tt in range(4):
                        n = 4 * it + tt
                        t.wait_ge(w_ev.sem, w_ev.base + 16 * (it + 1))
                        for oc in range(16):
                            k = 16 * n + oc
                            if k >= 2:
                                t.wait_ge(w_te.sem, w_te.base + k - 1)
                            t.transpose(pt[k % 2][:, 0:128],
                                        wb[:, (oc * 4 + tt) * 128:(oc * 4 + tt + 1) * 128],
                                        identF_sb[:, :]).then_inc(w_tp.sem, 1); bump(w_tp, 1)
                        t.wait_ge(w_xlr.sem, w_xlr.base + n + 1)
                        for dc in range(16):
                            k = 16 * n + dc
                            if k >= 2:
                                t.wait_ge(w_lev.sem, w_lev.base + k - 1)
                            t.transpose(pl[k % 2][:, 0:128], x1l[:, dc * 128:(dc + 1) * 128],
                                        identF_sb[:, :]).then_inc(w_ltp.sem, 1); bump(w_ltp, 1)

            @blk.vector
            def _(ve):
                ec = [0]

                def edge(ins):
                    ins.then_inc(w_e.sem, 1); bump(w_e, 1)
                    ec[0] += 1
                    ve.wait_ge(w_e.sem, w_e.base + ec[0])

                edge(ve.memset(stl[:, :], 0.0))
                for it in range(NI):
                    for oc in range(16):
                        n = 16 * it + oc
                        ve.wait_ge(w_pe.sem, w_pe.base + n + 1)
                        ve.tensor_copy(wb[:, oc * 512:(oc + 1) * 512],
                                       pw[n % 2][:, :]).then_inc(w_ev.sem, 1); bump(w_ev, 1)
                    for tt in range(4):
                        n = 4 * it + tt
                        ve.wait_ge(w_xl[n % 2].sem, w_xl[n % 2].base + 16 * (n // 2 + 1))
                        if n >= 1:
                            ve.wait_ge(w_xlr.sem, w_xlr.base + n)
                        for oc in range(16):
                            k = 16 * n + oc
                            ve.wait_ge(w_tp.sem, w_tp.base + k + 1)
                            if n >= 1 and oc == 0:
                                ve.wait_ge(w_st[(n - 1) % 2].sem, w_st[(n - 1) % 2].base + 32 * ((n - 1) // 2 + 1))
                            ve.tensor_add(x1r[:, oc * 128:(oc + 1) * 128], pt[k % 2][:, 0:128],
                                          xt[n % 2][:, oc * 128:(oc + 1) * 128]
                                          ).then_inc(w_te.sem, 1); bump(w_te, 1)
                        for cch in range(4):
                            edge(ve.bn_stats(stl[:, 6 * cch:6 * cch + 6],
                                             x1r[:, 512 * cch:512 * (cch + 1)]))
                        edge(ve.bn_aggr(stl[:, 24:26], stl[:, 0:12]))
                        edge(ve.bn_aggr(stl[:, 26:28], stl[:, 12:24]))
                        edge(ve.tensor_tensor(stl[:, 2:3], stl[:, 24:25], stl[:, 26:27], ALU.add))
                        edge(ve.tensor_scalar(stl[:, 2:3], stl[:, 2:3], 0.5, None, ALU.mult))
                        edge(ve.tensor_tensor(stl[:, 3:4], stl[:, 24:25], stl[:, 26:27],
                                              ALU.subtract))
                        edge(ve.tensor_scalar(stl[:, 3:4], stl[:, 3:4], 0.5, None, ALU.mult))
                        edge(ve.tensor_scalar(stl[:, 3:4], stl[:, 3:4], stl[:, 3:4], None, ALU.mult))
                        edge(ve.tensor_tensor(stl[:, 4:5], stl[:, 25:26], stl[:, 27:28], ALU.add))
                        ve.tensor_scalar(stl[:, 4:5], stl[:, 4:5], 0.5, stl[:, 3:4],
                                          ALU.mult, ALU.add).then_inc(w_m.sem, 1); bump(w_m, 1)
                        ve.wait_ge(w_var.sem, w_var.base + n + 1)
                        edge(ve.reciprocal(stl[:, 6:7], stl[:, 5:6]))
                        ve.tensor_scalar(x1l[:, :], x1r[:, :], stl[:, 2:3], stl[:, 6:7],
                                         ALU.subtract, ALU.mult).then_inc(w_xlr.sem, 1); bump(w_xlr, 1)
                        for dc in range(16):
                            k = 16 * n + dc
                            ve.wait_ge(w_ltp.sem, w_ltp.base + k + 1)
                            if n >= 1 and dc == 0:
                                ve.wait_ge(w_st[(n - 1) % 2].sem, w_st[(n - 1) % 2].base + 32 * ((n - 1) // 2 + 1))
                            ve.tensor_copy(txb[n % 2][:, dc * 128:(dc + 1) * 128],
                                           pl[k % 2][:, 0:128]).then_inc(w_lev.sem, 1); bump(w_lev, 1)

            @blk.scalar
            def _(s):
                s.wait_ge(w_e.sem, w_e.base + 1)
                for it in range(NI):
                    for tt in range(4):
                        n = 4 * it + tt
                        s.wait_ge(w_m.sem, w_m.base + n + 1)
                        s.activation(stl[:, 5:6], stl[:, 4:5], AF.Sqrt, bias=smallc[:, 0:1],
                                     scale=1.0).then_inc(w_var.sem, 1); bump(w_var, 1)

        full_barrier()

    if "ffna" in stages:
        with ExitStack() as sc:
            xr = sc.enter_context(nc.sbuf_tensor("fa_x", [128, 16 * NB * S], BF16))
            wgs = [sc.enter_context(nc.sbuf_tensor(f"fa_wg{i}", [128, 16 * 128], BF16)) for i in range(2)]
            wus = [sc.enter_context(nc.sbuf_tensor(f"fa_wu{i}", [128, 16 * 128], BF16)) for i in range(2)]
            gsil = sc.enter_context(nc.sbuf_tensor("fa_gs", [128, 512], F32))
            gbuf = [sc.enter_context(nc.sbuf_tensor("fa_gb0", [128, NB * S], BF16))] * 2
            pg = [P0, P1]
            pu = [P2, P3]
            f_x = rsem("cm_ld")
            f_w = [rsem("fa_w0"), rsem("fa_w1")]
            f_pg = rsem("ro_pe")
            f_pu = rsem("ro_t2")
            f_si = rsem("ro_dv")
            f_gt = rsem("cm_mm")
            f_st = [rsem("fa_st0"), rsem("fa_st1")]

            @blk.sync
            def _(sy):
                for seq in range(NB):
                    for dc in range(16):
                        dma(sy, xr[:, (dc * NB + seq) * S:(dc * NB + seq + 1) * S],
                            xln2T[seq, dc, :, :], f_x)

            @blk.gpsimd
            def _(g_):
                for hc in range(65):
                    if hc < 64:
                        if hc >= 2:
                            g_.wait_ge(f_pu.sem, f_pu.base + 8 * (hc - 1))
                        dma(g_, wgs[hc % 2][:, :].rearrange("p (ic m) -> p ic m", ic=16),
                            ffn_wgT[:, hc * 128:(hc + 1) * 128].rearrange("(ic p) m -> p ic m", p=128),
                            f_w[hc % 2])
                        dma(g_, wus[hc % 2][:, :].rearrange("p (ic m) -> p ic m", ic=16),
                            ffn_wuT[:, hc * 128:(hc + 1) * 128].rearrange("(ic p) m -> p ic m", p=128),
                            f_w[hc % 2])
                    if hc >= 1:
                        g_.wait_ge(f_gt.sem, f_gt.base + 8 * hc)
                        dma(g_, gatedD[hc - 1, :, :], gbuf[(hc - 1) % 2][:, :], f_st[(hc - 1) % 2])
                for i in range(2):
                    g_.wait_ge(f_st[i].sem, f_st[i].val)

            @blk.tensor
            def _(t):
                t.wait_ge(f_x.sem, f_x.val)
                for hc in range(64):
                    t.wait_ge(f_w[hc % 2].sem, f_w[hc % 2].base + 32 * (hc // 2 + 1))
                    for tb in range(8):
                        n = 8 * hc + tb
                        if n >= 2:
                            t.wait_ge(f_si.sem, f_si.base + n - 1)
                        for ic in range(16):
                            mm = t.matmul(pg[n % 2][:, :], wgs[hc % 2][:, ic * 128:(ic + 1) * 128],
                                          xr[:, :].rearrange("p (ic t) -> p ic t", ic=16)[:, ic,
                                          tb * 512:(tb + 1) * 512],
                                          start=(ic == 0), stop=(ic == 15))
                        mm.then_inc(f_pg.sem, 1); bump(f_pg, 1)
                        if n >= 2:
                            t.wait_ge(f_gt.sem, f_gt.base + n - 1)
                        for ic in range(16):
                            mm = t.matmul(pu[n % 2][:, :], wus[hc % 2][:, ic * 128:(ic + 1) * 128],
                                          xr[:, :].rearrange("p (ic t) -> p ic t", ic=16)[:, ic,
                                          tb * 512:(tb + 1) * 512],
                                          start=(ic == 0), stop=(ic == 15))
                        mm.then_inc(f_pu.sem, 1); bump(f_pu, 1)

            @blk.scalar
            def _(s):
                for hc in range(64):
                    for tb in range(8):
                        n = 8 * hc + tb
                        s.wait_ge(f_pg.sem, f_pg.base + n + 1)
                        if n >= 1:
                            s.wait_ge(f_gt.sem, f_gt.base + n)
                        s.activation(gsil[:, :], pg[n % 2][:, :], AF.Silu).then_inc(f_si.sem, 1); bump(f_si, 1)

            @blk.vector
            def _(ve):
                for hc in range(64):
                    for tb in range(8):
                        n = 8 * hc + tb
                        ve.wait_ge(f_si.sem, f_si.base + n + 1)
                        ve.wait_ge(f_pu.sem, f_pu.base + n + 1)
                        if hc >= 1 and tb == 0:
                            ve.wait_ge(f_st[(hc - 1) % 2].sem,
                                       f_st[(hc - 1) % 2].base + 16 * ((hc - 1) // 2 + 1))
                        ve.tensor_mul(gbuf[hc % 2][:, tb * 512:(tb + 1) * 512], gsil[:, :],
                                      pu[n % 2][:, :]).then_inc(f_gt.sem, 1); bump(f_gt, 1)

        full_barrier()

    if "ffnb" in stages:
        with ExitStack() as sc:
            gr = sc.enter_context(nc.sbuf_tensor("fb_g", [128, 64 * 512], BF16))
            wds = [sc.enter_context(nc.sbuf_tensor(f"fb_w{i}", [128, 64 * 128], BF16)) for i in range(2)]
            ob = sc.enter_context(nc.sbuf_tensor("fb_o", [128, 16 * 512], F32))
            x1t = [sc.enter_context(nc.sbuf_tensor("fb_x0", [128, DM], F32))] * 2
            orow = [sc.enter_context(nc.sbuf_tensor(f"fb_or{i}", [128, DM], F32)) for i in range(2)]
            pd = [P0, P1]
            pt2 = [PIM, PTR]
            b_g = rsem("cm_ld")
            b_w = [rsem("fb_w0"), rsem("fb_w1")]
            b_x = [rsem("ln_ld0"), rsem("ln_ld1")]
            b_pe = rsem("qk_pe")
            b_ev = rsem("qk_ev")
            b_tp = rsem("ln_tp")
            b_te = rsem("ln_ev")
            b_st = [rsem("ln_st0"), rsem("ln_st1")]
            NSH = 8

            @blk.sync
            def _(sy):
                for sh in range(NSH):
                    if sh >= 1:
                        sy.wait_ge(b_pe.sem, b_pe.base + 16 * sh)
                    for hc in range(64):
                        dma(sy, gr[:, hc * 512:(hc + 1) * 512],
                            gatedD[hc, :, sh * 512:(sh + 1) * 512], b_g)
                    for tt in range(4):
                        n = 4 * sh + tt
                        tg = sh * 512 + tt * 128
                        seq, trow = divmod(tg, S)
                        if n >= 1:
                            sy.wait_ge(b_te.sem, b_te.base + 16 * n)
                        dma(sy, x1t[n % 2][:, :], x1D[seq, trow:trow + 128, :], b_x[n % 2])
                        sy.wait_ge(b_te.sem, b_te.base + 16 * (n + 1))
                        dma(sy, out[seq, trow:trow + 128, :], orow[n % 2][:, :], b_st[n % 2])
                for i in range(2):
                    sy.wait_ge(b_st[i].sem, b_st[i].val)

            @blk.gpsimd
            def _(g_):
                for sh in range(NSH):
                    for oc in range(16):
                        n = 16 * sh + oc
                        if n >= 2:
                            g_.wait_ge(b_pe.sem, b_pe.base + n - 1)
                        dma(g_, wds[n % 2][:, :].rearrange("p (hc m) -> p hc m", hc=64),
                            ffn_wdT[:, oc * 128:(oc + 1) * 128].rearrange("(hc p) m -> p hc m", p=128),
                            b_w[n % 2])

            @blk.tensor
            def _(t):
                for sh in range(NSH):
                    t.wait_ge(b_g.sem, b_g.base + 64 * 16 * (sh + 1))
                    for oc in range(16):
                        n = 16 * sh + oc
                        t.wait_ge(b_w[n % 2].sem, b_w[n % 2].base + 16 * (n // 2 + 1))
                        if n >= 2:
                            t.wait_ge(b_ev.sem, b_ev.base + n - 1)
                        for hc in range(64):
                            mm = t.matmul(pd[n % 2][:, :],
                                          wds[n % 2][:, hc * 128:(hc + 1) * 128],
                                          gr[:, :].rearrange("p (hc t) -> p hc t", hc=64)[:, hc, :],
                                          start=(hc == 0), stop=(hc == 63))
                        mm.then_inc(b_pe.sem, 1); bump(b_pe, 1)
                    for tt in range(4):
                        n = 4 * sh + tt
                        t.wait_ge(b_ev.sem, b_ev.base + 16 * (sh + 1))
                        for oc in range(16):
                            k = 16 * n + oc
                            if k >= 2:
                                t.wait_ge(b_te.sem, b_te.base + k - 1)
                            t.transpose(pt2[k % 2][:, 0:128],
                                        ob[:, (oc * 4 + tt) * 128:(oc * 4 + tt) * 128 + 128],
                                        identF_sb[:, :]).then_inc(b_tp.sem, 1); bump(b_tp, 1)

            @blk.vector
            def _(ve):
                for sh in range(NSH):
                    for oc in range(16):
                        n = 16 * sh + oc
                        ve.wait_ge(b_pe.sem, b_pe.base + n + 1)
                        if sh >= 1 and oc == 0:
                            ve.wait_ge(b_tp.sem, b_tp.base + 64 * sh)
                        ve.tensor_copy(ob[:, oc * 512:(oc + 1) * 512],
                                       pd[n % 2][:, :]).then_inc(b_ev.sem, 1); bump(b_ev, 1)
                    for tt in range(4):
                        n = 4 * sh + tt
                        ve.wait_ge(b_x[n % 2].sem, b_x[n % 2].base + 16 * (n // 2 + 1))
                        for oc in range(16):
                            k = 16 * n + oc
                            ve.wait_ge(b_tp.sem, b_tp.base + k + 1)
                            if n >= 2 and oc == 0:
                                ve.wait_ge(b_st[n % 2].sem, b_st[n % 2].base + 16 * (n // 2))
                            ve.tensor_add(orow[n % 2][:, oc * 128:(oc + 1) * 128], pt2[k % 2][:, 0:128],
                                          x1t[n % 2][:, oc * 128:(oc + 1) * 128]
                                          ).then_inc(b_te.sem, 1); bump(b_te, 1)

        full_barrier()

    if "ffnb" not in stages:
        s_fin = sem("fin")
        zb = ctx.enter_context(nc.sbuf_tensor("zb", [128, DM], F32))

        @blk.vector
        def _(ve):
            ve.memset(zb[:, :], 0.0).then_inc(s_fin.sem, 1)
        s_fin.val += 1

        @blk.sync
        def _(sy):
            sy.wait_ge(s_fin.sem, s_fin.val)
            for seq in range(NB):
                for tc in range(16):
                    dma(sy, out[seq, tc * 128:(tc + 1) * 128, :], zb[:, :], s_fin)
            sy.wait_ge(s_fin.sem, s_fin.val)

    ctx.close()
    nc.finalize()
    return nc


def prep_inputs(inputs):
    c = make_consts()
    w = {
        "w_qkv": inputs["L0_attn_w_qkv"],
        "w_gate": inputs["L0_attn_w_gate"],
        "b_gate": np.ascontiguousarray(inputs["L0_attn_b_gate"].reshape(96, 1)),
        "w_out": inputs["L0_attn_w_out"],
        "k_cw1": inputs["L0_attn_k_cw1"],
        "k_cb1": np.ascontiguousarray(inputs["L0_attn_k_cb1"].reshape(CDIM, 1)),
        "k_cw2": inputs["L0_attn_k_cw2"],
        "k_cb2": np.ascontiguousarray(inputs["L0_attn_k_cb2"].reshape(DH, 1)),
        "v_cw1": inputs["L0_attn_v_cw1"],
        "v_cb1": np.ascontiguousarray(inputs["L0_attn_v_cb1"].reshape(CDIM, 1)),
        "v_cw2": inputs["L0_attn_v_cw2"],
        "v_cb2": np.ascontiguousarray(inputs["L0_attn_v_cb2"].reshape(DH, 1)),
        "kposT": np.ascontiguousarray(inputs["L0_attn_k_pos"].transpose(0, 2, 1)),
        "vposT": np.ascontiguousarray(inputs["L0_attn_v_pos"].transpose(0, 2, 1)),
        "memckT": np.ascontiguousarray(inputs["L0_attn_mem_ck"].reshape(KVH, DH).T),
        "memcv": np.ascontiguousarray(inputs["L0_attn_mem_cv"].reshape(KVH, DH)),
        "ffn_wgT": np.ascontiguousarray(inputs["L0_ffn_w_gate"].T),
        "ffn_wuT": np.ascontiguousarray(inputs["L0_ffn_w_up"].T),
        "ffn_wdT": np.ascontiguousarray(inputs["L0_ffn_w_down"].T),
    }
    w.update(c)
    return w


def kernel(**inputs):
    x = inputs["L0_attn_hidden_states"]
    shared = prep_inputs(inputs)
    nc = build()
    in_maps = []
    for core in range(8):
        m = dict(shared)
        m["x"] = np.ascontiguousarray(x[2 * core:2 * core + 2])
        in_maps.append(m)
    res = run_bass_kernel_spmd(nc, in_maps, core_ids=list(range(8)))
    return np.concatenate([r["out"] for r in res.results], axis=0)


# version 06
# speedup vs baseline: 1.2779x; 1.0592x over previous
import numpy as np
import ml_dtypes
from contextlib import ExitStack

import concourse.bass as bass
import concourse.mybir as mybir
from concourse.bass_utils import run_bass_kernel_spmd

F32 = mybir.dt.float32
BF16 = mybir.dt.bfloat16
AF = mybir.ActivationFunctionType
ALU = mybir.AluOpType
AX = mybir.AxisListType

NB = 2
S = 2048
DM = 2048
HEADS, KVH, G, DH = 32, 8, 4, 64
CBLK = SBLK = 16
NSEL = 8
WIN = 64
SCALE = DH ** -0.5
CDIM = CBLK * DH
NBLK = S // CBLK
DI = HEADS * DH
DKV = KVH * DH
DFF = 8192
BIGF = 1600.0
NEG30 = -1e30
EPS = 1e-5

ALL_STAGES = ("ln1", "qkv", "rope", "cmp", "attn", "wout", "ffna", "ffnb")


def make_consts():
    c = {}
    ident = np.eye(128, dtype=np.float32)
    c["identF"] = ident
    c["identB"] = ident
    R = np.zeros((DH, DH), np.float32)
    for i in range(DH // 2):
        R[2 * i, 2 * i + 1] = -1.0
        R[2 * i + 1, 2 * i] = 1.0
    R2T = np.zeros((128, 128), np.float32)
    R2T[:64, :64] = R.T
    R2T[64:, 64:] = R.T
    c["R2T"] = R2T
    inv = 1.0 / 10000.0 ** (np.arange(0, DH, 2, dtype=np.float32) / DH)
    ang = np.arange(S, dtype=np.float32)[None, :] * inv[:, None]
    cos = np.repeat(np.cos(ang), 2, axis=0)
    sin = np.repeat(np.sin(ang), 2, axis=0)
    c["C2"] = np.ascontiguousarray(np.concatenate([cos, cos], 0))
    c["S2"] = np.ascontiguousarray(np.concatenate([sin, sin], 0))
    blk = np.arange(NBLK)[:, None]
    qi = np.arange(S)[None, :]
    valid = (16 * (blk + 1) - 1) < qi
    c["cmaskADD"] = np.where(valid, 0.0, -BIGF).astype(np.float32)
    c["cmaskNEG"] = np.where(valid, 0.0, NEG30).astype(np.float32)
    key = np.arange(S)[None, :]
    c["EBIG"] = np.where(np.arange(NBLK)[:, None] == (key // SBLK), BIGF, 0.0).astype(np.float32)
    kk = np.arange(128)[:, None]
    qq = np.arange(128)[None, :]
    own01 = ((qq // 16) == (kk // 16)) & ((qq % 16) >= (kk % 16))
    c["ownADD"] = np.where(own01, BIGF, 0.0).astype(np.float32)
    j = np.arange(WIN)[:, None]
    q = np.arange(WIN)[None, :]
    c["sADDp"] = np.where(q <= j, 0.0, -BIGF).astype(np.float32)
    c["sADDo"] = np.where(j <= q, 0.0, -BIGF).astype(np.float32)
    return c


class Ctr:
    __slots__ = ("sem", "val")

    def __init__(self, sem):
        self.sem = sem
        self.val = 0


def build(dbg=(), stages=ALL_STAGES):
    nc = bass.Bass("TRN2", target_bir_lowering=False)

    def din(name, shape, dtype=F32):
        return nc.dram_tensor(name, list(shape), dtype, kind="ExternalInput").ap()

    def dscr(name, shape, dtype):
        kind = "ExternalOutput" if name in dbg else "Internal"
        return nc.dram_tensor(name, list(shape), dtype, kind=kind).ap()

    x_in = din("x", (NB, S, DM))
    w_qkv = din("w_qkv", (DM, DI + 2 * DKV))
    w_gate = din("w_gate", (DM, 96))
    b_gate = din("b_gate", (96, 1))
    w_out = din("w_out", (DI, DM))
    k_cw1 = din("k_cw1", (CDIM, CDIM))
    k_cb1 = din("k_cb1", (CDIM, 1))
    k_cw2 = din("k_cw2", (CDIM, DH))
    k_cb2 = din("k_cb2", (DH, 1))
    v_cw1 = din("v_cw1", (CDIM, CDIM))
    v_cb1 = din("v_cb1", (CDIM, 1))
    v_cw2 = din("v_cw2", (CDIM, DH))
    v_cb2 = din("v_cb2", (DH, 1))
    kposT = din("kposT", (KVH, DH, CBLK))
    vposT = din("vposT", (KVH, DH, CBLK))
    memckT = din("memckT", (DH, KVH))
    memcv = din("memcv", (KVH, DH))
    ffn_wgT = din("ffn_wgT", (DM, DFF))
    ffn_wuT = din("ffn_wuT", (DM, DFF))
    ffn_wdT = din("ffn_wdT", (DFF, DM))
    identF = din("identF", (128, 128))
    identB_in = din("identB", (128, 128))
    R2T_in = din("R2T", (128, 128))
    C2_in = din("C2", (128, S))
    S2_in = din("S2", (128, S))
    cmaskADD_in = din("cmaskADD", (128, S))
    cmaskNEG_in = din("cmaskNEG", (128, S))
    EBIG_in = din("EBIG", (128, S))
    ownADD_in = din("ownADD", (128, 128))
    sADDp_in = din("sADDp", (64, 64))
    sADDo_in = din("sADDo", (64, 64))

    out = nc.dram_tensor("out", [NB, S, DM], F32, kind="ExternalOutput").ap()
    xlnT = dscr("xlnT", (NB, 16, 128, S), BF16)
    qF = dscr("qF", (NB, 16, 128, S), F32)
    kF = dscr("kF", (NB, 4, 128, S), F32)
    vF = dscr("vF", (NB, 4, 128, S), F32)
    gatesT = dscr("gatesT", (NB, 16, 128, 96), F32)
    rqT = dscr("rqT", (NB, 16, 128, S), BF16)
    rkT = dscr("rkT", (NB, 4, 128, S), BF16)
    kbT = dscr("kbT", (NB, 4, 128, S), BF16)
    vbT = dscr("vbT", (NB, 4, 128, S), BF16)
    ckD = dscr("ckD", (DH, NB * KVH * NBLK), F32)
    cvD = dscr("cvD", (DH, NB * KVH * NBLK), F32)
    attnFM = dscr("attnFM", (NB, 32, 64, S), BF16)
    x1D = dscr("x1D", (NB, S, DM), F32)
    xln2T = dscr("xln2T", (NB, 16, 128, S), BF16)
    gatedD = dscr("gatedD", (64, 128, NB * S), BF16)
    wdB = dscr("wdB", (DFF, DM), BF16)
    bmD = dscr("bmD", (NB, KVH, 16, 128, 128), BF16)

    ctx = ExitStack()
    blk = ctx.enter_context(nc.Block())
    _sems = {}

    def sem(name):
        if name not in _sems:
            _sems[name] = Ctr(ctx.enter_context(nc.semaphore(name)))
        return _sems[name]

    class RS:
        __slots__ = ("c", "base")

        def __init__(self, c):
            self.c = c
            self.base = c.val

        @property
        def sem(self):
            return self.c.sem

        @property
        def val(self):
            return self.c.val

    def rsem(pool):
        return RS(sem(pool))

    def bump(x, n=1):
        (x.c if isinstance(x, RS) else x).val += n

    def dma(eng, dst, src, ctr):
        tgt = ctr.c if isinstance(ctr, RS) else ctr
        eng.dma_start(dst, src).then_inc(tgt.sem, 16)
        tgt.val += 16

    def full_barrier():
        bar = sem("bar")
        bar.val += 5
        v = bar.val

        @blk.tensor
        def _(t):
            t.sem_inc(bar.sem, 1)
            t.wait_ge(bar.sem, v)

        @blk.vector
        def _(ve):
            ve.sem_inc(bar.sem, 1)
            ve.wait_ge(bar.sem, v)

        @blk.scalar
        def _(s):
            s.sem_inc(bar.sem, 1)
            s.wait_ge(bar.sem, v)

        @blk.gpsimd
        def _(g):
            g.sem_inc(bar.sem, 1)
            g.wait_ge(bar.sem, v)

        @blk.sync
        def _(sy):
            sy.sem_inc(bar.sem, 1)
            sy.wait_ge(bar.sem, v)

    ckT = ctx.enter_context(nc.sbuf_tensor("ckT", [DH, NB * KVH * NBLK], BF16))
    cvT = ctx.enter_context(nc.sbuf_tensor("cvT", [DH, NB * KVH * NBLK], F32))
    cvKM = ctx.enter_context(nc.sbuf_tensor("cvKM", [128, NB * KVH * 65], BF16))
    memcv_st = ctx.enter_context(nc.sbuf_tensor("memcv_st", [1, KVH * 65], BF16))
    memck_sb = ctx.enter_context(nc.sbuf_tensor("memck_sb", [DH, KVH], BF16))
    identB = ctx.enter_context(nc.sbuf_tensor("identB_sb", [128, 128], BF16))
    identF_sb = ctx.enter_context(nc.sbuf_tensor("identF_sb", [128, 128], F32))
    smallc = ctx.enter_context(nc.sbuf_tensor("smallc", [128, 4], F32))
    P0 = ctx.enter_context(nc.psum_tensor("P0", [128, 512], F32))
    P1 = ctx.enter_context(nc.psum_tensor("P1", [128, 512], F32))
    P2 = ctx.enter_context(nc.psum_tensor("P2", [128, 512], F32))
    P3 = ctx.enter_context(nc.psum_tensor("P3", [128, 512], F32))
    P4 = ctx.enter_context(nc.psum_tensor("P4", [128, 512], F32))
    PTR = ctx.enter_context(nc.psum_tensor("PTR", [128, 390], F32))
    PIM = ctx.enter_context(nc.psum_tensor("PIM", [128, 128], F32))
    PMEM = ctx.enter_context(nc.psum_tensor("PMEM", [1, 512], F32))
    cst = {}
    for nm, src, shp, dt in [("R2T", R2T_in, (128, 128), F32), ("C2", C2_in, (128, S), F32),
                             ("S2", S2_in, (128, S), F32), ("cmaskADD", cmaskADD_in, (128, S), BF16),
                             ("cmaskNEG", cmaskNEG_in, (128, S), F32), ("EBIG", EBIG_in, (128, S), BF16),
                             ("ownADD", ownADD_in, (128, 128), BF16), ("sADDp", sADDp_in, (64, 64), BF16),
                             ("sADDo", sADDo_in, (64, 64), BF16)]:
        cst[nm] = ctx.enter_context(nc.sbuf_tensor("c_" + nm, list(shp), dt))

    ld0 = sem("ld0")

    @blk.gpsimd
    def _(g):
        dma(g, identB[:, :], identB_in[:, :], ld0)
        dma(g, identF_sb[:, :], identF[:, :], ld0)
        for nm, src in [("R2T", R2T_in), ("C2", C2_in), ("S2", S2_in), ("cmaskADD", cmaskADD_in),
                        ("cmaskNEG", cmaskNEG_in), ("EBIG", EBIG_in), ("ownADD", ownADD_in),
                        ("sADDp", sADDp_in), ("sADDo", sADDo_in)]:
            dma(g, cst[nm][:, :], src[:, :], ld0)
        dma(g, memck_sb[:, :], memckT[:, :], ld0)
        for h in range(KVH):
            dma(g, memcv_st[0:1, h * 65:h * 65 + 64], memcv[h:h + 1, :], ld0)

    @blk.vector
    def _(ve):
        ve.wait_ge(ld0.sem, ld0.val)
        ve.memset(memcv_st[0:1, :].rearrange("p (h o) -> p h o", h=KVH)[:, :, 64:65], 1.0)
        ve.memset(smallc[:, 0:1], EPS)
        ve.memset(smallc[:, 1:2], -1000.0)

    full_barrier()

    if "ln1" in stages:
        with ExitStack() as sc:
            NT = NB * 16
            xt = [sc.enter_context(nc.sbuf_tensor(f"ln_x{i}", [128, DM], F32)) for i in range(2)]
            xl = [sc.enter_context(nc.sbuf_tensor(f"ln_o{i}", [128, DM], F32)) for i in range(2)]
            tx = [sc.enter_context(nc.sbuf_tensor(f"ln_t{i}", [128, DM], BF16)) for i in range(2)]
            st = sc.enter_context(nc.sbuf_tensor("ln_st", [128, 32], F32))
            sq = sc.enter_context(nc.sbuf_tensor("ln_sq", [128, DM], F32))
            pt = [PIM, PTR]
            s_ld = [sem("ln_ld0"), sem("ln_ld1")]
            s_m = sem("ln_m")
            s_var = sem("ln_var")
            s_xl = sem("ln_xl")
            s_tp = sem("ln_tp")
            s_ev = sem("ln_ev")
            s_st = [sem("ln_st0"), sem("ln_st1")]

            @blk.sync
            def _(sy):
                for i in range(min(2, NT)):
                    seq, tc = divmod(i, 16)
                    dma(sy, xt[i % 2][:, :], x_in[seq, tc * 128:(tc + 1) * 128, :], s_ld[i % 2])
                for i in range(NT):
                    sy.wait_ge(s_ev.sem, 16 * (i + 1))
                    seq, tc = divmod(i, 16)
                    dma(sy, xlnT[seq, :, :, tc * 128:(tc + 1) * 128].transpose([1, 0, 2]),
                        tx[i % 2][:, :].rearrange("p (dc t) -> p dc t", dc=16), s_st[i % 2])
                    if i + 2 < NT:
                        sy.wait_ge(s_xl.sem, i + 1)
                        seq2, tc2 = divmod(i + 2, 16)
                        dma(sy, xt[i % 2][:, :], x_in[seq2, tc2 * 128:(tc2 + 1) * 128, :], s_ld[i % 2])
                sy.wait_ge(s_st[0].sem, s_st[0].val)
                sy.wait_ge(s_st[1].sem, s_st[1].val)

            s_e = sem("ln_e")
            s_sq = sem("ln_sq2")
            s_v2 = sem("ln_v2")

            @blk.vector
            def _(ve):
                for i in range(NT):
                    bi = i % 2
                    ve.wait_ge(s_ld[i % 2].sem, 16 * (i // 2 + 1))
                    if i >= 1:
                        ve.wait_ge(s_xl.sem, i)
                    ve.tensor_reduce(st[:, 0:1], xt[bi][:, :], AX.X, ALU.add).then_inc(s_e.sem, 1); bump(s_e, 1)
                    ve.wait_ge(s_e.sem, 5 * i + 1)
                    ve.tensor_scalar(st[:, 2:3], st[:, 0:1], 1.0 / DM, None, ALU.mult).then_inc(s_e.sem, 1); bump(s_e, 1)
                    ve.wait_ge(s_sq.sem, i + 1)
                    ve.tensor_reduce(st[:, 1:2], sq[:, :], AX.X, ALU.add).then_inc(s_v2.sem, 1); bump(s_v2, 1)
                    ve.wait_ge(s_v2.sem, i + 1)
                    ve.tensor_scalar(st[:, 3:4], st[:, 2:3], st[:, 2:3], None, ALU.mult).then_inc(s_e.sem, 1); bump(s_e, 1)
                    ve.wait_ge(s_e.sem, 5 * i + 3)
                    ve.tensor_scalar(st[:, 4:5], st[:, 1:2], 1.0 / DM, st[:, 3:4],
                                     ALU.mult, ALU.subtract).then_inc(s_m.sem, 1); bump(s_m, 1)
                    ve.wait_ge(s_var.sem, i + 1)
                    ve.reciprocal(st[:, 27:28], st[:, 26:27]).then_inc(s_e.sem, 1); bump(s_e, 1)
                    ve.wait_ge(s_e.sem, 5 * i + 4)
                    ve.tensor_scalar(xl[bi][:, :], xt[bi][:, :], st[:, 2:3], st[:, 27:28],
                                     ALU.subtract, ALU.mult).then_inc(s_xl.sem, 1); bump(s_xl, 1)
                    ve.sem_inc(s_e.sem, 1); bump(s_e, 1)
                    for dc in range(16):
                        n = 16 * i + dc
                        ve.wait_ge(s_tp.sem, n + 1)
                        if i >= 2 and dc == 0:
                            ve.wait_ge(s_st[i % 2].sem, 16 * (i // 2))
                        ve.tensor_copy(tx[bi][:, dc * 128:(dc + 1) * 128], pt[n % 2][:, 0:128]).then_inc(s_ev.sem, 1); bump(s_ev, 1)

            @blk.scalar
            def _(s):
                for i in range(NT):
                    bi = i % 2
                    s.wait_ge(s_ld[i % 2].sem, 16 * (i // 2 + 1))
                    if i >= 1:
                        s.wait_ge(s_v2.sem, i)
                    s.activation(sq[:, :], xt[bi][:, :], AF.Square).then_inc(s_sq.sem, 1); bump(s_sq, 1)
                    s.wait_ge(s_m.sem, i + 1)
                    s.activation(st[:, 26:27], st[:, 4:5], AF.Sqrt, bias=smallc[:, 0:1],
                                 scale=1.0).then_inc(s_var.sem, 1); bump(s_var, 1)

            @blk.tensor
            def _(t):
                for i in range(NT):
                    bi = i % 2
                    t.wait_ge(s_xl.sem, i + 1)
                    for dc in range(16):
                        n = 16 * i + dc
                        if n >= 2:
                            t.wait_ge(s_ev.sem, n - 1)
                        t.transpose(pt[n % 2][:, 0:128], xl[bi][:, dc * 128:(dc + 1) * 128],
                                    identF_sb[:, :]).then_inc(s_tp.sem, 1); bump(s_tp, 1)

        full_barrier()

    if "qkv" in stages:
        with ExitStack() as sc:
            OC = 24
            wq = sc.enter_context(nc.sbuf_tensor("qk_w", [128, OC * 16 * 128], BF16))
            wg = sc.enter_context(nc.sbuf_tensor("qk_wg", [128, 16 * 96], BF16))
            bg = sc.enter_context(nc.sbuf_tensor("qk_bg", [96, 1], F32))
            xs = [sc.enter_context(nc.sbuf_tensor(f"qk_x{i}", [128, 16 * 512], BF16)) for i in range(2)]
            stg = [sc.enter_context(nc.sbuf_tensor(f"qk_s{i}", [128, 512], F32)) for i in range(4)]
            gst = sc.enter_context(nc.sbuf_tensor("qk_gs", [96, 512], F32))
            gtt = sc.enter_context(nc.sbuf_tensor("qk_gt", [128, 4 * 96], F32))
            pq = [P0, P1]
            pg = P2
            pgt = P3
            s_w = sem("qk_w")
            s_x = [sem("qk_x0"), sem("qk_x1")]
            s_pe = sem("qk_pe")
            s_ev = sem("qk_ev")
            s_pg = sem("qk_pg")
            s_sg = sem("qk_sg")
            s_tg = sem("qk_tg")
            s_eg = sem("qk_eg")
            s_st = [sem(f"qk_st{i}") for i in range(4)]
            s_gs = sem("qk_gs2")
            NT = NB * 4

            @blk.gpsimd
            def _(g):
                for oc in range(OC):
                    dma(g, wq[:, oc * 2048:(oc + 1) * 2048].rearrange("p (dc m) -> p dc m", dc=16),
                        w_qkv[:, oc * 128:(oc + 1) * 128].rearrange("(dc p) m -> p dc m", p=128), s_w)
                dma(g, wg[:, :].rearrange("p (dc m) -> p dc m", dc=16),
                    w_gate[:, :].rearrange("(dc p) m -> p dc m", p=128), s_w)
                dma(g, bg[:, :], b_gate[:, :], s_w)

            @blk.sync
            def _(sy):
                for it in range(min(2, NT)):
                    seq, tq = divmod(it, 4)
                    dma(sy, xs[it % 2][:, :].rearrange("p (dc t) -> p dc t", dc=16),
                        xlnT[seq, :, :, tq * 512:(tq + 1) * 512].transpose([1, 0, 2]), s_x[it % 2])
                for it in range(NT):
                    seq, tq = divmod(it, 4)
                    for oc in range(OC):
                        n = it * OC + oc
                        sy.wait_ge(s_ev.sem, n + 1)
                        dstf = (qF[seq, oc, :, tq * 512:(tq + 1) * 512] if oc < 16 else
                                kF[seq, oc - 16, :, tq * 512:(tq + 1) * 512] if oc < 20 else
                                vF[seq, oc - 20, :, tq * 512:(tq + 1) * 512])
                        dma(sy, dstf, stg[n % 4][:, :], s_st[n % 4])
                    sy.wait_ge(s_eg.sem, 4 * (it + 1))
                    dma(sy, gatesT[seq, tq * 4:(tq + 1) * 4, :, :].transpose([1, 0, 2]),
                        gtt[:, :].rearrange("p (tt g) -> p tt g", tt=4), s_gs)
                    if it + 2 < NT:
                        seq2, tq2 = divmod(it + 2, 4)
                        dma(sy, xs[it % 2][:, :].rearrange("p (dc t) -> p dc t", dc=16),
                            xlnT[seq2, :, :, tq2 * 512:(tq2 + 1) * 512].transpose([1, 0, 2]), s_x[it % 2])
                for i in range(4):
                    sy.wait_ge(s_st[i].sem, s_st[i].val)
                sy.wait_ge(s_gs.sem, s_gs.val)

            @blk.tensor
            def _(t):
                t.wait_ge(s_w.sem, 16 * (OC + 2))
                for it in range(NT):
                    xv = xs[it % 2][:, :].rearrange("p (dc t) -> p dc t", dc=16)
                    t.wait_ge(s_x[it % 2].sem, 16 * (it // 2 + 1))
                    for oc in range(OC):
                        n = it * OC + oc
                        if n >= 2:
                            t.wait_ge(s_ev.sem, n - 1)
                        for dc in range(16):
                            mm = t.matmul(pq[n % 2][:, :], wq[:, (oc * 16 + dc) * 128:(oc * 16 + dc + 1) * 128],
                                          xv[:, dc, :], start=(dc == 0), stop=(dc == 15))
                        mm.then_inc(s_pe.sem, 1); bump(s_pe, 1)
                    if it >= 1:
                        t.wait_ge(s_sg.sem, it)
                    for dc in range(16):
                        mm = t.matmul(pg[0:96, :], wg[:, dc * 96:(dc + 1) * 96], xv[:, dc, :],
                                      start=(dc == 0), stop=(dc == 15))
                    mm.then_inc(s_pg.sem, 1); bump(s_pg, 1)
                    t.wait_ge(s_sg.sem, it + 1)
                    for tt in range(4):
                        n = 4 * it + tt
                        if n >= 1:
                            t.wait_ge(s_eg.sem, n)
                        t.transpose(pgt[:, 0:96], gst[0:96, tt * 128:(tt + 1) * 128],
                                    identF_sb[0:96, 0:96]).then_inc(s_tg.sem, 1); bump(s_tg, 1)

            @blk.vector
            def _(ve):
                for it in range(NT):
                    for oc in range(OC):
                        n = it * OC + oc
                        ve.wait_ge(s_pe.sem, n + 1)
                        if n >= 4:
                            ve.wait_ge(s_st[n % 4].sem, 16 * (n // 4))
                        ve.tensor_copy(stg[n % 4][:, :], pq[n % 2][:, :]).then_inc(s_ev.sem, 1); bump(s_ev, 1)
                    for tt in range(4):
                        n = 4 * it + tt
                        ve.wait_ge(s_tg.sem, n + 1)
                        if it >= 1 and tt == 0:
                            ve.wait_ge(s_gs.sem, 16 * it)
                        ve.tensor_copy(gtt[:, tt * 96:(tt + 1) * 96], pgt[:, 0:96]).then_inc(s_eg.sem, 1); bump(s_eg, 1)

            @blk.scalar
            def _(s):
                for it in range(NT):
                    s.wait_ge(s_pg.sem, it + 1)
                    if it >= 1:
                        s.wait_ge(s_tg.sem, 4 * it)
                    s.activation(gst[0:96, :], pg[0:96, :], AF.Sigmoid,
                                 bias=bg[0:96, 0:1]).then_inc(s_sg.sem, 1); bump(s_sg, 1)

        full_barrier()

    if "rope" in stages:
        with ExitStack() as sc:
            bf = [sc.enter_context(nc.sbuf_tensor(f"ro_x{i}", [128, S], F32)) for i in range(2)]
            t1 = sc.enter_context(nc.sbuf_tensor("ro_t1", [128, S], F32))
            t2 = sc.enter_context(nc.sbuf_tensor("ro_t2", [128, S], F32))
            ob = [sc.enter_context(nc.sbuf_tensor(f"ro_o{i}", [128, S], BF16)) for i in range(2)]
            kpb = sc.enter_context(nc.sbuf_tensor("ro_kp", [128, CBLK], F32))
            pr = [P0, P1]
            s_ld = [sem("ro_ld0"), sem("ro_ld1")]
            s_pe = sem("ro_pe")
            s_t2 = sem("ro_t2")
            s_dv = sem("ro_dv")
            s_st = [sem("ro_st0"), sem("ro_st1")]
            s_kp = sem("ro_kp")

            bands = []
            for seq in range(NB):
                for bd in range(16):
                    bands.append((0, seq, bd))
                for bd in range(4):
                    bands.append((1, seq, bd))
                for bd in range(4):
                    bands.append((2, seq, bd))
                for bd in range(4):
                    bands.append((3, seq, bd))

            def src_of(kind, seq, bd):
                return (qF if kind == 0 else kF if kind in (1, 2) else vF)[seq, bd, :, :]

            @blk.sync
            def _(sy):
                for i in range(min(2, len(bands))):
                    kind, seq, bd = bands[i]
                    dma(sy, bf[i % 2][:, :], src_of(kind, seq, bd), s_ld[i % 2])
                for i, (kind, seq, bd) in enumerate(bands):
                    sy.wait_ge(s_dv.sem, i + 1)
                    dst = (rqT, rkT, kbT, vbT)[kind]
                    dma(sy, dst[seq, bd, :, :], ob[i % 2][:, :], s_st[i % 2])
                    if i + 2 < len(bands):
                        kind2, seq2, bd2 = bands[i + 2]
                        dma(sy, bf[i % 2][:, :], src_of(kind2, seq2, bd2), s_ld[i % 2])
                sy.wait_ge(s_st[0].sem, s_st[0].val)
                sy.wait_ge(s_st[1].sem, s_st[1].val)

            @blk.gpsimd
            def _(g):
                kpi = 0
                for i, (kind, seq, bd) in enumerate(bands):
                    if kind in (2, 3):
                        if kpi >= 1:
                            g.wait_ge(s_dv.sem, prev_kp + 1)
                        src = kposT if kind == 2 else vposT
                        dma(g, kpb[0:64, :], src[2 * bd, :, :], s_kp)
                        dma(g, kpb[64:128, :], src[2 * bd + 1, :, :], s_kp)
                        kpi += 1
                        prev_kp = i

            @blk.tensor
            def _(t):
                nrot = 0
                for i, (kind, seq, bd) in enumerate(bands):
                    if kind in (0, 1):
                        t.wait_ge(s_ld[i % 2].sem, 16 * (i // 2 + 1))
                        for c in range(4):
                            if nrot >= 2:
                                t.wait_ge(s_t2.sem, nrot - 1)
                            t.matmul(pr[nrot % 2][:, :], cst["R2T"][:, :],
                                     bf[i % 2][:, c * 512:(c + 1) * 512], start=True,
                                     stop=True).then_inc(s_pe.sem, 1); bump(s_pe, 1)
                            nrot += 1

            @blk.vector
            def _(ve):
                nrot = 0
                nkp = 0
                for i, (kind, seq, bd) in enumerate(bands):
                    bi = i % 2
                    ve.wait_ge(s_ld[i % 2].sem, 16 * (i // 2 + 1))
                    if i >= 1:
                        ve.wait_ge(s_dv.sem, i)
                    if i >= 2:
                        ve.wait_ge(s_st[i % 2].sem, 16 * (i // 2))
                    if kind in (0, 1):
                        ve.tensor_mul(t1[:, :], bf[bi][:, :], cst["C2"][:, :])
                        for c in range(4):
                            ve.wait_ge(s_pe.sem, nrot + 1)
                            ve.tensor_mul(t2[:, c * 512:(c + 1) * 512], pr[nrot % 2][:, :],
                                          cst["S2"][:, c * 512:(c + 1) * 512]).then_inc(s_t2.sem, 1); bump(s_t2, 1)
                            nrot += 1
                        ve.wait_ge(s_t2.sem, nrot)
                        ve.tensor_add(ob[bi][:, :], t1[:, :], t2[:, :]).then_inc(s_dv.sem, 1); bump(s_dv, 1)
                    else:
                        nkp += 32
                        ve.wait_ge(s_kp.sem, nkp)
                        ve.tensor_add(ob[bi][:, :].rearrange("p (c blk) -> p c blk", c=CBLK),
                                      bf[bi][:, :].rearrange("p (blk c) -> p c blk", c=CBLK),
                                      kpb[:, :].unsqueeze(2).to_broadcast([128, CBLK, NBLK])
                                      ).then_inc(s_dv.sem, 1); bump(s_dv, 1)

        full_barrier()

    s_ck = sem("cm_ck")
    s_km = sem("cm_km")
    if "cmp" in stages:
        with ExitStack() as sc:
            cw1 = {}
            cw2 = {}
            cb1 = {}
            cb2 = {}
            for t_, w1, w2 in [("k", k_cw1, k_cw2), ("v", v_cw1, v_cw2)]:
                cw1[t_] = sc.enter_context(nc.sbuf_tensor(f"cm_w1{t_}", [128, 8 * 1024], BF16))
                cw2[t_] = sc.enter_context(nc.sbuf_tensor(f"cm_w2{t_}", [128, 8 * 64], BF16))
                cb1[t_] = sc.enter_context(nc.sbuf_tensor(f"cm_b1{t_}", [128, 8], F32))
                cb2[t_] = sc.enter_context(nc.sbuf_tensor(f"cm_b2{t_}", [64, 1], F32))
            kbt = sc.enter_context(nc.sbuf_tensor("cm_kb", [128, 8 * 512], BF16))
            vbt = sc.enter_context(nc.sbuf_tensor("cm_vb", [128, 8 * 512], BF16))
            h1 = sc.enter_context(nc.sbuf_tensor("cm_h1", [128, 8 * 512], BF16))
            ph = [P0, P1]
            pc = P2
            ptk = [PIM, PTR]
            s_w = sem("cm_w")
            s_ldg = sem("cm_ld")
            s_mm = sem("cm_mm")
            s_h1 = sem("cm_h1")
            s_pc = sem("cm_pc")
            s_tp = sem("cm_tp")
            NG = 4

            @blk.gpsimd
            def _(g):
                for t_, w1, w2, b1, b2 in [("k", k_cw1, k_cw2, k_cb1, k_cb2),
                                           ("v", v_cw1, v_cw2, v_cb1, v_cb2)]:
                    for uu in range(8):
                        dma(g, cw1[t_][:, uu * 1024:(uu + 1) * 1024],
                            w1[uu * 128:(uu + 1) * 128, :], s_w)
                    dma(g, cw2[t_][:, :].rearrange("p (mc m) -> p mc m", mc=8),
                        w2[:, :].rearrange("(mc p) m -> p mc m", p=128), s_w)
                    g.dma_start(cb1[t_][:, :].unsqueeze(2),
                                b1[:, :].rearrange("(mc p) o -> p mc o", p=128),
                                allow_slow_non_contiguous=True).then_inc(s_w.sem, 16); bump(s_w, 16)
                    dma(g, cb2[t_][:, :], b2[:, :], s_w)

            @blk.sync
            def _(sy):
                for g in range(NG):
                    if g >= 1:
                        sy.wait_ge(s_h1.sem, 16 * g)
                    b = (4 * g) // 8
                    band0 = ((4 * g) % 8) // 2
                    for j4 in range(4):
                        band = band0 + j4 // 2
                        r0 = (j4 % 2) * 64
                        for par in range(2):
                            ksrc = kbT[b, band, r0:r0 + 64, :].rearrange(
                                "p (u par blk) -> p par u blk", par=2, blk=NBLK)[:, par, :, :]
                            vsrc = vbT[b, band, r0:r0 + 64, :].rearrange(
                                "p (u par blk) -> p par u blk", par=2, blk=NBLK)[:, par, :, :]
                            kdst = kbt[par * 64:par * 64 + 64, :].rearrange(
                                "p (u j4 blk) -> p u j4 blk", u=8, j4=4)[:, :, j4, :]
                            vdst = vbt[par * 64:par * 64 + 64, :].rearrange(
                                "p (u j4 blk) -> p u j4 blk", u=8, j4=4)[:, :, j4, :]
                            dma(sy, kdst, ksrc, s_ldg)
                            dma(sy, vdst, vsrc, s_ldg)

            def kb_mov(t_, u):
                tile = kbt if t_ == "k" else vbt
                return tile[:, u * 512:(u + 1) * 512]

            @blk.tensor
            def _(t):
                t.wait_ge(s_w.sem, s_w.val)
                for g in range(NG):
                    t.wait_ge(s_ldg.sem, 256 * (g + 1))
                    for ti, t_ in enumerate(("k", "v")):
                        for mc in range(8):
                            n = 16 * g + 8 * ti + mc
                            if n >= 2:
                                t.wait_ge(s_h1.sem, n - 1)
                            for u in range(8):
                                w1ap = cw1[t_][:, u * 1024 + mc * 128:u * 1024 + (mc + 1) * 128]
                                mm = t.matmul(ph[n % 2][:, :], w1ap, kb_mov(t_, u),
                                              start=(u == 0), stop=(u == 7))
                            mm.then_inc(s_mm.sem, 1); bump(s_mm, 1)
                        t.wait_ge(s_h1.sem, 16 * g + 8 * ti + 8)
                        for mc in range(8):
                            mm = t.matmul(pc[0:64, :], cw2[t_][:, mc * 64:(mc + 1) * 64],
                                          h1[:, mc * 512:(mc + 1) * 512],
                                          start=(mc == 0), stop=(mc == 7))
                        mm.then_inc(s_pc.sem, 1); bump(s_pc, 1)
                        if not (g == NG - 1 and ti == 1):
                            t.wait_ge(s_ck.sem, 2 * g + ti + 1)
                    t.wait_ge(s_ck.sem, 2 * g + 2)
                    for j4 in range(4):
                        n = 4 * g + j4
                        if n >= 2:
                            t.wait_ge(s_km.sem, n - 1)
                        bh = 4 * g + j4
                        t.transpose(ptk[n % 2][:, 0:64], cvT[:, bh * 128:(bh + 1) * 128],
                                    identF_sb[0:64, 0:64]).then_inc(s_tp.sem, 1); bump(s_tp, 1)

            @blk.scalar
            def _(s):
                for g in range(NG):
                    for ti, t_ in enumerate(("k", "v")):
                        for mc in range(8):
                            n = 16 * g + 8 * ti + mc
                            s.wait_ge(s_mm.sem, n + 1)
                            s.activation(h1[:, mc * 512:(mc + 1) * 512], ph[n % 2][:, :],
                                         AF.Relu, bias=cb1[t_][:, mc:mc + 1]).then_inc(s_h1.sem, 1); bump(s_h1, 1)
                        s.wait_ge(s_pc.sem, 2 * g + ti + 1)
                        dstT = ckT if t_ == "k" else cvT
                        s.activation(dstT[:, g * 512:(g + 1) * 512], pc[0:64, :], AF.Identity,
                                     bias=cb2[t_][:, 0:1]).then_inc(s_ck.sem, 1); bump(s_ck, 1)

            @blk.vector
            def _(ve):
                ve.memset(cvKM[:, :].rearrange("p (bh o) -> p bh o", bh=NB * KVH)[:, :, 64:65], 1.0)
                for g in range(NG):
                    for j4 in range(4):
                        n = 4 * g + j4
                        ve.wait_ge(s_tp.sem, n + 1)
                        bh = 4 * g + j4
                        ve.tensor_copy(cvKM[:, bh * 65:bh * 65 + 64], ptk[n % 2][:, 0:64]
                                       ).then_inc(s_km.sem, 1); bump(s_km, 1)

            if "ckD" in dbg or "cvD" in dbg:
                @blk.gpsimd
                def _(g):
                    g.wait_ge(s_ck.sem, 8)
                    dma(g, ckD[:, :], ckT[:, :], s_w)
                    dma(g, cvD[:, :], cvT[:, :], s_w)
                    g.wait_ge(s_w.sem, s_w.val)

        full_barrier()

    if "attn" in stages:
        with ExitStack() as sc:
            rq4 = [sc.enter_context(nc.sbuf_tensor(f"at_rq{i}", [64, 4 * S], BF16)) for i in range(2)]
            q4 = [sc.enter_context(nc.sbuf_tensor(f"at_q{i}", [64, 4 * S], BF16)) for i in range(2)]
            rkh = [sc.enter_context(nc.sbuf_tensor(f"at_rk{i}", [64, S], BF16)) for i in range(2)]
            vth = [sc.enter_context(nc.sbuf_tensor("at_vt0", [64, S], F32))] * 2
            vKM = sc.enter_context(nc.sbuf_tensor("at_vkm", [128, 16 * 65], BF16))
            vKM2 = sc.enter_context(nc.sbuf_tensor("at_vkm2", [64, 32 * 65], BF16))
            impsum = sc.enter_context(nc.sbuf_tensor("at_imps", [128, S], F32))
            esh = impsum
            impq = sc.enter_context(nc.sbuf_tensor("at_impq", [128, S], F32))
            top8 = sc.enter_context(nc.sbuf_tensor("at_top8", [128, 16 * 8], F32))
            zr = sc.enter_context(nc.sbuf_tensor("at_zr", [128, 6 * 16], F32))
            BMq = sc.enter_context(nc.sbuf_tensor("at_bmq", [128, S], F32))
            BMT1 = sc.enter_context(nc.sbuf_tensor("at_bmt1", [128, S], BF16))
            pf = sc.enter_context(nc.sbuf_tensor("at_pf", [128, 8 * 512], BF16))
            pc_sb = sc.enter_context(nc.sbuf_tensor("at_pc", [128, 512], BF16))
            pm_sb = sc.enter_context(nc.sbuf_tensor("at_pm", [1, 512], BF16))
            psl = sc.enter_context(nc.sbuf_tensor("at_psl", [64, 512], BF16))
            pso = sc.enter_context(nc.sbuf_tensor("at_pso", [64, 512], BF16))
            cnum = sc.enter_context(nc.sbuf_tensor("at_cn", [65, 512], F32))
            fnum = sc.enter_context(nc.sbuf_tensor("at_fn", [65, 512], F32))
            snum = sc.enter_context(nc.sbuf_tensor("at_sn", [65, 512], F32))
            zrec = sc.enter_context(nc.sbuf_tensor("at_zrec", [128, 12], F32))
            mbr = sc.enter_context(nc.sbuf_tensor("at_mbr", [128, 12], F32))
            acc = sc.enter_context(nc.sbuf_tensor("at_acc", [128, 256], F32))
            tmpc = sc.enter_context(nc.sbuf_tensor("at_tmpc", [128, 256], F32))
            gtile = sc.enter_context(nc.sbuf_tensor("at_gt", [128, 16 * 96], F32))
            asm = [sc.enter_context(nc.sbuf_tensor(f"at_asm{i}", [64, S], BF16)) for i in range(4)]
            pcs = [P0, P1]
            pfs = [P2, P3]
            pbr = P4
            ptr = PTR
            pim = PIM
            pmem = PMEM

            names = ["a_ld0", "a_ld1", "a_lq0", "a_lq1", "a_gt", "a_tpv", "a_evv", "a_cs",
                     "a_imp", "a_msk", "a_imtp", "a_imev", "a_bm", "a_bmtp", "a_bmev", "a_e",
                     "a_c1", "a_cm", "a_xc", "a_xm", "a_co", "a_bev", "a_fs", "a_xf", "a_fv",
                     "a_fo", "a_ss", "a_xs", "a_so", "a_tr", "a_cb", "a_tb", "a_te", "a_st",
                     "a_ms", "a_lsh", "a_mpr", "a_es", "a_es2", "a_zv", "a_es3", "a_bd", "a_wdb"]
            sm = {n: sem(n) for n in names}
            NBH = NB * KVH

            def bh_of(i):
                return i // 8, i % 8

            def ldw(bh):
                return 16 * 6 * (bh // 2 + 1)

            def ss_base(c):
                full, j = divmod(c, 16)
                tot = full * 252
                for jj in range(j):
                    tot += 15 if jj // 4 == 0 else 16
                return tot

            def ss_nprev(c):
                return 7 if (c % 16) // 4 == 0 else 8

            @blk.sync
            def _(sy):
                for bh in range(NBH):
                    b, h = bh_of(bh)
                    p = bh % 2
                    if bh >= 2:
                        sy.wait_ge(sm["a_te"].sem, 16 * (bh - 1))
                    for g in range(G):
                        band = 2 * h + g // 2
                        r0 = (g % 2) * 64
                        dma(sy, rq4[p][:, g * S:(g + 1) * S], rqT[b, band, r0:r0 + 64, :], sm[f"a_ld{p}"])
                    dma(sy, rkh[p][:, :], rkT[b, h // 2, (h % 2) * 64:(h % 2) * 64 + 64, :], sm[f"a_ld{p}"])
                    if bh >= 1:
                        sy.wait_ge(sm["a_tpv"].sem, 48 * bh)
                    dma(sy, vth[p][:, :], vF[b, h // 2, (h % 2) * 64:(h % 2) * 64 + 64, :], sm[f"a_ld{p}"])
                    if bh >= 2:
                        bp, hp = bh_of(bh - 2)
                        sy.wait_ge(sm["a_te"].sem, 16 * (bh - 1))
                        for g in range(G):
                            dma(sy, attnFM[bp, 4 * hp + g, :, :], asm[g][:, :], sm["a_st"])
                    if h == 0:
                        if bh >= 2:
                            sy.wait_ge(sm["a_cb"].sem, 32 * bh)
                        dma(sy, gtile[:, :].rearrange("p (qt g) -> p qt g", qt=16),
                            gatesT[b, :, :, :].transpose([1, 0, 2]), sm["a_gt"])
                for bh in (NBH - 2, NBH - 1):
                    bp, hp = bh_of(bh)
                    sy.wait_ge(sm["a_te"].sem, 16 * (bh + 1))
                    for g in range(G):
                        dma(sy, attnFM[bp, 4 * hp + g, :, :], asm[g][:, :], sm["a_st"])
                sy.wait_ge(sm["a_st"].sem, sm["a_st"].val)

            @blk.gpsimd
            def _(g_):
                for bh in range(NBH):
                    b, h = bh_of(bh)
                    p = bh % 2
                    if bh >= 2:
                        g_.wait_ge(sm["a_cs"].sem, 16 * (bh - 1))
                        g_.wait_ge(sm["a_c1"].sem, 16 * (bh - 1))
                        g_.wait_ge(sm["a_cm"].sem, 16 * (bh - 1))
                    for g in range(G):
                        band = 2 * h + g // 2
                        r0 = (g % 2) * 64
                        dma(g_, q4[p][:, g * S:(g + 1) * S], qF[b, band, r0:r0 + 64, :], sm[f"a_lq{p}"])
                    if 4 <= bh < 12:
                        ch = bh - 4
                        dma(g_, wdB[ch * 1024:(ch + 1) * 1024, :],
                            ffn_wdT[ch * 1024:(ch + 1) * 1024, :], sm["a_wdb"])
                    if "bmD" in dbg:
                        g_.wait_ge(sm["a_bm"].sem, bh + 1)
                        dma(g_, bmD[b, h, :, :, :].transpose([1, 0, 2]),
                            BMq[:, :].rearrange("p (qt blk) -> p qt blk", qt=16), sm["a_bd"])

            @blk.tensor
            def _(t):
                def flush_tr_r(cp, r):
                    if r == 0:
                        t.wait_ge(sm["a_bev"].sem, 3 * (cp + 1))
                    t.wait_ge(sm["a_cb"].sem, 2 * cp + r)
                    for q2 in range(2):
                        qt2 = 2 * r + q2
                        for br, buf in enumerate((cnum, fnum, snum)):
                            t.transpose(ptr[:, (q2 * 3 + br) * 65:(q2 * 3 + br + 1) * 65],
                                        buf[0:65, qt2 * 128:(qt2 + 1) * 128],
                                        identF_sb[0:65, 0:65]).then_inc(sm["a_tr"].sem, 1); bump(sm["a_tr"], 1)

                def flush_tr(cp):
                    flush_tr_r(cp, 0)
                    flush_tr_r(cp, 1)

                def flush_tb(cp):
                    t.wait_ge(sm["a_cb"].sem, 2 * (cp + 1))
                    t.wait_ge(sm["a_xc"].sem, cp + 1)
                    for qt2 in range(4):
                        t.transpose(pcs[cp % 2][0:64, qt2 * 128:(qt2 + 1) * 128],
                                    acc[:, qt2 * 64:(qt2 + 1) * 64],
                                    identF_sb[:, :]).then_inc(sm["a_tb"].sem, 1); bump(sm["a_tb"], 1)

                def emit_T3(bh):
                    b, h = bh_of(bh)
                    p = bh % 2
                    t.wait_ge(sm["a_evv"].sem, 48 * (bh + 1))
                    t.wait_ge(sm["a_imp"].sem, min(16 * (bh + 2), 16 * NBH))
                    t.wait_ge(sm["a_bmev"].sem, 16 * (bh + 1))
                    for j in range(16):
                        qc, g = divmod(j, 4)
                        c = 16 * bh + j
                        if j > 0:
                            flush_tr_r(c - 1, 0)
                        qsl = q4[p][:, g * S + qc * 512:g * S + (qc + 1) * 512]
                        rqsl = rq4[p][:, g * S + qc * 512:g * S + (qc + 1) * 512]
                        if j >= 2:
                            t.wait_ge(sm["a_xc"].sem, c - 1)
                            t.wait_ge(sm["a_te"].sem, c - 1)
                        t.matmul(pcs[c % 2][:, :], ckT[:, bh * 128:(bh + 1) * 128], qsl,
                                 start=True, stop=False)
                        t.matmul(pcs[c % 2][:, :], identB[:, :],
                                 cst["cmaskADD"][:, qc * 512:(qc + 1) * 512],
                                 start=False, stop=True).then_inc(sm["a_c1"].sem, 1); bump(sm["a_c1"], 1)
                        if c >= 1:
                            t.wait_ge(sm["a_xm"].sem, c)
                        t.matmul(pmem[0:1, :], memck_sb[:, h:h + 1], qsl,
                                 start=True, stop=True).then_inc(sm["a_cm"].sem, 1); bump(sm["a_cm"], 1)
                        for kc in range(16):
                            n = 16 * c + kc
                            if n >= 2:
                                t.wait_ge(sm["a_xf"].sem, n - 1)
                            if kc < 2 and c >= 1:
                                t.wait_ge(sm["a_so"].sem, 8 * c)
                            dg = kc - 4 * qc
                            diag = dg if 0 <= dg < 4 else None
                            t.matmul(pfs[kc % 2][:, :], rkh[p][:, kc * 128:(kc + 1) * 128],
                                     rqsl, start=True, stop=False)
                            mmE = t.matmul(pfs[kc % 2][:, :], cst["EBIG"][:, kc * 128:(kc + 1) * 128],
                                           BMT1[:, qc * 512:(qc + 1) * 512],
                                           start=False, stop=(diag is None))
                            if diag is not None:
                                mmE = t.matmul(pfs[kc % 2][:, diag * 128:(diag + 1) * 128],
                                               identB[:, :], cst["ownADD"][:, :],
                                               start=False, stop=True)
                            mmE.then_inc(sm["a_fs"].sem, 1); bump(sm["a_fs"], 1)
                            if kc == 1:
                                if j > 0:
                                    flush_tr_r(c - 1, 1)
                                if c >= 1:
                                    t.wait_ge(sm["a_bev"].sem, 3 * c)
                                t.wait_ge(sm["a_xm"].sem, c + 1)
                                t.matmul(pbr[0:65, :], memcv_st[0:1, h * 65:(h + 1) * 65],
                                         pm_sb[0:1, :], start=True, stop=False)
                                t.wait_ge(sm["a_xc"].sem, c + 1)
                                t.matmul(pbr[0:65, :], cvKM[:, bh * 65:(bh + 1) * 65], pc_sb[:, :],
                                         start=False, stop=True).then_inc(sm["a_co"].sem, 1); bump(sm["a_co"], 1)
                            if kc >= 2:
                                nf = kc - 2
                                if nf == 0:
                                    t.wait_ge(sm["a_bev"].sem, 3 * c + 1)
                                t.wait_ge(sm["a_xf"].sem, 16 * c + nf + 1)
                                t.matmul(pbr[0:65, :], vKM[:, nf * 65:(nf + 1) * 65],
                                         pf[:, (nf % 8) * 512:(nf % 8 + 1) * 512],
                                         start=(nf == 0), stop=False).then_inc(sm["a_fv"].sem, 1); bump(sm["a_fv"], 1)
                        for nf in (14, 15):
                            t.wait_ge(sm["a_xf"].sem, 16 * c + nf + 1)
                            t.matmul(pbr[0:65, :], vKM[:, nf * 65:(nf + 1) * 65],
                                     pf[:, (nf % 8) * 512:(nf % 8 + 1) * 512],
                                     start=False, stop=(nf == 15)).then_inc(sm["a_fv"].sem, 1); bump(sm["a_fv"], 1)
                        t.nop().then_inc(sm["a_fo"].sem, 1); bump(sm["a_fo"], 1)
                        t.wait_ge(sm["a_xf"].sem, 16 * (c + 1))
                        for w8 in range(8):
                            w = qc * 8 + w8
                            if w == 0:
                                continue
                            t.matmul(pfs[0][0:64, w8 * 64:(w8 + 1) * 64],
                                     rkh[p][:, (w - 1) * 64:w * 64],
                                     rq4[p][:, g * S + w * 64:g * S + (w + 1) * 64],
                                     start=True, stop=False)
                            t.matmul(pfs[0][0:64, w8 * 64:(w8 + 1) * 64],
                                     identB[0:64, 0:64], cst["sADDp"][:, :],
                                     start=False, stop=True).then_inc(sm["a_ss"].sem, 1); bump(sm["a_ss"], 1)
                        for w8 in range(8):
                            w = qc * 8 + w8
                            t.matmul(pfs[1][0:64, w8 * 64:(w8 + 1) * 64],
                                     rkh[p][:, w * 64:(w + 1) * 64],
                                     rq4[p][:, g * S + w * 64:g * S + (w + 1) * 64],
                                     start=True, stop=False)
                            t.matmul(pfs[1][0:64, w8 * 64:(w8 + 1) * 64],
                                     identB[0:64, 0:64], cst["sADDo"][:, :],
                                     start=False, stop=True).then_inc(sm["a_ss"].sem, 1); bump(sm["a_ss"], 1)
                        t.wait_ge(sm["a_bev"].sem, 3 * c + 2)
                        t.wait_ge(sm["a_xs"].sem, 2 * (c + 1))
                        for w8 in range(8):
                            w = qc * 8 + w8
                            if w > 0:
                                t.matmul(pbr[0:65, w8 * 64:(w8 + 1) * 64],
                                         vKM2[:, (w - 1) * 65:(w - 1) * 65 + 65],
                                         psl[:, w8 * 64:(w8 + 1) * 64], start=True, stop=False)
                            t.matmul(pbr[0:65, w8 * 64:(w8 + 1) * 64],
                                     vKM2[:, w * 65:w * 65 + 65],
                                     pso[:, w8 * 64:(w8 + 1) * 64],
                                     start=(w == 0), stop=True).then_inc(sm["a_so"].sem, 1); bump(sm["a_so"], 1)
                        for dd in range(2):
                            t.matmul(pfs[0][0:128, dd * 128:(dd + 1) * 128],
                                     cst["EBIG"][:, (2 * qc + dd) * 128:(2 * qc + dd + 1) * 128],
                                     BMT1[:, qc * 512:qc * 512 + 128],
                                     start=True, stop=True)
                        if c >= 1 and j > 0:
                            flush_tb(c - 1)

                t.wait_ge(s_ck.sem, s_ck.val)
                t.wait_ge(s_km.sem, s_km.val)
                t.wait_ge(sm["a_ms"].sem, 2)
                for bh in range(NBH + 1):
                    b, h = bh_of(bh) if bh < NBH else (0, 0)
                    p = bh % 2
                    if bh < NBH:
                        t.wait_ge(sm[f"a_lq{p}"].sem, 64 * (bh // 2 + 1))
                        pA = [pcs[0], pcs[1], pfs[0], pfs[1]]
                        for j in range(16):
                            qc, g = divmod(j, 4)
                            n = 16 * bh + j
                            if n >= 4:
                                t.wait_ge(sm["a_imp"].sem, n - 3)
                            if bh >= 2 and j == 0:
                                t.wait_ge(sm["a_xc"].sem, 16 * (bh - 1))
                                t.wait_ge(sm["a_te"].sem, 16 * (bh - 1))
                                t.wait_ge(sm["a_xf"].sem, 256 * (bh - 1))
                                t.wait_ge(sm["a_xs"].sem, 32 * (bh - 1))
                            t.matmul(pA[j % 4][:, :], ckT[:, bh * 128:(bh + 1) * 128],
                                     q4[p][:, g * S + qc * 512:g * S + (qc + 1) * 512],
                                     start=True, stop=False)
                            t.matmul(pA[j % 4][:, :], identB[:, :],
                                     cst["cmaskADD"][:, qc * 512:(qc + 1) * 512],
                                     start=False, stop=True).then_inc(sm["a_cs"].sem, 1); bump(sm["a_cs"], 1)
                    if bh >= 1:
                        emit_T3(bh - 1)
                        flush_tr(16 * bh - 1)
                        flush_tb(16 * bh - 1)
                    if bh < NBH:
                        t.wait_ge(sm[f"a_ld{p}"].sem, ldw(bh))
                        if bh >= 1:
                            t.wait_ge(sm["a_cb"].sem, 32 * bh)
                            t.wait_ge(sm["a_bmev"].sem, 16 * bh)
                        for n in range(48):
                            dst = pim if n % 2 == 0 else ptr
                            if n >= 2:
                                t.wait_ge(sm["a_evv"].sem, 48 * bh + n - 1)
                            if n < 16:
                                t.transpose(dst[:, 0:64], vth[p][:, n * 128:(n + 1) * 128],
                                            identF_sb[0:64, 0:64]).then_inc(sm["a_tpv"].sem, 1); bump(sm["a_tpv"], 1)
                            else:
                                w = n - 16
                                t.transpose(dst[0:64, 0:64], vth[p][:, w * 64:(w + 1) * 64],
                                            identF_sb[0:64, 0:64]).then_inc(sm["a_tpv"].sem, 1); bump(sm["a_tpv"], 1)
                        t.wait_ge(sm["a_msk"].sem, bh + 1)
                        t.wait_ge(sm["a_evv"].sem, 48 * (bh + 1))
                        for qt in range(16):
                            n = 16 * bh + qt
                            dst = pim[:, 0:128] if qt % 2 == 0 else ptr[:, 0:128]
                            if qt > 1:
                                t.wait_ge(sm["a_imev"].sem, n - 1)
                            t.transpose(dst, impsum[:, qt * 128:(qt + 1) * 128],
                                        identF_sb[:, :]).then_inc(sm["a_imtp"].sem, 1); bump(sm["a_imtp"], 1)
                        t.wait_ge(sm["a_bm"].sem, bh + 1)
                        t.wait_ge(sm["a_imev"].sem, 16 * (bh + 1))
                        for qt in range(16):
                            n = 16 * bh + qt
                            dst = pim[:, 0:128] if qt % 2 == 0 else ptr[:, 0:128]
                            if qt > 1:
                                t.wait_ge(sm["a_bmev"].sem, n - 1)
                            t.transpose(dst, BMq[:, qt * 128:(qt + 1) * 128],
                                        identF_sb[:, :]).then_inc(sm["a_bmtp"].sem, 1); bump(sm["a_bmtp"], 1)

            @blk.scalar
            def _(s):
                zrv = zr[:, :].rearrange("p (k qt) -> p k qt", k=6)
                for bh in range(NBH + 1):
                    if bh >= 1:
                        bhp = bh - 1
                        hh = bhp % 8
                        for j in range(16):
                            qc, g = divmod(j, 4)
                            c = 16 * bhp + j
                            s.wait_ge(sm["a_c1"].sem, c + 1)
                            if c >= 1:
                                s.wait_ge(sm["a_co"].sem, c)
                            s.activation(pc_sb[:, :], pcs[c % 2][:, :], AF.Exp,
                                         scale=SCALE).then_inc(sm["a_xc"].sem, 1); bump(sm["a_xc"], 1)
                            s.wait_ge(sm["a_cm"].sem, c + 1)
                            s.activation(pm_sb[0:1, :], pmem[0:1, :], AF.Exp,
                                         scale=SCALE).then_inc(sm["a_xm"].sem, 1); bump(sm["a_xm"], 1)
                            for kc in range(16):
                                n = 16 * c + kc
                                s.wait_ge(sm["a_fs"].sem, n + 1)
                                if c >= 1 or kc >= 8:
                                    s.wait_ge(sm["a_fv"].sem, 16 * c + kc - 7)
                                s.activation(pf[:, (kc % 8) * 512:(kc % 8 + 1) * 512],
                                             pfs[kc % 2][:, :],
                                             AF.Exp, scale=SCALE).then_inc(sm["a_xf"].sem, 1); bump(sm["a_xf"], 1)
                            s.wait_ge(sm["a_ss"].sem, ss_base(c) + ss_nprev(c))
                            if c >= 1:
                                s.wait_ge(sm["a_so"].sem, 8 * c)
                            if qc == 0:
                                s.activation(psl[:, 64:512], pfs[0][0:64, 64:512], AF.Exp,
                                             scale=SCALE).then_inc(sm["a_xs"].sem, 1); bump(sm["a_xs"], 1)
                            else:
                                s.activation(psl[:, :], pfs[0][0:64, :], AF.Exp,
                                             scale=SCALE).then_inc(sm["a_xs"].sem, 1); bump(sm["a_xs"], 1)
                            s.wait_ge(sm["a_ss"].sem, ss_base(c) + ss_nprev(c) + 8)
                            s.activation(pso[:, :], pfs[1][0:64, :], AF.Exp,
                                         scale=SCALE).then_inc(sm["a_xs"].sem, 1); bump(sm["a_xs"], 1)
                    if bh < NBH:
                        s.wait_ge(sm["a_lsh"].sem, bh + 1)
                        s.activation(esh[:, :], esh[:, :], AF.Exp).then_inc(sm["a_es"].sem, 1); bump(sm["a_es"], 1)
                        s.wait_ge(sm["a_mpr"].sem, bh + 1)
                        s.activation(zrv[:, 2, :], zrv[:, 1, :], AF.Exp,
                                     bias=smallc[:, 1:2]).then_inc(sm["a_es2"].sem, 1); bump(sm["a_es2"], 1)
                        s.wait_ge(sm["a_zv"].sem, bh + 1)
                        s.activation(zrv[:, 4, :], zrv[:, 3, :], AF.Ln).then_inc(sm["a_es3"].sem, 1); bump(sm["a_es3"], 1)

            @blk.vector
            def _(ve):
                ve.memset(vKM[:, :].rearrange("p (kc o) -> p kc o", kc=16)[:, :, 64:65],
                          1.0).then_inc(sm["a_ms"].sem, 1); bump(sm["a_ms"], 1)
                ve.memset(vKM2[:, :].rearrange("p (w o) -> p w o", w=32)[:, :, 64:65],
                          1.0).then_inc(sm["a_ms"].sem, 1); bump(sm["a_ms"], 1)
                ec = [0]

                def edge(ins):
                    ins.then_inc(sm["a_e"].sem, 1)
                    ec[0] += 1
                    ve.wait_ge(sm["a_e"].sem, ec[0])
                    return ins

                def tback_evac(cp):
                    bhp, jp = divmod(cp, 16)
                    qcp, gp = divmod(jp, 4)
                    ve.wait_ge(sm["a_tb"].sem, 4 * (cp + 1))
                    if jp == 0 and bhp >= 1:
                        ve.wait_ge(sm["a_st"].sem, 64 * bhp)
                    ve.tensor_copy(asm[gp][:, qcp * 512:(qcp + 1) * 512],
                                   pcs[cp % 2][0:64, :]).then_inc(sm["a_te"].sem, 1); bump(sm["a_te"], 1)

                for bh in range(NBH + 1):
                    b, h = bh_of(bh) if bh < NBH else (0, 0)
                    if bh >= 2:
                        tback_evac(16 * (bh - 1) - 1)
                    if bh < NBH:
                        for j in range(16):
                            qc, g = divmod(j, 4)
                            n = 16 * bh + j
                            if j == 0 and bh >= 1:
                                ve.wait_ge(sm["a_imtp"].sem, 16 * bh)
                                ve.wait_ge(sm["a_es"].sem, bh)
                            ve.wait_ge(sm["a_cs"].sem, n + 1)
                            if g >= 1:
                                ve.wait_ge(sm["a_imp"].sem, n)
                            sl = impsum[:, qc * 512:(qc + 1) * 512]
                            pAv = [pcs[0], pcs[1], pfs[0], pfs[1]][j % 4]
                            if g == 0:
                                ve.tensor_copy(sl, pAv[:, :]).then_inc(sm["a_imp"].sem, 1); bump(sm["a_imp"], 1)
                            else:
                                ve.tensor_add(sl, sl, pAv[:, :]).then_inc(sm["a_imp"].sem, 1); bump(sm["a_imp"], 1)
                        ve.wait_ge(sm["a_imp"].sem, 16 * (bh + 1))
                        edge(ve.scalar_tensor_tensor(impsum[:, :], impsum[:, :], SCALE / G,
                                                     cst["cmaskNEG"][:, :], ALU.mult, ALU.add))
                        ve.sem_inc(sm["a_msk"].sem, 1); bump(sm["a_msk"], 1)
                    if bh >= 1:
                        bhp = bh - 1
                        hh = bhp % 8
                        ve.wait_ge(sm["a_gt"].sem, 16 * (bhp // 8 + 1))
                        for j in range(16):
                            qc, g = divmod(j, 4)
                            c = 16 * bhp + j
                            if c >= 1:
                                ve.wait_ge(sm["a_tr"].sem, 12 * c)
                            ve.wait_ge(sm["a_co"].sem, c + 1)
                            ve.tensor_copy(cnum[0:65, :], pbr[0:65, :]).then_inc(sm["a_bev"].sem, 1); bump(sm["a_bev"], 1)
                            ve.wait_ge(sm["a_fo"].sem, c + 1)
                            ve.wait_ge(sm["a_fv"].sem, 16 * (c + 1))
                            ve.tensor_copy(fnum[0:65, :], pbr[0:65, :]).then_inc(sm["a_bev"].sem, 1); bump(sm["a_bev"], 1)
                            ve.wait_ge(sm["a_so"].sem, 8 * (c + 1))
                            ve.tensor_copy(snum[0:65, :], pbr[0:65, :]).then_inc(sm["a_bev"].sem, 1); bump(sm["a_bev"], 1)
                            if j > 0:
                                tback_evac(c - 1)
                            ptrv = ptr[:, 0:390].rearrange("p (qt br o) -> p qt br o", qt=2, br=3)
                            gv = gtile[:, :].rearrange("p (qt gg) -> p qt gg", qt=16)
                            mv = mbr[:, 0:6].rearrange("p (qt br) -> p qt br", qt=2)
                            for r in range(2):
                                ve.wait_ge(sm["a_tr"].sem, 12 * c + 6 * (r + 1))
                                if r == 0 and c >= 1:
                                    ve.wait_ge(sm["a_tb"].sem, 4 * c)
                                edge(ve.reciprocal(zrec[:, 0:6].rearrange("p (qt br) -> p qt br", qt=2),
                                                   ptrv[:, :, :, 64]))
                                edge(ve.tensor_mul(mv,
                                                   gv[:, 4 * qc + 2 * r:4 * qc + 2 * r + 2,
                                                      12 * hh + 3 * g:12 * hh + 3 * g + 3],
                                                   zrec[:, 0:6].rearrange("p (qt br) -> p qt br", qt=2)))
                                av = acc[:, 128 * r:128 * (r + 1)].rearrange("p (qt d) -> p qt d", qt=2)
                                tv = tmpc[:, 0:128].rearrange("p (qt d) -> p qt d", qt=2)
                                edge(ve.tensor_tensor(av, ptrv[:, :, 0, 0:64],
                                                      mv[:, :, 0].unsqueeze(2).to_broadcast([128, 2, 64]),
                                                      ALU.mult))
                                edge(ve.tensor_tensor(tv, ptrv[:, :, 1, 0:64],
                                                      mv[:, :, 1].unsqueeze(2).to_broadcast([128, 2, 64]),
                                                      ALU.mult))
                                edge(ve.tensor_add(av, av, tv))
                                edge(ve.tensor_tensor(tv, ptrv[:, :, 2, 0:64],
                                                      mv[:, :, 2].unsqueeze(2).to_broadcast([128, 2, 64]),
                                                      ALU.mult))
                                ve.tensor_add(av, av, tv).then_inc(sm["a_cb"].sem, 1); bump(sm["a_cb"], 1)
                    if bh < NBH:
                        for n in range(48):
                            src = pim if n % 2 == 0 else ptr
                            ve.wait_ge(sm["a_tpv"].sem, 48 * bh + n + 1)
                            if n < 16:
                                ve.tensor_copy(vKM[:, n * 65:n * 65 + 64],
                                               src[:, 0:64]).then_inc(sm["a_evv"].sem, 1); bump(sm["a_evv"], 1)
                            else:
                                w = n - 16
                                ve.tensor_copy(vKM2[:, w * 65:w * 65 + 64],
                                               src[0:64, 0:64]).then_inc(sm["a_evv"].sem, 1); bump(sm["a_evv"], 1)
                        for qt in range(16):
                            n = 16 * bh + qt
                            srcp = pim[:, 0:128] if qt % 2 == 0 else ptr[:, 0:128]
                            ve.wait_ge(sm["a_imtp"].sem, n + 1)
                            ve.tensor_copy(impq[:, qt * 128:(qt + 1) * 128],
                                           srcp).then_inc(sm["a_imev"].sem, 1); bump(sm["a_imev"], 1)
                        ve.wait_ge(sm["a_imev"].sem, 16 * (bh + 1))
                        for qt in range(16):
                            edge(ve.max(top8[:, qt * 8:(qt + 1) * 8], impq[:, qt * 128:(qt + 1) * 128]))
                        t8v = top8[:, :].rearrange("p (qt e) -> p qt e", qt=16)
                        zrv = zr[:, :].rearrange("p (k qt) -> p k qt", k=6)
                        iv = impq[:, :].rearrange("p (qt q) -> p qt q", qt=16)
                        ev = esh[:, :].rearrange("p (qt q) -> p qt q", qt=16)
                        edge(ve.tensor_scalar(zrv[:, 0, :], t8v[:, :, 0], -1000.0, None, ALU.max))
                        ve.tensor_scalar(zrv[:, 1, :], zrv[:, 0, :], -1.0, None,
                                         ALU.mult).then_inc(sm["a_mpr"].sem, 1); bump(sm["a_mpr"], 1)
                        ve.wait_ge(sm["a_mpr"].sem, bh + 1)
                        ve.tensor_add(ev, iv, zrv[:, 1, :].unsqueeze(2).to_broadcast([128, 16, 128])
                                      ).then_inc(sm["a_lsh"].sem, 1); bump(sm["a_lsh"], 1)
                        ve.wait_ge(sm["a_es"].sem, bh + 1)
                        edge(ve.tensor_reduce(zrv[:, 3, :].unsqueeze(2), ev, AX.X, ALU.add))
                        ve.wait_ge(sm["a_es2"].sem, bh + 1)
                        ve.tensor_add(zrv[:, 3, :], zrv[:, 3, :], zrv[:, 2, :]).then_inc(sm["a_zv"].sem, 1); bump(sm["a_zv"], 1)
                        ve.wait_ge(sm["a_es3"].sem, bh + 1)
                        edge(ve.tensor_scalar(zrv[:, 5, :], zrv[:, 4, :], float(np.log(1e-10)),
                                              None, ALU.add))
                        edge(ve.tensor_add(zrv[:, 5, :], zrv[:, 5, :], zrv[:, 0, :]))
                        if "bmD" in dbg and bh >= 1:
                            ve.wait_ge(sm["a_bd"].sem, 16 * bh)
                        bv = BMq[:, :].rearrange("p (qt q) -> p qt q", qt=16)
                        sv = ev
                        edge(ve.tensor_tensor(bv, iv, t8v[:, :, 7].unsqueeze(2).to_broadcast(
                            [128, 16, 128]), ALU.is_ge))
                        edge(ve.tensor_tensor(sv, iv, zrv[:, 5, :].unsqueeze(2).to_broadcast(
                            [128, 16, 128]), ALU.is_gt))
                        ve.tensor_mul(BMq[:, :], BMq[:, :], esh[:, :]).then_inc(sm["a_bm"].sem, 1); bump(sm["a_bm"], 1)
                        for qt in range(16):
                            n = 16 * bh + qt
                            srcp = pim[:, 0:128] if qt % 2 == 0 else ptr[:, 0:128]
                            ve.wait_ge(sm["a_bmtp"].sem, n + 1)
                            if bh >= 1 and qt == 0:
                                ve.wait_ge(sm["a_fs"].sem, 256 * bh)
                            ve.tensor_scalar(BMT1[:, qt * 128:(qt + 1) * 128], srcp, 1.0,
                                             None, ALU.subtract).then_inc(sm["a_bmev"].sem, 1); bump(sm["a_bmev"], 1)
                tback_evac(16 * NBH - 1)

        full_barrier()

    if "wout" in stages:
        with ExitStack() as sc:
            wo = sc.enter_context(nc.sbuf_tensor("wo_w", [128, 16 * 16 * 128], BF16))
            am = [sc.enter_context(nc.sbuf_tensor("wo_a0", [128, 16 * 512], BF16))] * 2
            wb = sc.enter_context(nc.sbuf_tensor("wo_wb", [128, 16 * 512], F32))
            xt = [sc.enter_context(nc.sbuf_tensor(f"wo_x{i}", [128, DM], F32)) for i in range(2)]
            x1r = sc.enter_context(nc.sbuf_tensor("wo_xr", [128, DM], F32))
            x1l = sc.enter_context(nc.sbuf_tensor("wo_xl", [128, DM], F32))
            stl = sc.enter_context(nc.sbuf_tensor("wo_st", [128, 32], F32))
            txb = [sc.enter_context(nc.sbuf_tensor("wo_t0", [128, DM], BF16))] * 2
            pw = [P0, P1]
            pt = [PIM, PTR]
            pl = [P2, P3]
            w_w = rsem("wo_w2")
            w_am = [rsem("qk_x0"), rsem("qk_x1")]
            w_xl = [rsem("ln_ld0"), rsem("ln_ld1")]
            w_pe = rsem("qk_pe")
            w_ev = rsem("qk_ev")
            w_tp = rsem("ln_tp")
            w_te = rsem("ln_ev")
            w_m = rsem("ln_m")
            w_var = rsem("ln_var")
            w_xlr = rsem("ln_xl")
            w_ltp = rsem("qk_tg")
            w_lev = rsem("qk_eg")
            w_st = [rsem("ln_st0"), rsem("ln_st1")]
            w_e = rsem("ln_e")
            NI = NB * 4

            @blk.gpsimd
            def _(g_):
                for oc in range(16):
                    dma(g_, wo[:, oc * 2048:(oc + 1) * 2048].rearrange("p (ic m) -> p ic m", ic=16),
                        w_out[:, oc * 128:(oc + 1) * 128].rearrange("(ic p) m -> p ic m", p=128), w_w)

            @blk.sync
            def _(sy):
                for it in range(NI):
                    seq, tq = divmod(it, 4)
                    if it >= 1:
                        sy.wait_ge(w_pe.sem, w_pe.base + 16 * it)
                    for hf in range(2):
                        dma(sy, am[it % 2][hf * 64:hf * 64 + 64, :].rearrange("p (ic t) -> p ic t", ic=16),
                            attnFM[seq, hf:32:2, :, tq * 512:(tq + 1) * 512].transpose([1, 0, 2]),
                            w_am[0])
                    for tt in range(4):
                        n = 4 * it + tt
                        if n >= 2:
                            sy.wait_ge(w_xlr.sem, w_xlr.base + n - 1)
                        dma(sy, xt[n % 2][:, :],
                            x_in[seq, tq * 512 + tt * 128:tq * 512 + (tt + 1) * 128, :],
                            w_xl[n % 2])
                        sy.wait_ge(w_lev.sem, w_lev.base + 16 * (n + 1))
                        sy.wait_ge(w_xlr.sem, w_xlr.base + n + 1)
                        tg = tq * 512 + tt * 128
                        dma(sy, xln2T[seq, :, :, tg:tg + 128].transpose([1, 0, 2]),
                            txb[n % 2][:, :].rearrange("p (dc t) -> p dc t", dc=16), w_st[n % 2])
                        dma(sy, x1D[seq, tg:tg + 128, :], x1r[:, :], w_st[n % 2])
                for i in range(2):
                    sy.wait_ge(w_st[i].sem, w_st[i].val)

            @blk.tensor
            def _(t):
                t.wait_ge(w_w.sem, w_w.val)
                for it in range(NI):
                    seq, tq = divmod(it, 4)
                    t.wait_ge(w_am[0].sem, w_am[0].base + 32 * (it + 1))
                    amv = am[it % 2][:, :].rearrange("p (ic t) -> p ic t", ic=16)
                    for oc in range(16):
                        n = 16 * it + oc
                        if n >= 2:
                            t.wait_ge(w_ev.sem, w_ev.base + n - 1)
                        for ic in range(16):
                            mm = t.matmul(pw[n % 2][:, :],
                                          wo[:, (oc * 16 + ic) * 128:(oc * 16 + ic + 1) * 128],
                                          amv[:, ic, :], start=(ic == 0), stop=(ic == 15))
                        mm.then_inc(w_pe.sem, 1); bump(w_pe, 1)
                    for tt in range(4):
                        n = 4 * it + tt
                        t.wait_ge(w_ev.sem, w_ev.base + 16 * (it + 1))
                        for oc in range(16):
                            k = 16 * n + oc
                            if k >= 2:
                                t.wait_ge(w_te.sem, w_te.base + k - 1)
                            t.transpose(pt[k % 2][:, 0:128],
                                        wb[:, (oc * 4 + tt) * 128:(oc * 4 + tt + 1) * 128],
                                        identF_sb[:, :]).then_inc(w_tp.sem, 1); bump(w_tp, 1)
                        t.wait_ge(w_xlr.sem, w_xlr.base + n + 1)
                        for dc in range(16):
                            k = 16 * n + dc
                            if k >= 2:
                                t.wait_ge(w_lev.sem, w_lev.base + k - 1)
                            t.transpose(pl[k % 2][:, 0:128], x1l[:, dc * 128:(dc + 1) * 128],
                                        identF_sb[:, :]).then_inc(w_ltp.sem, 1); bump(w_ltp, 1)

            @blk.vector
            def _(ve):
                ec = [0]

                def edge(ins):
                    ins.then_inc(w_e.sem, 1); bump(w_e, 1)
                    ec[0] += 1
                    ve.wait_ge(w_e.sem, w_e.base + ec[0])

                edge(ve.memset(stl[:, :], 0.0))
                for it in range(NI):
                    for oc in range(16):
                        n = 16 * it + oc
                        ve.wait_ge(w_pe.sem, w_pe.base + n + 1)
                        ve.tensor_copy(wb[:, oc * 512:(oc + 1) * 512],
                                       pw[n % 2][:, :]).then_inc(w_ev.sem, 1); bump(w_ev, 1)
                    for tt in range(4):
                        n = 4 * it + tt
                        ve.wait_ge(w_xl[n % 2].sem, w_xl[n % 2].base + 16 * (n // 2 + 1))
                        if n >= 1:
                            ve.wait_ge(w_xlr.sem, w_xlr.base + n)
                        for oc in range(16):
                            k = 16 * n + oc
                            ve.wait_ge(w_tp.sem, w_tp.base + k + 1)
                            if n >= 1 and oc == 0:
                                ve.wait_ge(w_st[(n - 1) % 2].sem, w_st[(n - 1) % 2].base + 32 * ((n - 1) // 2 + 1))
                            ve.tensor_add(x1r[:, oc * 128:(oc + 1) * 128], pt[k % 2][:, 0:128],
                                          xt[n % 2][:, oc * 128:(oc + 1) * 128]
                                          ).then_inc(w_te.sem, 1); bump(w_te, 1)
                        for cch in range(4):
                            edge(ve.bn_stats(stl[:, 6 * cch:6 * cch + 6],
                                             x1r[:, 512 * cch:512 * (cch + 1)]))
                        edge(ve.bn_aggr(stl[:, 24:26], stl[:, 0:12]))
                        edge(ve.bn_aggr(stl[:, 26:28], stl[:, 12:24]))
                        edge(ve.tensor_tensor(stl[:, 2:3], stl[:, 24:25], stl[:, 26:27], ALU.add))
                        edge(ve.tensor_scalar(stl[:, 2:3], stl[:, 2:3], 0.5, None, ALU.mult))
                        edge(ve.tensor_tensor(stl[:, 3:4], stl[:, 24:25], stl[:, 26:27],
                                              ALU.subtract))
                        edge(ve.tensor_scalar(stl[:, 3:4], stl[:, 3:4], 0.5, None, ALU.mult))
                        edge(ve.tensor_scalar(stl[:, 3:4], stl[:, 3:4], stl[:, 3:4], None, ALU.mult))
                        edge(ve.tensor_tensor(stl[:, 4:5], stl[:, 25:26], stl[:, 27:28], ALU.add))
                        ve.tensor_scalar(stl[:, 4:5], stl[:, 4:5], 0.5, stl[:, 3:4],
                                          ALU.mult, ALU.add).then_inc(w_m.sem, 1); bump(w_m, 1)
                        ve.wait_ge(w_var.sem, w_var.base + n + 1)
                        edge(ve.reciprocal(stl[:, 6:7], stl[:, 5:6]))
                        ve.tensor_scalar(x1l[:, :], x1r[:, :], stl[:, 2:3], stl[:, 6:7],
                                         ALU.subtract, ALU.mult).then_inc(w_xlr.sem, 1); bump(w_xlr, 1)
                        for dc in range(16):
                            k = 16 * n + dc
                            ve.wait_ge(w_ltp.sem, w_ltp.base + k + 1)
                            if n >= 1 and dc == 0:
                                ve.wait_ge(w_st[(n - 1) % 2].sem, w_st[(n - 1) % 2].base + 32 * ((n - 1) // 2 + 1))
                            ve.tensor_copy(txb[n % 2][:, dc * 128:(dc + 1) * 128],
                                           pl[k % 2][:, 0:128]).then_inc(w_lev.sem, 1); bump(w_lev, 1)

            @blk.scalar
            def _(s):
                s.wait_ge(w_e.sem, w_e.base + 1)
                for it in range(NI):
                    for tt in range(4):
                        n = 4 * it + tt
                        s.wait_ge(w_m.sem, w_m.base + n + 1)
                        s.activation(stl[:, 5:6], stl[:, 4:5], AF.Sqrt, bias=smallc[:, 0:1],
                                     scale=1.0).then_inc(w_var.sem, 1); bump(w_var, 1)

        full_barrier()

    if "ffna" in stages:
        with ExitStack() as sc:
            xr = sc.enter_context(nc.sbuf_tensor("fa_x", [128, 16 * NB * S], BF16))
            wgs = [sc.enter_context(nc.sbuf_tensor(f"fa_wg{i}", [128, 16 * 128], BF16)) for i in range(2)]
            wus = [sc.enter_context(nc.sbuf_tensor(f"fa_wu{i}", [128, 16 * 128], BF16)) for i in range(2)]
            gsil = sc.enter_context(nc.sbuf_tensor("fa_gs", [128, 512], F32))
            gbuf = [sc.enter_context(nc.sbuf_tensor("fa_gb0", [128, NB * S], BF16))] * 2
            pg = [P0, P1]
            pu = [P2, P3]
            f_x = rsem("cm_ld")
            f_w = [rsem("fa_w0"), rsem("fa_w1")]
            f_pg = rsem("ro_pe")
            f_pu = rsem("ro_t2")
            f_si = rsem("ro_dv")
            f_gt = rsem("cm_mm")
            f_st = [rsem("fa_st0"), rsem("fa_st1")]

            @blk.sync
            def _(sy):
                for seq in range(NB):
                    for dc in range(16):
                        dma(sy, xr[:, (dc * NB + seq) * S:(dc * NB + seq + 1) * S],
                            xln2T[seq, dc, :, :], f_x)

            @blk.gpsimd
            def _(g_):
                for hc in range(65):
                    if hc < 64:
                        if hc >= 2:
                            g_.wait_ge(f_pu.sem, f_pu.base + 8 * (hc - 1))
                        dma(g_, wgs[hc % 2][:, :].rearrange("p (ic m) -> p ic m", ic=16),
                            ffn_wgT[:, hc * 128:(hc + 1) * 128].rearrange("(ic p) m -> p ic m", p=128),
                            f_w[hc % 2])
                        dma(g_, wus[hc % 2][:, :].rearrange("p (ic m) -> p ic m", ic=16),
                            ffn_wuT[:, hc * 128:(hc + 1) * 128].rearrange("(ic p) m -> p ic m", p=128),
                            f_w[hc % 2])
                    if hc >= 1:
                        g_.wait_ge(f_gt.sem, f_gt.base + 8 * hc)
                        dma(g_, gatedD[hc - 1, :, :], gbuf[(hc - 1) % 2][:, :], f_st[(hc - 1) % 2])
                for i in range(2):
                    g_.wait_ge(f_st[i].sem, f_st[i].val)

            @blk.tensor
            def _(t):
                t.wait_ge(f_x.sem, f_x.val)
                for hc in range(64):
                    t.wait_ge(f_w[hc % 2].sem, f_w[hc % 2].base + 32 * (hc // 2 + 1))
                    for tb in range(8):
                        n = 8 * hc + tb
                        if n >= 2:
                            t.wait_ge(f_si.sem, f_si.base + n - 1)
                        for ic in range(16):
                            mm = t.matmul(pg[n % 2][:, :], wgs[hc % 2][:, ic * 128:(ic + 1) * 128],
                                          xr[:, :].rearrange("p (ic t) -> p ic t", ic=16)[:, ic,
                                          tb * 512:(tb + 1) * 512],
                                          start=(ic == 0), stop=(ic == 15))
                        mm.then_inc(f_pg.sem, 1); bump(f_pg, 1)
                        if n >= 2:
                            t.wait_ge(f_gt.sem, f_gt.base + n - 1)
                        for ic in range(16):
                            mm = t.matmul(pu[n % 2][:, :], wus[hc % 2][:, ic * 128:(ic + 1) * 128],
                                          xr[:, :].rearrange("p (ic t) -> p ic t", ic=16)[:, ic,
                                          tb * 512:(tb + 1) * 512],
                                          start=(ic == 0), stop=(ic == 15))
                        mm.then_inc(f_pu.sem, 1); bump(f_pu, 1)

            @blk.scalar
            def _(s):
                for hc in range(64):
                    for tb in range(8):
                        n = 8 * hc + tb
                        s.wait_ge(f_pg.sem, f_pg.base + n + 1)
                        if n >= 1:
                            s.wait_ge(f_gt.sem, f_gt.base + n)
                        s.activation(gsil[:, :], pg[n % 2][:, :], AF.Silu).then_inc(f_si.sem, 1); bump(f_si, 1)

            @blk.vector
            def _(ve):
                for hc in range(64):
                    for tb in range(8):
                        n = 8 * hc + tb
                        ve.wait_ge(f_si.sem, f_si.base + n + 1)
                        ve.wait_ge(f_pu.sem, f_pu.base + n + 1)
                        if hc >= 1 and tb == 0:
                            ve.wait_ge(f_st[(hc - 1) % 2].sem,
                                       f_st[(hc - 1) % 2].base + 16 * ((hc - 1) // 2 + 1))
                        ve.tensor_mul(gbuf[hc % 2][:, tb * 512:(tb + 1) * 512], gsil[:, :],
                                      pu[n % 2][:, :]).then_inc(f_gt.sem, 1); bump(f_gt, 1)

        full_barrier()

    if "ffnb" in stages:
        with ExitStack() as sc:
            gr = sc.enter_context(nc.sbuf_tensor("fb_g", [128, 64 * 512], BF16))
            wds = [sc.enter_context(nc.sbuf_tensor(f"fb_w{i}", [128, 64 * 128], BF16)) for i in range(2)]
            ob = sc.enter_context(nc.sbuf_tensor("fb_o", [128, 16 * 512], F32))
            x1t = [sc.enter_context(nc.sbuf_tensor("fb_x0", [128, DM], F32))] * 2
            orow = [sc.enter_context(nc.sbuf_tensor(f"fb_or{i}", [128, DM], F32)) for i in range(2)]
            pd = [P0, P1]
            pt2 = [PIM, PTR]
            b_g = rsem("cm_ld")
            b_w = [rsem("fb_w0"), rsem("fb_w1")]
            b_x = [rsem("ln_ld0"), rsem("ln_ld1")]
            b_pe = rsem("qk_pe")
            b_ev = rsem("qk_ev")
            b_tp = rsem("ln_tp")
            b_te = rsem("ln_ev")
            b_st = [rsem("ln_st0"), rsem("ln_st1")]
            NSH = 8

            @blk.sync
            def _(sy):
                for sh in range(NSH):
                    if sh >= 1:
                        sy.wait_ge(b_pe.sem, b_pe.base + 16 * sh)
                    for q4_ in range(2):
                        hc0 = q4_ * 16
                        dma(sy, gr[:, hc0 * 512:(hc0 + 16) * 512].rearrange(
                                "p (hc t) -> p hc t", hc=16),
                            gatedD[hc0:hc0 + 16, :, sh * 512:(sh + 1) * 512].transpose([1, 0, 2]),
                            b_g)
                    for tt in range(4):
                        n = 4 * sh + tt
                        tg = sh * 512 + tt * 128
                        seq, trow = divmod(tg, S)
                        if n >= 1:
                            sy.wait_ge(b_te.sem, b_te.base + 16 * n)
                        dma(sy, x1t[n % 2][:, :], x1D[seq, trow:trow + 128, :], b_x[n % 2])
                        sy.wait_ge(b_te.sem, b_te.base + 16 * (n + 1))
                        dma(sy, out[seq, trow:trow + 128, :], orow[n % 2][:, :], b_st[n % 2])
                for i in range(2):
                    sy.wait_ge(b_st[i].sem, b_st[i].val)

            @blk.scalar
            def _(s):
                for sh in range(NSH):
                    if sh >= 1:
                        s.wait_ge(b_pe.sem, b_pe.base + 16 * sh)
                    for q4_ in range(2):
                        hc0 = 32 + q4_ * 16
                        dma(s, gr[:, hc0 * 512:(hc0 + 16) * 512].rearrange(
                                "p (hc t) -> p hc t", hc=16),
                            gatedD[hc0:hc0 + 16, :, sh * 512:(sh + 1) * 512].transpose([1, 0, 2]),
                            b_g)

            @blk.gpsimd
            def _(g_):
                g_.wait_ge(sem("a_wdb").sem, 128)
                for sh in range(NSH):
                    for oc in range(16):
                        n = 16 * sh + oc
                        if n >= 2:
                            g_.wait_ge(b_pe.sem, b_pe.base + n - 1)
                        dma(g_, wds[n % 2][:, :].rearrange("p (hc m) -> p hc m", hc=64),
                            wdB[:, oc * 128:(oc + 1) * 128].rearrange("(hc p) m -> p hc m", p=128),
                            b_w[n % 2])

            @blk.tensor
            def _(t):
                for sh in range(NSH):
                    t.wait_ge(b_g.sem, b_g.base + 4 * 16 * (sh + 1))
                    for oc in range(16):
                        n = 16 * sh + oc
                        t.wait_ge(b_w[n % 2].sem, b_w[n % 2].base + 16 * (n // 2 + 1))
                        if n >= 2:
                            t.wait_ge(b_ev.sem, b_ev.base + n - 1)
                        for hc in range(64):
                            mm = t.matmul(pd[n % 2][:, :],
                                          wds[n % 2][:, hc * 128:(hc + 1) * 128],
                                          gr[:, :].rearrange("p (hc t) -> p hc t", hc=64)[:, hc, :],
                                          start=(hc == 0), stop=(hc == 63))
                        mm.then_inc(b_pe.sem, 1); bump(b_pe, 1)
                    for tt in range(4):
                        n = 4 * sh + tt
                        t.wait_ge(b_ev.sem, b_ev.base + 16 * (sh + 1))
                        for oc in range(16):
                            k = 16 * n + oc
                            if k >= 2:
                                t.wait_ge(b_te.sem, b_te.base + k - 1)
                            t.transpose(pt2[k % 2][:, 0:128],
                                        ob[:, (oc * 4 + tt) * 128:(oc * 4 + tt) * 128 + 128],
                                        identF_sb[:, :]).then_inc(b_tp.sem, 1); bump(b_tp, 1)

            @blk.vector
            def _(ve):
                for sh in range(NSH):
                    for oc in range(16):
                        n = 16 * sh + oc
                        ve.wait_ge(b_pe.sem, b_pe.base + n + 1)
                        if sh >= 1 and oc == 0:
                            ve.wait_ge(b_tp.sem, b_tp.base + 64 * sh)
                        ve.tensor_copy(ob[:, oc * 512:(oc + 1) * 512],
                                       pd[n % 2][:, :]).then_inc(b_ev.sem, 1); bump(b_ev, 1)
                    for tt in range(4):
                        n = 4 * sh + tt
                        ve.wait_ge(b_x[n % 2].sem, b_x[n % 2].base + 16 * (n // 2 + 1))
                        for oc in range(16):
                            k = 16 * n + oc
                            ve.wait_ge(b_tp.sem, b_tp.base + k + 1)
                            if n >= 2 and oc == 0:
                                ve.wait_ge(b_st[n % 2].sem, b_st[n % 2].base + 16 * (n // 2))
                            ve.tensor_add(orow[n % 2][:, oc * 128:(oc + 1) * 128], pt2[k % 2][:, 0:128],
                                          x1t[n % 2][:, oc * 128:(oc + 1) * 128]
                                          ).then_inc(b_te.sem, 1); bump(b_te, 1)

        full_barrier()

    if "ffnb" not in stages:
        s_fin = sem("fin")
        zb = ctx.enter_context(nc.sbuf_tensor("zb", [128, DM], F32))

        @blk.vector
        def _(ve):
            ve.memset(zb[:, :], 0.0).then_inc(s_fin.sem, 1)
        s_fin.val += 1

        @blk.sync
        def _(sy):
            sy.wait_ge(s_fin.sem, s_fin.val)
            for seq in range(NB):
                for tc in range(16):
                    dma(sy, out[seq, tc * 128:(tc + 1) * 128, :], zb[:, :], s_fin)
            sy.wait_ge(s_fin.sem, s_fin.val)

    ctx.close()
    nc.finalize()
    return nc


def prep_inputs(inputs):
    c = make_consts()
    w = {
        "w_qkv": inputs["L0_attn_w_qkv"],
        "w_gate": inputs["L0_attn_w_gate"],
        "b_gate": np.ascontiguousarray(inputs["L0_attn_b_gate"].reshape(96, 1)),
        "w_out": inputs["L0_attn_w_out"],
        "k_cw1": inputs["L0_attn_k_cw1"],
        "k_cb1": np.ascontiguousarray(inputs["L0_attn_k_cb1"].reshape(CDIM, 1)),
        "k_cw2": inputs["L0_attn_k_cw2"],
        "k_cb2": np.ascontiguousarray(inputs["L0_attn_k_cb2"].reshape(DH, 1)),
        "v_cw1": inputs["L0_attn_v_cw1"],
        "v_cb1": np.ascontiguousarray(inputs["L0_attn_v_cb1"].reshape(CDIM, 1)),
        "v_cw2": inputs["L0_attn_v_cw2"],
        "v_cb2": np.ascontiguousarray(inputs["L0_attn_v_cb2"].reshape(DH, 1)),
        "kposT": np.ascontiguousarray(inputs["L0_attn_k_pos"].transpose(0, 2, 1)),
        "vposT": np.ascontiguousarray(inputs["L0_attn_v_pos"].transpose(0, 2, 1)),
        "memckT": np.ascontiguousarray(inputs["L0_attn_mem_ck"].reshape(KVH, DH).T),
        "memcv": np.ascontiguousarray(inputs["L0_attn_mem_cv"].reshape(KVH, DH)),
        "ffn_wgT": np.ascontiguousarray(inputs["L0_ffn_w_gate"].T),
        "ffn_wuT": np.ascontiguousarray(inputs["L0_ffn_w_up"].T),
        "ffn_wdT": np.ascontiguousarray(inputs["L0_ffn_w_down"].T),
    }
    w.update(c)
    return w


def kernel(**inputs):
    x = inputs["L0_attn_hidden_states"]
    shared = prep_inputs(inputs)
    nc = build()
    in_maps = []
    for core in range(8):
        m = dict(shared)
        m["x"] = np.ascontiguousarray(x[2 * core:2 * core + 2])
        in_maps.append(m)
    res = run_bass_kernel_spmd(nc, in_maps, core_ids=list(range(8)))
    return np.concatenate([r["out"] for r in res.results], axis=0)


# version 07
# speedup vs baseline: 1.3323x; 1.0425x over previous
import numpy as np
import ml_dtypes
from contextlib import ExitStack

import concourse.bass as bass
import concourse.mybir as mybir
from concourse.bass_utils import run_bass_kernel_spmd

F32 = mybir.dt.float32
BF16 = mybir.dt.bfloat16
AF = mybir.ActivationFunctionType
ALU = mybir.AluOpType
AX = mybir.AxisListType

NB = 2
S = 2048
DM = 2048
HEADS, KVH, G, DH = 32, 8, 4, 64
CBLK = SBLK = 16
NSEL = 8
WIN = 64
SCALE = DH ** -0.5
CDIM = CBLK * DH
NBLK = S // CBLK
DI = HEADS * DH
DKV = KVH * DH
DFF = 8192
BIGF = 1600.0
NEG30 = -1e30
EPS = 1e-5

ALL_STAGES = ("ln1", "qkv", "rope", "cmp", "attn", "wout", "ffna", "ffnb")


def make_consts():
    c = {}
    ident = np.eye(128, dtype=np.float32)
    c["identF"] = ident
    c["identB"] = ident
    R = np.zeros((DH, DH), np.float32)
    for i in range(DH // 2):
        R[2 * i, 2 * i + 1] = -1.0
        R[2 * i + 1, 2 * i] = 1.0
    R2T = np.zeros((128, 128), np.float32)
    R2T[:64, :64] = R.T
    R2T[64:, 64:] = R.T
    c["R2T"] = R2T
    inv = 1.0 / 10000.0 ** (np.arange(0, DH, 2, dtype=np.float32) / DH)
    ang = np.arange(S, dtype=np.float32)[None, :] * inv[:, None]
    cos = np.repeat(np.cos(ang), 2, axis=0)
    sin = np.repeat(np.sin(ang), 2, axis=0)
    c["C2"] = np.ascontiguousarray(np.concatenate([cos, cos], 0))
    c["S2"] = np.ascontiguousarray(np.concatenate([sin, sin], 0))
    blk = np.arange(NBLK)[:, None]
    qi = np.arange(S)[None, :]
    valid = (16 * (blk + 1) - 1) < qi
    c["cmaskADD"] = np.where(valid, 0.0, -BIGF).astype(np.float32)
    c["cmaskNEG"] = np.where(valid, 0.0, NEG30).astype(np.float32)
    key = np.arange(S)[None, :]
    c["EBIG"] = np.where(np.arange(NBLK)[:, None] == (key // SBLK), BIGF, 0.0).astype(np.float32)
    kk = np.arange(128)[:, None]
    qq = np.arange(128)[None, :]
    own01 = ((qq // 16) == (kk // 16)) & ((qq % 16) >= (kk % 16))
    c["ownADD"] = np.where(own01, BIGF, 0.0).astype(np.float32)
    j = np.arange(WIN)[:, None]
    q = np.arange(WIN)[None, :]
    c["sADDp"] = np.where(q <= j, 0.0, -BIGF).astype(np.float32)
    c["sADDo"] = np.where(j <= q, 0.0, -BIGF).astype(np.float32)
    return c


class Ctr:
    __slots__ = ("sem", "val")

    def __init__(self, sem):
        self.sem = sem
        self.val = 0


def build(dbg=(), stages=ALL_STAGES):
    nc = bass.Bass("TRN2", target_bir_lowering=False)

    def din(name, shape, dtype=F32):
        return nc.dram_tensor(name, list(shape), dtype, kind="ExternalInput").ap()

    def dscr(name, shape, dtype):
        kind = "ExternalOutput" if name in dbg else "Internal"
        return nc.dram_tensor(name, list(shape), dtype, kind=kind).ap()

    x_in = din("x", (NB, S, DM))
    w_qkv = din("w_qkv", (DM, DI + 2 * DKV))
    w_gate = din("w_gate", (DM, 96))
    b_gate = din("b_gate", (96, 1))
    w_out = din("w_out", (DI, DM))
    k_cw1 = din("k_cw1", (CDIM, CDIM))
    k_cb1 = din("k_cb1", (CDIM, 1))
    k_cw2 = din("k_cw2", (CDIM, DH))
    k_cb2 = din("k_cb2", (DH, 1))
    v_cw1 = din("v_cw1", (CDIM, CDIM))
    v_cb1 = din("v_cb1", (CDIM, 1))
    v_cw2 = din("v_cw2", (CDIM, DH))
    v_cb2 = din("v_cb2", (DH, 1))
    kposT = din("kposT", (KVH, DH, CBLK))
    vposT = din("vposT", (KVH, DH, CBLK))
    memckT = din("memckT", (DH, KVH))
    memcv = din("memcv", (KVH, DH))
    ffn_wgT = din("ffn_wgT", (DM, DFF))
    ffn_wuT = din("ffn_wuT", (DM, DFF))
    ffn_wdT = din("ffn_wdT", (DFF, DM))
    identF = din("identF", (128, 128))
    identB_in = din("identB", (128, 128))
    R2T_in = din("R2T", (128, 128))
    C2_in = din("C2", (128, S))
    S2_in = din("S2", (128, S))
    cmaskADD_in = din("cmaskADD", (128, S))
    cmaskNEG_in = din("cmaskNEG", (128, S))
    EBIG_in = din("EBIG", (128, S))
    ownADD_in = din("ownADD", (128, 128))
    sADDp_in = din("sADDp", (64, 64))
    sADDo_in = din("sADDo", (64, 64))

    out = nc.dram_tensor("out", [NB, S, DM], F32, kind="ExternalOutput").ap()
    xlnT = dscr("xlnT", (NB, 16, 128, S), BF16)
    qF = dscr("qF", (NB, 16, 128, S), F32)
    kF = dscr("kF", (NB, 4, 128, S), F32)
    vF = dscr("vF", (NB, 4, 128, S), F32)
    gatesT = dscr("gatesT", (NB, 16, 128, 96), F32)
    rqT = dscr("rqT", (NB, 16, 128, S), BF16)
    rkT = dscr("rkT", (NB, 4, 128, S), BF16)
    kbT = dscr("kbT", (NB, 4, 128, S), BF16)
    vbT = dscr("vbT", (NB, 4, 128, S), BF16)
    ckD = dscr("ckD", (DH, NB * KVH * NBLK), F32)
    cvD = dscr("cvD", (DH, NB * KVH * NBLK), F32)
    attnFM = dscr("attnFM", (NB, 32, 64, S), BF16)
    x1D = dscr("x1D", (NB, S, DM), F32)
    xln2T = dscr("xln2T", (NB, 16, 128, S), BF16)
    gatedD = dscr("gatedD", (64, 128, NB * S), BF16)
    wdB = dscr("wdB", (DFF, DM), BF16)
    bmD = dscr("bmD", (NB, KVH, 16, 128, 128), BF16)

    ctx = ExitStack()
    blk = ctx.enter_context(nc.Block())
    _sems = {}

    def sem(name):
        if name not in _sems:
            _sems[name] = Ctr(ctx.enter_context(nc.semaphore(name)))
        return _sems[name]

    class RS:
        __slots__ = ("c", "base")

        def __init__(self, c):
            self.c = c
            self.base = c.val

        @property
        def sem(self):
            return self.c.sem

        @property
        def val(self):
            return self.c.val

    def rsem(pool):
        return RS(sem(pool))

    def bump(x, n=1):
        (x.c if isinstance(x, RS) else x).val += n

    def dma(eng, dst, src, ctr):
        tgt = ctr.c if isinstance(ctr, RS) else ctr
        eng.dma_start(dst, src).then_inc(tgt.sem, 16)
        tgt.val += 16

    def full_barrier():
        bar = sem("bar")
        bar.val += 5
        v = bar.val

        @blk.tensor
        def _(t):
            t.sem_inc(bar.sem, 1)
            t.wait_ge(bar.sem, v)

        @blk.vector
        def _(ve):
            ve.sem_inc(bar.sem, 1)
            ve.wait_ge(bar.sem, v)

        @blk.scalar
        def _(s):
            s.sem_inc(bar.sem, 1)
            s.wait_ge(bar.sem, v)

        @blk.gpsimd
        def _(g):
            g.sem_inc(bar.sem, 1)
            g.wait_ge(bar.sem, v)

        @blk.sync
        def _(sy):
            sy.sem_inc(bar.sem, 1)
            sy.wait_ge(bar.sem, v)

    ckT = ctx.enter_context(nc.sbuf_tensor("ckT", [DH, NB * KVH * NBLK], BF16))
    cvT = ctx.enter_context(nc.sbuf_tensor("cvT", [DH, NB * KVH * NBLK], F32))
    cvKM = ctx.enter_context(nc.sbuf_tensor("cvKM", [128, NB * KVH * 65], BF16))
    memcv_st = ctx.enter_context(nc.sbuf_tensor("memcv_st", [1, KVH * 65], BF16))
    memck_sb = ctx.enter_context(nc.sbuf_tensor("memck_sb", [DH, KVH], BF16))
    identB = ctx.enter_context(nc.sbuf_tensor("identB_sb", [128, 128], BF16))
    identF_sb = ctx.enter_context(nc.sbuf_tensor("identF_sb", [128, 128], F32))
    smallc = ctx.enter_context(nc.sbuf_tensor("smallc", [128, 4], F32))
    P0 = ctx.enter_context(nc.psum_tensor("P0", [128, 512], F32))
    P1 = ctx.enter_context(nc.psum_tensor("P1", [128, 512], F32))
    P2 = ctx.enter_context(nc.psum_tensor("P2", [128, 512], F32))
    P3 = ctx.enter_context(nc.psum_tensor("P3", [128, 512], F32))
    P4 = ctx.enter_context(nc.psum_tensor("P4", [128, 512], F32))
    PTR = ctx.enter_context(nc.psum_tensor("PTR", [128, 390], F32))
    PIM = ctx.enter_context(nc.psum_tensor("PIM", [128, 128], F32))
    PMEM = ctx.enter_context(nc.psum_tensor("PMEM", [1, 512], F32))
    cst = {}
    for nm, src, shp, dt in [("R2T", R2T_in, (128, 128), F32), ("C2", C2_in, (128, S), F32),
                             ("S2", S2_in, (128, S), F32), ("cmaskADD", cmaskADD_in, (128, S), BF16),
                             ("cmaskNEG", cmaskNEG_in, (128, S), F32), ("EBIG", EBIG_in, (128, S), BF16),
                             ("ownADD", ownADD_in, (128, 128), BF16), ("sADDp", sADDp_in, (64, 64), BF16),
                             ("sADDo", sADDo_in, (64, 64), BF16)]:
        cst[nm] = ctx.enter_context(nc.sbuf_tensor("c_" + nm, list(shp), dt))

    ld0 = sem("ld0")

    @blk.gpsimd
    def _(g):
        dma(g, identB[:, :], identB_in[:, :], ld0)
        dma(g, identF_sb[:, :], identF[:, :], ld0)
        for nm, src in [("R2T", R2T_in), ("C2", C2_in), ("S2", S2_in), ("cmaskADD", cmaskADD_in),
                        ("cmaskNEG", cmaskNEG_in), ("EBIG", EBIG_in), ("ownADD", ownADD_in),
                        ("sADDp", sADDp_in), ("sADDo", sADDo_in)]:
            dma(g, cst[nm][:, :], src[:, :], ld0)
        dma(g, memck_sb[:, :], memckT[:, :], ld0)
        for h in range(KVH):
            dma(g, memcv_st[0:1, h * 65:h * 65 + 64], memcv[h:h + 1, :], ld0)

    @blk.vector
    def _(ve):
        ve.wait_ge(ld0.sem, ld0.val)
        ve.memset(memcv_st[0:1, :].rearrange("p (h o) -> p h o", h=KVH)[:, :, 64:65], 1.0)
        ve.memset(smallc[:, 0:1], EPS)
        ve.memset(smallc[:, 1:2], -1000.0)

    full_barrier()

    if "ln1" in stages:
        with ExitStack() as sc:
            NT = NB * 16
            xt = [sc.enter_context(nc.sbuf_tensor(f"ln_x{i}", [128, DM], F32)) for i in range(2)]
            xl = [sc.enter_context(nc.sbuf_tensor(f"ln_o{i}", [128, DM], F32)) for i in range(2)]
            tx = [sc.enter_context(nc.sbuf_tensor(f"ln_t{i}", [128, DM], BF16)) for i in range(2)]
            st = sc.enter_context(nc.sbuf_tensor("ln_st", [128, 32], F32))
            sq = sc.enter_context(nc.sbuf_tensor("ln_sq", [128, DM], F32))
            pt = [PIM, PTR]
            s_ld = [sem("ln_ld0"), sem("ln_ld1")]
            s_m = sem("ln_m")
            s_var = sem("ln_var")
            s_xl = sem("ln_xl")
            s_tp = sem("ln_tp")
            s_ev = sem("ln_ev")
            s_st = [sem("ln_st0"), sem("ln_st1")]

            @blk.sync
            def _(sy):
                for i in range(min(2, NT)):
                    seq, tc = divmod(i, 16)
                    dma(sy, xt[i % 2][:, :], x_in[seq, tc * 128:(tc + 1) * 128, :], s_ld[i % 2])
                for i in range(NT):
                    sy.wait_ge(s_ev.sem, 16 * (i + 1))
                    seq, tc = divmod(i, 16)
                    dma(sy, xlnT[seq, :, :, tc * 128:(tc + 1) * 128].transpose([1, 0, 2]),
                        tx[i % 2][:, :].rearrange("p (dc t) -> p dc t", dc=16), s_st[i % 2])
                    if i + 2 < NT:
                        sy.wait_ge(s_xl.sem, i + 1)
                        seq2, tc2 = divmod(i + 2, 16)
                        dma(sy, xt[i % 2][:, :], x_in[seq2, tc2 * 128:(tc2 + 1) * 128, :], s_ld[i % 2])
                sy.wait_ge(s_st[0].sem, s_st[0].val)
                sy.wait_ge(s_st[1].sem, s_st[1].val)

            s_e = sem("ln_e")
            s_sq = sem("ln_sq2")
            s_v2 = sem("ln_v2")

            @blk.vector
            def _(ve):
                for i in range(NT):
                    bi = i % 2
                    ve.wait_ge(s_ld[i % 2].sem, 16 * (i // 2 + 1))
                    if i >= 1:
                        ve.wait_ge(s_xl.sem, i)
                    ve.tensor_reduce(st[:, 0:1], xt[bi][:, :], AX.X, ALU.add).then_inc(s_e.sem, 1); bump(s_e, 1)
                    ve.wait_ge(s_e.sem, 5 * i + 1)
                    ve.tensor_scalar(st[:, 2:3], st[:, 0:1], 1.0 / DM, None, ALU.mult).then_inc(s_e.sem, 1); bump(s_e, 1)
                    ve.wait_ge(s_sq.sem, i + 1)
                    ve.tensor_reduce(st[:, 1:2], sq[:, :], AX.X, ALU.add).then_inc(s_v2.sem, 1); bump(s_v2, 1)
                    ve.wait_ge(s_v2.sem, i + 1)
                    ve.tensor_scalar(st[:, 3:4], st[:, 2:3], st[:, 2:3], None, ALU.mult).then_inc(s_e.sem, 1); bump(s_e, 1)
                    ve.wait_ge(s_e.sem, 5 * i + 3)
                    ve.tensor_scalar(st[:, 4:5], st[:, 1:2], 1.0 / DM, st[:, 3:4],
                                     ALU.mult, ALU.subtract).then_inc(s_m.sem, 1); bump(s_m, 1)
                    ve.wait_ge(s_var.sem, i + 1)
                    ve.reciprocal(st[:, 27:28], st[:, 26:27]).then_inc(s_e.sem, 1); bump(s_e, 1)
                    ve.wait_ge(s_e.sem, 5 * i + 4)
                    ve.tensor_scalar(xl[bi][:, :], xt[bi][:, :], st[:, 2:3], st[:, 27:28],
                                     ALU.subtract, ALU.mult).then_inc(s_xl.sem, 1); bump(s_xl, 1)
                    ve.sem_inc(s_e.sem, 1); bump(s_e, 1)
                    for dc in range(16):
                        n = 16 * i + dc
                        ve.wait_ge(s_tp.sem, n + 1)
                        if i >= 2 and dc == 0:
                            ve.wait_ge(s_st[i % 2].sem, 16 * (i // 2))
                        ve.tensor_copy(tx[bi][:, dc * 128:(dc + 1) * 128], pt[n % 2][:, 0:128]).then_inc(s_ev.sem, 1); bump(s_ev, 1)

            @blk.scalar
            def _(s):
                for i in range(NT):
                    bi = i % 2
                    s.wait_ge(s_ld[i % 2].sem, 16 * (i // 2 + 1))
                    if i >= 1:
                        s.wait_ge(s_v2.sem, i)
                    s.activation(sq[:, :], xt[bi][:, :], AF.Square).then_inc(s_sq.sem, 1); bump(s_sq, 1)
                    s.wait_ge(s_m.sem, i + 1)
                    s.activation(st[:, 26:27], st[:, 4:5], AF.Sqrt, bias=smallc[:, 0:1],
                                 scale=1.0).then_inc(s_var.sem, 1); bump(s_var, 1)

            @blk.tensor
            def _(t):
                for i in range(NT):
                    bi = i % 2
                    t.wait_ge(s_xl.sem, i + 1)
                    for dc in range(16):
                        n = 16 * i + dc
                        if n >= 2:
                            t.wait_ge(s_ev.sem, n - 1)
                        t.transpose(pt[n % 2][:, 0:128], xl[bi][:, dc * 128:(dc + 1) * 128],
                                    identF_sb[:, :]).then_inc(s_tp.sem, 1); bump(s_tp, 1)

        full_barrier()

    if "qkv" in stages:
        with ExitStack() as sc:
            OC = 24
            wq = sc.enter_context(nc.sbuf_tensor("qk_w", [128, OC * 16 * 128], BF16))
            wg = sc.enter_context(nc.sbuf_tensor("qk_wg", [128, 16 * 96], BF16))
            bg = sc.enter_context(nc.sbuf_tensor("qk_bg", [96, 1], F32))
            xs = [sc.enter_context(nc.sbuf_tensor(f"qk_x{i}", [128, 16 * 512], BF16)) for i in range(2)]
            stg = [sc.enter_context(nc.sbuf_tensor(f"qk_s{i}", [128, 512], F32)) for i in range(4)]
            gst = sc.enter_context(nc.sbuf_tensor("qk_gs", [96, 512], F32))
            gtt = sc.enter_context(nc.sbuf_tensor("qk_gt", [128, 4 * 96], F32))
            pq = [P0, P1]
            pg = P2
            pgt = P3
            s_w = sem("qk_w")
            s_x = [sem("qk_x0"), sem("qk_x1")]
            s_pe = sem("qk_pe")
            s_ev = sem("qk_ev")
            s_pg = sem("qk_pg")
            s_sg = sem("qk_sg")
            s_tg = sem("qk_tg")
            s_eg = sem("qk_eg")
            s_st = [sem(f"qk_st{i}") for i in range(4)]
            s_gs = sem("qk_gs2")
            NT = NB * 4

            @blk.gpsimd
            def _(g):
                for oc in range(OC):
                    dma(g, wq[:, oc * 2048:(oc + 1) * 2048].rearrange("p (dc m) -> p dc m", dc=16),
                        w_qkv[:, oc * 128:(oc + 1) * 128].rearrange("(dc p) m -> p dc m", p=128), s_w)
                dma(g, wg[:, :].rearrange("p (dc m) -> p dc m", dc=16),
                    w_gate[:, :].rearrange("(dc p) m -> p dc m", p=128), s_w)
                dma(g, bg[:, :], b_gate[:, :], s_w)

            @blk.sync
            def _(sy):
                for it in range(min(2, NT)):
                    seq, tq = divmod(it, 4)
                    dma(sy, xs[it % 2][:, :].rearrange("p (dc t) -> p dc t", dc=16),
                        xlnT[seq, :, :, tq * 512:(tq + 1) * 512].transpose([1, 0, 2]), s_x[it % 2])
                for it in range(NT):
                    seq, tq = divmod(it, 4)
                    for oc in range(OC):
                        n = it * OC + oc
                        sy.wait_ge(s_ev.sem, n + 1)
                        dstf = (qF[seq, oc, :, tq * 512:(tq + 1) * 512] if oc < 16 else
                                kF[seq, oc - 16, :, tq * 512:(tq + 1) * 512] if oc < 20 else
                                vF[seq, oc - 20, :, tq * 512:(tq + 1) * 512])
                        dma(sy, dstf, stg[n % 4][:, :], s_st[n % 4])
                    sy.wait_ge(s_eg.sem, 4 * (it + 1))
                    dma(sy, gatesT[seq, tq * 4:(tq + 1) * 4, :, :].transpose([1, 0, 2]),
                        gtt[:, :].rearrange("p (tt g) -> p tt g", tt=4), s_gs)
                    if it + 2 < NT:
                        seq2, tq2 = divmod(it + 2, 4)
                        dma(sy, xs[it % 2][:, :].rearrange("p (dc t) -> p dc t", dc=16),
                            xlnT[seq2, :, :, tq2 * 512:(tq2 + 1) * 512].transpose([1, 0, 2]), s_x[it % 2])
                for i in range(4):
                    sy.wait_ge(s_st[i].sem, s_st[i].val)
                sy.wait_ge(s_gs.sem, s_gs.val)

            @blk.tensor
            def _(t):
                t.wait_ge(s_w.sem, 16 * (OC + 2))
                for it in range(NT):
                    xv = xs[it % 2][:, :].rearrange("p (dc t) -> p dc t", dc=16)
                    t.wait_ge(s_x[it % 2].sem, 16 * (it // 2 + 1))
                    for oc in range(OC):
                        n = it * OC + oc
                        if n >= 2:
                            t.wait_ge(s_ev.sem, n - 1)
                        for dc in range(16):
                            mm = t.matmul(pq[n % 2][:, :], wq[:, (oc * 16 + dc) * 128:(oc * 16 + dc + 1) * 128],
                                          xv[:, dc, :], start=(dc == 0), stop=(dc == 15))
                        mm.then_inc(s_pe.sem, 1); bump(s_pe, 1)
                    if it >= 1:
                        t.wait_ge(s_sg.sem, it)
                    for dc in range(16):
                        mm = t.matmul(pg[0:96, :], wg[:, dc * 96:(dc + 1) * 96], xv[:, dc, :],
                                      start=(dc == 0), stop=(dc == 15))
                    mm.then_inc(s_pg.sem, 1); bump(s_pg, 1)
                    t.wait_ge(s_sg.sem, it + 1)
                    for tt in range(4):
                        n = 4 * it + tt
                        if n >= 1:
                            t.wait_ge(s_eg.sem, n)
                        t.transpose(pgt[:, 0:96], gst[0:96, tt * 128:(tt + 1) * 128],
                                    identF_sb[0:96, 0:96]).then_inc(s_tg.sem, 1); bump(s_tg, 1)

            @blk.vector
            def _(ve):
                for it in range(NT):
                    for oc in range(OC):
                        n = it * OC + oc
                        ve.wait_ge(s_pe.sem, n + 1)
                        if n >= 4:
                            ve.wait_ge(s_st[n % 4].sem, 16 * (n // 4))
                        ve.tensor_copy(stg[n % 4][:, :], pq[n % 2][:, :]).then_inc(s_ev.sem, 1); bump(s_ev, 1)
                    for tt in range(4):
                        n = 4 * it + tt
                        ve.wait_ge(s_tg.sem, n + 1)
                        if it >= 1 and tt == 0:
                            ve.wait_ge(s_gs.sem, 16 * it)
                        ve.tensor_copy(gtt[:, tt * 96:(tt + 1) * 96], pgt[:, 0:96]).then_inc(s_eg.sem, 1); bump(s_eg, 1)

            @blk.scalar
            def _(s):
                for it in range(NT):
                    s.wait_ge(s_pg.sem, it + 1)
                    if it >= 1:
                        s.wait_ge(s_tg.sem, 4 * it)
                    s.activation(gst[0:96, :], pg[0:96, :], AF.Sigmoid,
                                 bias=bg[0:96, 0:1]).then_inc(s_sg.sem, 1); bump(s_sg, 1)

        full_barrier()

    if "rope" in stages:
        with ExitStack() as sc:
            bf = [sc.enter_context(nc.sbuf_tensor(f"ro_x{i}", [128, S], F32)) for i in range(2)]
            t1 = sc.enter_context(nc.sbuf_tensor("ro_t1", [128, S], F32))
            t2 = sc.enter_context(nc.sbuf_tensor("ro_t2", [128, S], F32))
            ob = [sc.enter_context(nc.sbuf_tensor(f"ro_o{i}", [128, S], BF16)) for i in range(2)]
            kpb = sc.enter_context(nc.sbuf_tensor("ro_kp", [128, CBLK], F32))
            pr = [P0, P1]
            s_ld = [sem("ro_ld0"), sem("ro_ld1")]
            s_pe = sem("ro_pe")
            s_t2 = sem("ro_t2")
            s_dv = sem("ro_dv")
            s_st = [sem("ro_st0"), sem("ro_st1")]
            s_kp = sem("ro_kp")

            bands = []
            for seq in range(NB):
                for bd in range(16):
                    bands.append((0, seq, bd))
                for bd in range(4):
                    bands.append((1, seq, bd))
                for bd in range(4):
                    bands.append((2, seq, bd))
                for bd in range(4):
                    bands.append((3, seq, bd))

            def src_of(kind, seq, bd):
                return (qF if kind == 0 else kF if kind in (1, 2) else vF)[seq, bd, :, :]

            @blk.sync
            def _(sy):
                for i in range(min(2, len(bands))):
                    kind, seq, bd = bands[i]
                    dma(sy, bf[i % 2][:, :], src_of(kind, seq, bd), s_ld[i % 2])
                for i, (kind, seq, bd) in enumerate(bands):
                    sy.wait_ge(s_dv.sem, i + 1)
                    dst = (rqT, rkT, kbT, vbT)[kind]
                    dma(sy, dst[seq, bd, :, :], ob[i % 2][:, :], s_st[i % 2])
                    if i + 2 < len(bands):
                        kind2, seq2, bd2 = bands[i + 2]
                        dma(sy, bf[i % 2][:, :], src_of(kind2, seq2, bd2), s_ld[i % 2])
                sy.wait_ge(s_st[0].sem, s_st[0].val)
                sy.wait_ge(s_st[1].sem, s_st[1].val)

            @blk.gpsimd
            def _(g):
                kpi = 0
                for i, (kind, seq, bd) in enumerate(bands):
                    if kind in (2, 3):
                        if kpi >= 1:
                            g.wait_ge(s_dv.sem, prev_kp + 1)
                        src = kposT if kind == 2 else vposT
                        dma(g, kpb[0:64, :], src[2 * bd, :, :], s_kp)
                        dma(g, kpb[64:128, :], src[2 * bd + 1, :, :], s_kp)
                        kpi += 1
                        prev_kp = i

            @blk.tensor
            def _(t):
                nrot = 0
                for i, (kind, seq, bd) in enumerate(bands):
                    if kind in (0, 1):
                        t.wait_ge(s_ld[i % 2].sem, 16 * (i // 2 + 1))
                        for c in range(4):
                            if nrot >= 2:
                                t.wait_ge(s_t2.sem, nrot - 1)
                            t.matmul(pr[nrot % 2][:, :], cst["R2T"][:, :],
                                     bf[i % 2][:, c * 512:(c + 1) * 512], start=True,
                                     stop=True).then_inc(s_pe.sem, 1); bump(s_pe, 1)
                            nrot += 1

            @blk.vector
            def _(ve):
                nrot = 0
                nkp = 0
                for i, (kind, seq, bd) in enumerate(bands):
                    bi = i % 2
                    ve.wait_ge(s_ld[i % 2].sem, 16 * (i // 2 + 1))
                    if i >= 1:
                        ve.wait_ge(s_dv.sem, i)
                    if i >= 2:
                        ve.wait_ge(s_st[i % 2].sem, 16 * (i // 2))
                    if kind in (0, 1):
                        ve.tensor_mul(t1[:, :], bf[bi][:, :], cst["C2"][:, :])
                        for c in range(4):
                            ve.wait_ge(s_pe.sem, nrot + 1)
                            ve.tensor_mul(t2[:, c * 512:(c + 1) * 512], pr[nrot % 2][:, :],
                                          cst["S2"][:, c * 512:(c + 1) * 512]).then_inc(s_t2.sem, 1); bump(s_t2, 1)
                            nrot += 1
                        ve.wait_ge(s_t2.sem, nrot)
                        ve.tensor_add(ob[bi][:, :], t1[:, :], t2[:, :]).then_inc(s_dv.sem, 1); bump(s_dv, 1)
                    else:
                        nkp += 32
                        ve.wait_ge(s_kp.sem, nkp)
                        ve.tensor_add(ob[bi][:, :].rearrange("p (c blk) -> p c blk", c=CBLK),
                                      bf[bi][:, :].rearrange("p (blk c) -> p c blk", c=CBLK),
                                      kpb[:, :].unsqueeze(2).to_broadcast([128, CBLK, NBLK])
                                      ).then_inc(s_dv.sem, 1); bump(s_dv, 1)

        full_barrier()

    s_ck = sem("cm_ck")
    s_km = sem("cm_km")
    if "cmp" in stages:
        with ExitStack() as sc:
            cw1 = {}
            cw2 = {}
            cb1 = {}
            cb2 = {}
            for t_, w1, w2 in [("k", k_cw1, k_cw2), ("v", v_cw1, v_cw2)]:
                cw1[t_] = sc.enter_context(nc.sbuf_tensor(f"cm_w1{t_}", [128, 8 * 1024], BF16))
                cw2[t_] = sc.enter_context(nc.sbuf_tensor(f"cm_w2{t_}", [128, 8 * 64], BF16))
                cb1[t_] = sc.enter_context(nc.sbuf_tensor(f"cm_b1{t_}", [128, 8], F32))
                cb2[t_] = sc.enter_context(nc.sbuf_tensor(f"cm_b2{t_}", [64, 1], F32))
            kbt = sc.enter_context(nc.sbuf_tensor("cm_kb", [128, 8 * 512], BF16))
            vbt = sc.enter_context(nc.sbuf_tensor("cm_vb", [128, 8 * 512], BF16))
            h1 = sc.enter_context(nc.sbuf_tensor("cm_h1", [128, 8 * 512], BF16))
            ph = [P0, P1]
            pc = P2
            ptk = [PIM, PTR]
            s_w = sem("cm_w")
            s_ldg = sem("cm_ld")
            s_mm = sem("cm_mm")
            s_h1 = sem("cm_h1")
            s_pc = sem("cm_pc")
            s_tp = sem("cm_tp")
            NG = 4

            @blk.gpsimd
            def _(g):
                for t_, w1, w2, b1, b2 in [("k", k_cw1, k_cw2, k_cb1, k_cb2),
                                           ("v", v_cw1, v_cw2, v_cb1, v_cb2)]:
                    for uu in range(8):
                        dma(g, cw1[t_][:, uu * 1024:(uu + 1) * 1024],
                            w1[uu * 128:(uu + 1) * 128, :], s_w)
                    dma(g, cw2[t_][:, :].rearrange("p (mc m) -> p mc m", mc=8),
                        w2[:, :].rearrange("(mc p) m -> p mc m", p=128), s_w)
                    g.dma_start(cb1[t_][:, :].unsqueeze(2),
                                b1[:, :].rearrange("(mc p) o -> p mc o", p=128),
                                allow_slow_non_contiguous=True).then_inc(s_w.sem, 16); bump(s_w, 16)
                    dma(g, cb2[t_][:, :], b2[:, :], s_w)

            @blk.sync
            def _(sy):
                for g in range(NG):
                    if g >= 1:
                        sy.wait_ge(s_h1.sem, 16 * g)
                    b = (4 * g) // 8
                    band0 = ((4 * g) % 8) // 2
                    for j4 in range(4):
                        band = band0 + j4 // 2
                        r0 = (j4 % 2) * 64
                        for par in range(2):
                            ksrc = kbT[b, band, r0:r0 + 64, :].rearrange(
                                "p (u par blk) -> p par u blk", par=2, blk=NBLK)[:, par, :, :]
                            vsrc = vbT[b, band, r0:r0 + 64, :].rearrange(
                                "p (u par blk) -> p par u blk", par=2, blk=NBLK)[:, par, :, :]
                            kdst = kbt[par * 64:par * 64 + 64, :].rearrange(
                                "p (u j4 blk) -> p u j4 blk", u=8, j4=4)[:, :, j4, :]
                            vdst = vbt[par * 64:par * 64 + 64, :].rearrange(
                                "p (u j4 blk) -> p u j4 blk", u=8, j4=4)[:, :, j4, :]
                            dma(sy, kdst, ksrc, s_ldg)
                            dma(sy, vdst, vsrc, s_ldg)

            def kb_mov(t_, u):
                tile = kbt if t_ == "k" else vbt
                return tile[:, u * 512:(u + 1) * 512]

            @blk.tensor
            def _(t):
                t.wait_ge(s_w.sem, s_w.val)
                for g in range(NG):
                    t.wait_ge(s_ldg.sem, 256 * (g + 1))
                    for ti, t_ in enumerate(("k", "v")):
                        for mc in range(8):
                            n = 16 * g + 8 * ti + mc
                            if n >= 2:
                                t.wait_ge(s_h1.sem, n - 1)
                            for u in range(8):
                                w1ap = cw1[t_][:, u * 1024 + mc * 128:u * 1024 + (mc + 1) * 128]
                                mm = t.matmul(ph[n % 2][:, :], w1ap, kb_mov(t_, u),
                                              start=(u == 0), stop=(u == 7))
                            mm.then_inc(s_mm.sem, 1); bump(s_mm, 1)
                        t.wait_ge(s_h1.sem, 16 * g + 8 * ti + 8)
                        for mc in range(8):
                            mm = t.matmul(pc[0:64, :], cw2[t_][:, mc * 64:(mc + 1) * 64],
                                          h1[:, mc * 512:(mc + 1) * 512],
                                          start=(mc == 0), stop=(mc == 7))
                        mm.then_inc(s_pc.sem, 1); bump(s_pc, 1)
                        if not (g == NG - 1 and ti == 1):
                            t.wait_ge(s_ck.sem, 2 * g + ti + 1)
                    t.wait_ge(s_ck.sem, 2 * g + 2)
                    for j4 in range(4):
                        n = 4 * g + j4
                        if n >= 2:
                            t.wait_ge(s_km.sem, n - 1)
                        bh = 4 * g + j4
                        t.transpose(ptk[n % 2][:, 0:64], cvT[:, bh * 128:(bh + 1) * 128],
                                    identF_sb[0:64, 0:64]).then_inc(s_tp.sem, 1); bump(s_tp, 1)

            @blk.scalar
            def _(s):
                for g in range(NG):
                    for ti, t_ in enumerate(("k", "v")):
                        for mc in range(8):
                            n = 16 * g + 8 * ti + mc
                            s.wait_ge(s_mm.sem, n + 1)
                            s.activation(h1[:, mc * 512:(mc + 1) * 512], ph[n % 2][:, :],
                                         AF.Relu, bias=cb1[t_][:, mc:mc + 1]).then_inc(s_h1.sem, 1); bump(s_h1, 1)
                        s.wait_ge(s_pc.sem, 2 * g + ti + 1)
                        dstT = ckT if t_ == "k" else cvT
                        s.activation(dstT[:, g * 512:(g + 1) * 512], pc[0:64, :], AF.Identity,
                                     bias=cb2[t_][:, 0:1]).then_inc(s_ck.sem, 1); bump(s_ck, 1)

            @blk.vector
            def _(ve):
                ve.memset(cvKM[:, :].rearrange("p (bh o) -> p bh o", bh=NB * KVH)[:, :, 64:65], 1.0)
                for g in range(NG):
                    for j4 in range(4):
                        n = 4 * g + j4
                        ve.wait_ge(s_tp.sem, n + 1)
                        bh = 4 * g + j4
                        ve.tensor_copy(cvKM[:, bh * 65:bh * 65 + 64], ptk[n % 2][:, 0:64]
                                       ).then_inc(s_km.sem, 1); bump(s_km, 1)

            if "ckD" in dbg or "cvD" in dbg:
                @blk.gpsimd
                def _(g):
                    g.wait_ge(s_ck.sem, 8)
                    dma(g, ckD[:, :], ckT[:, :], s_w)
                    dma(g, cvD[:, :], cvT[:, :], s_w)
                    g.wait_ge(s_w.sem, s_w.val)

        full_barrier()

    if "attn" in stages:
        with ExitStack() as sc:
            rq4 = [sc.enter_context(nc.sbuf_tensor(f"at_rq{i}", [64, 4 * S], BF16)) for i in range(2)]
            q4 = [sc.enter_context(nc.sbuf_tensor(f"at_q{i}", [64, 4 * S], BF16)) for i in range(2)]
            rkh = [sc.enter_context(nc.sbuf_tensor(f"at_rk{i}", [64, S], BF16)) for i in range(2)]
            vth = [sc.enter_context(nc.sbuf_tensor("at_vt0", [64, S], F32))] * 2
            vKM = sc.enter_context(nc.sbuf_tensor("at_vkm", [128, 16 * 65], BF16))
            vKM2 = sc.enter_context(nc.sbuf_tensor("at_vkm2", [64, 32 * 65], BF16))
            impsum = sc.enter_context(nc.sbuf_tensor("at_imps", [128, S], F32))
            esh = impsum
            impq = sc.enter_context(nc.sbuf_tensor("at_impq", [128, S], F32))
            top8 = sc.enter_context(nc.sbuf_tensor("at_top8", [128, 16 * 8], F32))
            zr = sc.enter_context(nc.sbuf_tensor("at_zr", [128, 6 * 16], F32))
            BMq = sc.enter_context(nc.sbuf_tensor("at_bmq", [128, S], F32))
            BMT1 = sc.enter_context(nc.sbuf_tensor("at_bmt1", [128, S], BF16))
            pf = sc.enter_context(nc.sbuf_tensor("at_pf", [128, 8 * 512], BF16))
            pc_sb = sc.enter_context(nc.sbuf_tensor("at_pc", [128, 512], BF16))
            pm_sb = sc.enter_context(nc.sbuf_tensor("at_pm", [1, 512], BF16))
            psl = sc.enter_context(nc.sbuf_tensor("at_psl", [64, 512], BF16))
            pso = sc.enter_context(nc.sbuf_tensor("at_pso", [64, 512], BF16))
            cnum = sc.enter_context(nc.sbuf_tensor("at_cn", [65, 512], F32))
            fnum = sc.enter_context(nc.sbuf_tensor("at_fn", [65, 512], F32))
            snum = sc.enter_context(nc.sbuf_tensor("at_sn", [65, 512], F32))
            zrec = sc.enter_context(nc.sbuf_tensor("at_zrec", [128, 12], F32))
            mbr = sc.enter_context(nc.sbuf_tensor("at_mbr", [128, 12], F32))
            acc = sc.enter_context(nc.sbuf_tensor("at_acc", [128, 256], F32))
            tmpc = sc.enter_context(nc.sbuf_tensor("at_tmpc", [128, 256], F32))
            gtile = sc.enter_context(nc.sbuf_tensor("at_gt", [128, 16 * 96], F32))
            asm = [sc.enter_context(nc.sbuf_tensor(f"at_asm{i}", [64, S], BF16)) for i in range(4)]
            pcs = [P0, P1]
            pfs = [P2, P3]
            pbr = P4
            ptr = PTR
            pim = PIM
            pmem = PMEM

            names = ["a_ld0", "a_ld1", "a_lq0", "a_lq1", "a_gt", "a_tpv", "a_evv", "a_cs",
                     "a_imp", "a_msk", "a_imtp", "a_imev", "a_bm", "a_bmtp", "a_bmev", "a_e",
                     "a_c1", "a_cm", "a_xc", "a_xm", "a_co", "a_bev", "a_fs", "a_xf", "a_fv",
                     "a_fo", "a_ss", "a_xs", "a_so", "a_tr", "a_cb", "a_tb", "a_te", "a_st",
                     "a_ms", "a_lsh", "a_mpr", "a_es", "a_es2", "a_zv", "a_es3", "a_bd", "a_wdb"]
            sm = {n: sem(n) for n in names}
            NBH = NB * KVH

            def bh_of(i):
                return i // 8, i % 8

            def ldw(bh):
                return 16 * 6 * (bh // 2 + 1)

            def ss_base(c):
                full, j = divmod(c, 16)
                tot = full * 252
                for jj in range(j):
                    tot += 15 if jj // 4 == 0 else 16
                return tot

            def ss_nprev(c):
                return 7 if (c % 16) // 4 == 0 else 8

            @blk.sync
            def _(sy):
                for bh in range(NBH):
                    b, h = bh_of(bh)
                    p = bh % 2
                    if bh >= 2:
                        sy.wait_ge(sm["a_te"].sem, 16 * (bh - 1))
                    for g in range(G):
                        band = 2 * h + g // 2
                        r0 = (g % 2) * 64
                        dma(sy, rq4[p][:, g * S:(g + 1) * S], rqT[b, band, r0:r0 + 64, :], sm[f"a_ld{p}"])
                    dma(sy, rkh[p][:, :], rkT[b, h // 2, (h % 2) * 64:(h % 2) * 64 + 64, :], sm[f"a_ld{p}"])
                    if bh >= 1:
                        sy.wait_ge(sm["a_tpv"].sem, 48 * bh)
                    dma(sy, vth[p][:, :], vF[b, h // 2, (h % 2) * 64:(h % 2) * 64 + 64, :], sm[f"a_ld{p}"])
                    if bh >= 2:
                        bp, hp = bh_of(bh - 2)
                        sy.wait_ge(sm["a_te"].sem, 16 * (bh - 1))
                        for g in range(G):
                            dma(sy, attnFM[bp, 4 * hp + g, :, :], asm[g][:, :], sm["a_st"])
                    if h == 0:
                        if bh >= 2:
                            sy.wait_ge(sm["a_cb"].sem, 32 * bh)
                        dma(sy, gtile[:, :].rearrange("p (qt g) -> p qt g", qt=16),
                            gatesT[b, :, :, :].transpose([1, 0, 2]), sm["a_gt"])
                for bh in (NBH - 2, NBH - 1):
                    bp, hp = bh_of(bh)
                    sy.wait_ge(sm["a_te"].sem, 16 * (bh + 1))
                    for g in range(G):
                        dma(sy, attnFM[bp, 4 * hp + g, :, :], asm[g][:, :], sm["a_st"])
                sy.wait_ge(sm["a_st"].sem, sm["a_st"].val)

            @blk.gpsimd
            def _(g_):
                for bh in range(NBH):
                    b, h = bh_of(bh)
                    p = bh % 2
                    if bh >= 2:
                        g_.wait_ge(sm["a_cs"].sem, 16 * (bh - 1))
                        g_.wait_ge(sm["a_c1"].sem, 16 * (bh - 1))
                        g_.wait_ge(sm["a_cm"].sem, 16 * (bh - 1))
                    for g in range(G):
                        band = 2 * h + g // 2
                        r0 = (g % 2) * 64
                        dma(g_, q4[p][:, g * S:(g + 1) * S], qF[b, band, r0:r0 + 64, :], sm[f"a_lq{p}"])
                    if 4 <= bh < 12:
                        ch = bh - 4
                        dma(g_, wdB[ch * 1024:(ch + 1) * 1024, :],
                            ffn_wdT[ch * 1024:(ch + 1) * 1024, :], sm["a_wdb"])
                    if "bmD" in dbg:
                        g_.wait_ge(sm["a_bm"].sem, bh + 1)
                        dma(g_, bmD[b, h, :, :, :].transpose([1, 0, 2]),
                            BMq[:, :].rearrange("p (qt blk) -> p qt blk", qt=16), sm["a_bd"])

            @blk.tensor
            def _(t):
                def flush_tr_r(cp, r):
                    if r == 0:
                        t.wait_ge(sm["a_bev"].sem, 3 * (cp + 1))
                    t.wait_ge(sm["a_cb"].sem, 2 * cp + r)
                    for q2 in range(2):
                        qt2 = 2 * r + q2
                        for br, buf in enumerate((cnum, fnum, snum)):
                            t.transpose(ptr[:, (q2 * 3 + br) * 65:(q2 * 3 + br + 1) * 65],
                                        buf[0:65, qt2 * 128:(qt2 + 1) * 128],
                                        identF_sb[0:65, 0:65]).then_inc(sm["a_tr"].sem, 1); bump(sm["a_tr"], 1)

                def flush_tr(cp):
                    flush_tr_r(cp, 0)
                    flush_tr_r(cp, 1)

                def flush_tb(cp):
                    t.wait_ge(sm["a_cb"].sem, 2 * (cp + 1))
                    t.wait_ge(sm["a_xc"].sem, cp + 1)
                    for qt2 in range(4):
                        t.transpose(pcs[cp % 2][0:64, qt2 * 128:(qt2 + 1) * 128],
                                    acc[:, qt2 * 64:(qt2 + 1) * 64],
                                    identF_sb[:, :]).then_inc(sm["a_tb"].sem, 1); bump(sm["a_tb"], 1)

                def emit_T3(bh):
                    b, h = bh_of(bh)
                    p = bh % 2
                    t.wait_ge(sm["a_evv"].sem, 48 * (bh + 1))
                    t.wait_ge(sm["a_imp"].sem, min(16 * (bh + 2), 16 * NBH))
                    t.wait_ge(sm["a_bmev"].sem, 16 * (bh + 1))
                    for j in range(16):
                        qc, g = divmod(j, 4)
                        c = 16 * bh + j
                        if j > 0:
                            flush_tr_r(c - 1, 0)
                        qsl = q4[p][:, g * S + qc * 512:g * S + (qc + 1) * 512]
                        rqsl = rq4[p][:, g * S + qc * 512:g * S + (qc + 1) * 512]
                        if j >= 2:
                            t.wait_ge(sm["a_xc"].sem, c - 1)
                            t.wait_ge(sm["a_te"].sem, c - 1)
                        t.matmul(pcs[c % 2][:, :], ckT[:, bh * 128:(bh + 1) * 128], qsl,
                                 start=True, stop=False)
                        t.matmul(pcs[c % 2][:, :], identB[:, :],
                                 cst["cmaskADD"][:, qc * 512:(qc + 1) * 512],
                                 start=False, stop=True).then_inc(sm["a_c1"].sem, 1); bump(sm["a_c1"], 1)
                        if c >= 1:
                            t.wait_ge(sm["a_xm"].sem, c)
                        t.matmul(pmem[0:1, :], memck_sb[:, h:h + 1], qsl,
                                 start=True, stop=True).then_inc(sm["a_cm"].sem, 1); bump(sm["a_cm"], 1)
                        for kc in range(16):
                            n = 16 * c + kc
                            if n >= 2:
                                t.wait_ge(sm["a_xf"].sem, n - 1)
                            if kc < 2 and c >= 1:
                                t.wait_ge(sm["a_so"].sem, 8 * c)
                            dg = kc - 4 * qc
                            diag = dg if 0 <= dg < 4 else None
                            t.matmul(pfs[kc % 2][:, :], rkh[p][:, kc * 128:(kc + 1) * 128],
                                     rqsl, start=True, stop=False)
                            mmE = t.matmul(pfs[kc % 2][:, :], cst["EBIG"][:, kc * 128:(kc + 1) * 128],
                                           BMT1[:, qc * 512:(qc + 1) * 512],
                                           start=False, stop=(diag is None))
                            if diag is not None:
                                mmE = t.matmul(pfs[kc % 2][:, diag * 128:(diag + 1) * 128],
                                               identB[:, :], cst["ownADD"][:, :],
                                               start=False, stop=True)
                            mmE.then_inc(sm["a_fs"].sem, 1); bump(sm["a_fs"], 1)
                            if kc == 1:
                                if j > 0:
                                    flush_tr_r(c - 1, 1)
                                if c >= 1:
                                    t.wait_ge(sm["a_bev"].sem, 3 * c)
                                t.wait_ge(sm["a_xm"].sem, c + 1)
                                t.matmul(pbr[0:65, :], memcv_st[0:1, h * 65:(h + 1) * 65],
                                         pm_sb[0:1, :], start=True, stop=False)
                                t.wait_ge(sm["a_xc"].sem, c + 1)
                                t.matmul(pbr[0:65, :], cvKM[:, bh * 65:(bh + 1) * 65], pc_sb[:, :],
                                         start=False, stop=True).then_inc(sm["a_co"].sem, 1); bump(sm["a_co"], 1)
                            if kc >= 2:
                                nf = kc - 2
                                if nf == 0:
                                    t.wait_ge(sm["a_bev"].sem, 3 * c + 1)
                                t.wait_ge(sm["a_xf"].sem, 16 * c + nf + 1)
                                t.matmul(pbr[0:65, :], vKM[:, nf * 65:(nf + 1) * 65],
                                         pf[:, (nf % 8) * 512:(nf % 8 + 1) * 512],
                                         start=(nf == 0), stop=False).then_inc(sm["a_fv"].sem, 1); bump(sm["a_fv"], 1)
                        for nf in (14, 15):
                            t.wait_ge(sm["a_xf"].sem, 16 * c + nf + 1)
                            t.matmul(pbr[0:65, :], vKM[:, nf * 65:(nf + 1) * 65],
                                     pf[:, (nf % 8) * 512:(nf % 8 + 1) * 512],
                                     start=False, stop=(nf == 15)).then_inc(sm["a_fv"].sem, 1); bump(sm["a_fv"], 1)
                        t.nop().then_inc(sm["a_fo"].sem, 1); bump(sm["a_fo"], 1)
                        t.wait_ge(sm["a_xf"].sem, 16 * (c + 1))
                        for w8 in range(8):
                            w = qc * 8 + w8
                            if w == 0:
                                continue
                            t.matmul(pfs[0][0:64, w8 * 64:(w8 + 1) * 64],
                                     rkh[p][:, (w - 1) * 64:w * 64],
                                     rq4[p][:, g * S + w * 64:g * S + (w + 1) * 64],
                                     start=True, stop=False)
                            t.matmul(pfs[0][0:64, w8 * 64:(w8 + 1) * 64],
                                     identB[0:64, 0:64], cst["sADDp"][:, :],
                                     start=False, stop=True).then_inc(sm["a_ss"].sem, 1); bump(sm["a_ss"], 1)
                        for w8 in range(8):
                            w = qc * 8 + w8
                            t.matmul(pfs[1][0:64, w8 * 64:(w8 + 1) * 64],
                                     rkh[p][:, w * 64:(w + 1) * 64],
                                     rq4[p][:, g * S + w * 64:g * S + (w + 1) * 64],
                                     start=True, stop=False)
                            t.matmul(pfs[1][0:64, w8 * 64:(w8 + 1) * 64],
                                     identB[0:64, 0:64], cst["sADDo"][:, :],
                                     start=False, stop=True).then_inc(sm["a_ss"].sem, 1); bump(sm["a_ss"], 1)
                        t.wait_ge(sm["a_bev"].sem, 3 * c + 2)
                        t.wait_ge(sm["a_xs"].sem, 2 * (c + 1))
                        for w8 in range(8):
                            w = qc * 8 + w8
                            if w > 0:
                                t.matmul(pbr[0:65, w8 * 64:(w8 + 1) * 64],
                                         vKM2[:, (w - 1) * 65:(w - 1) * 65 + 65],
                                         psl[:, w8 * 64:(w8 + 1) * 64], start=True, stop=False)
                            t.matmul(pbr[0:65, w8 * 64:(w8 + 1) * 64],
                                     vKM2[:, w * 65:w * 65 + 65],
                                     pso[:, w8 * 64:(w8 + 1) * 64],
                                     start=(w == 0), stop=True).then_inc(sm["a_so"].sem, 1); bump(sm["a_so"], 1)
                        if c >= 1 and j > 0:
                            flush_tb(c - 1)

                t.wait_ge(s_ck.sem, s_ck.val)
                t.wait_ge(s_km.sem, s_km.val)
                t.wait_ge(sm["a_ms"].sem, 2)
                for bh in range(NBH + 1):
                    b, h = bh_of(bh) if bh < NBH else (0, 0)
                    p = bh % 2
                    if bh < NBH:
                        t.wait_ge(sm[f"a_lq{p}"].sem, 64 * (bh // 2 + 1))
                        pA = [pcs[0], pcs[1], pfs[0], pfs[1]]
                        for j in range(16):
                            qc, g = divmod(j, 4)
                            n = 16 * bh + j
                            if n >= 4:
                                t.wait_ge(sm["a_imp"].sem, n - 3)
                            if bh >= 2 and j == 0:
                                t.wait_ge(sm["a_xc"].sem, 16 * (bh - 1))
                                t.wait_ge(sm["a_te"].sem, 16 * (bh - 1))
                                t.wait_ge(sm["a_xf"].sem, 256 * (bh - 1))
                                t.wait_ge(sm["a_xs"].sem, 32 * (bh - 1))
                            t.matmul(pA[j % 4][:, :], ckT[:, bh * 128:(bh + 1) * 128],
                                     q4[p][:, g * S + qc * 512:g * S + (qc + 1) * 512],
                                     start=True, stop=False)
                            t.matmul(pA[j % 4][:, :], identB[:, :],
                                     cst["cmaskADD"][:, qc * 512:(qc + 1) * 512],
                                     start=False, stop=True).then_inc(sm["a_cs"].sem, 1); bump(sm["a_cs"], 1)
                    if bh >= 1:
                        emit_T3(bh - 1)
                        flush_tr(16 * bh - 1)
                        flush_tb(16 * bh - 1)
                    if bh < NBH:
                        t.wait_ge(sm[f"a_ld{p}"].sem, ldw(bh))
                        if bh >= 1:
                            t.wait_ge(sm["a_cb"].sem, 32 * bh)
                            t.wait_ge(sm["a_bmev"].sem, 16 * bh)
                        for n in range(48):
                            dst = pim if n % 2 == 0 else ptr
                            if n >= 2:
                                t.wait_ge(sm["a_evv"].sem, 48 * bh + n - 1)
                            if n < 16:
                                t.transpose(dst[:, 0:64], vth[p][:, n * 128:(n + 1) * 128],
                                            identF_sb[0:64, 0:64]).then_inc(sm["a_tpv"].sem, 1); bump(sm["a_tpv"], 1)
                            else:
                                w = n - 16
                                t.transpose(dst[0:64, 0:64], vth[p][:, w * 64:(w + 1) * 64],
                                            identF_sb[0:64, 0:64]).then_inc(sm["a_tpv"].sem, 1); bump(sm["a_tpv"], 1)
                        t.wait_ge(sm["a_msk"].sem, bh + 1)
                        t.wait_ge(sm["a_evv"].sem, 48 * (bh + 1))
                        for qt in range(16):
                            n = 16 * bh + qt
                            dst = pim[:, 0:128] if qt % 2 == 0 else ptr[:, 0:128]
                            if qt > 1:
                                t.wait_ge(sm["a_imev"].sem, n - 1)
                            t.transpose(dst, impsum[:, qt * 128:(qt + 1) * 128],
                                        identF_sb[:, :]).then_inc(sm["a_imtp"].sem, 1); bump(sm["a_imtp"], 1)
                        t.wait_ge(sm["a_bm"].sem, bh + 1)
                        t.wait_ge(sm["a_imev"].sem, 16 * (bh + 1))
                        for qt in range(16):
                            n = 16 * bh + qt
                            dst = pim[:, 0:128] if qt % 2 == 0 else ptr[:, 0:128]
                            if qt > 1:
                                t.wait_ge(sm["a_bmev"].sem, n - 1)
                            t.transpose(dst, BMq[:, qt * 128:(qt + 1) * 128],
                                        identF_sb[:, :]).then_inc(sm["a_bmtp"].sem, 1); bump(sm["a_bmtp"], 1)

            @blk.scalar
            def _(s):
                zrv = zr[:, :].rearrange("p (k qt) -> p k qt", k=6)
                for bh in range(NBH + 1):
                    if bh >= 1:
                        bhp = bh - 1
                        hh = bhp % 8
                        for j in range(16):
                            qc, g = divmod(j, 4)
                            c = 16 * bhp + j
                            s.wait_ge(sm["a_c1"].sem, c + 1)
                            if c >= 1:
                                s.wait_ge(sm["a_co"].sem, c)
                            s.activation(pc_sb[:, :], pcs[c % 2][:, :], AF.Exp,
                                         scale=SCALE).then_inc(sm["a_xc"].sem, 1); bump(sm["a_xc"], 1)
                            s.wait_ge(sm["a_cm"].sem, c + 1)
                            s.activation(pm_sb[0:1, :], pmem[0:1, :], AF.Exp,
                                         scale=SCALE).then_inc(sm["a_xm"].sem, 1); bump(sm["a_xm"], 1)
                            for kc in range(16):
                                n = 16 * c + kc
                                s.wait_ge(sm["a_fs"].sem, n + 1)
                                if c >= 1 or kc >= 8:
                                    s.wait_ge(sm["a_fv"].sem, 16 * c + kc - 7)
                                s.activation(pf[:, (kc % 8) * 512:(kc % 8 + 1) * 512],
                                             pfs[kc % 2][:, :],
                                             AF.Exp, scale=SCALE).then_inc(sm["a_xf"].sem, 1); bump(sm["a_xf"], 1)
                            s.wait_ge(sm["a_ss"].sem, ss_base(c) + ss_nprev(c))
                            if c >= 1:
                                s.wait_ge(sm["a_so"].sem, 8 * c)
                            if qc == 0:
                                s.activation(psl[:, 64:512], pfs[0][0:64, 64:512], AF.Exp,
                                             scale=SCALE).then_inc(sm["a_xs"].sem, 1); bump(sm["a_xs"], 1)
                            else:
                                s.activation(psl[:, :], pfs[0][0:64, :], AF.Exp,
                                             scale=SCALE).then_inc(sm["a_xs"].sem, 1); bump(sm["a_xs"], 1)
                            s.wait_ge(sm["a_ss"].sem, ss_base(c) + ss_nprev(c) + 8)
                            s.activation(pso[:, :], pfs[1][0:64, :], AF.Exp,
                                         scale=SCALE).then_inc(sm["a_xs"].sem, 1); bump(sm["a_xs"], 1)
                    if bh < NBH:
                        s.wait_ge(sm["a_lsh"].sem, bh + 1)
                        s.activation(esh[:, :], esh[:, :], AF.Exp).then_inc(sm["a_es"].sem, 1); bump(sm["a_es"], 1)
                        s.wait_ge(sm["a_mpr"].sem, bh + 1)
                        s.activation(zrv[:, 2, :], zrv[:, 1, :], AF.Exp,
                                     bias=smallc[:, 1:2]).then_inc(sm["a_es2"].sem, 1); bump(sm["a_es2"], 1)
                        s.wait_ge(sm["a_zv"].sem, bh + 1)
                        s.activation(zrv[:, 4, :], zrv[:, 3, :], AF.Ln).then_inc(sm["a_es3"].sem, 1); bump(sm["a_es3"], 1)

            @blk.vector
            def _(ve):
                ve.memset(vKM[:, :].rearrange("p (kc o) -> p kc o", kc=16)[:, :, 64:65],
                          1.0).then_inc(sm["a_ms"].sem, 1); bump(sm["a_ms"], 1)
                ve.memset(vKM2[:, :].rearrange("p (w o) -> p w o", w=32)[:, :, 64:65],
                          1.0).then_inc(sm["a_ms"].sem, 1); bump(sm["a_ms"], 1)
                ec = [0]

                def edge(ins):
                    ins.then_inc(sm["a_e"].sem, 1)
                    ec[0] += 1
                    ve.wait_ge(sm["a_e"].sem, ec[0])
                    return ins

                def tback_evac(cp):
                    bhp, jp = divmod(cp, 16)
                    qcp, gp = divmod(jp, 4)
                    ve.wait_ge(sm["a_tb"].sem, 4 * (cp + 1))
                    if jp == 0 and bhp >= 1:
                        ve.wait_ge(sm["a_st"].sem, 64 * bhp)
                    ve.tensor_copy(asm[gp][:, qcp * 512:(qcp + 1) * 512],
                                   pcs[cp % 2][0:64, :]).then_inc(sm["a_te"].sem, 1); bump(sm["a_te"], 1)

                for bh in range(NBH + 1):
                    b, h = bh_of(bh) if bh < NBH else (0, 0)
                    if bh >= 2:
                        tback_evac(16 * (bh - 1) - 1)
                    if bh < NBH:
                        for j in range(16):
                            qc, g = divmod(j, 4)
                            n = 16 * bh + j
                            if j == 0 and bh >= 1:
                                ve.wait_ge(sm["a_imtp"].sem, 16 * bh)
                                ve.wait_ge(sm["a_es"].sem, bh)
                            ve.wait_ge(sm["a_cs"].sem, n + 1)
                            if g >= 1:
                                ve.wait_ge(sm["a_imp"].sem, n)
                            sl = impsum[:, qc * 512:(qc + 1) * 512]
                            pAv = [pcs[0], pcs[1], pfs[0], pfs[1]][j % 4]
                            if g == 0:
                                ve.tensor_copy(sl, pAv[:, :]).then_inc(sm["a_imp"].sem, 1); bump(sm["a_imp"], 1)
                            else:
                                ve.tensor_add(sl, sl, pAv[:, :]).then_inc(sm["a_imp"].sem, 1); bump(sm["a_imp"], 1)
                        ve.wait_ge(sm["a_imp"].sem, 16 * (bh + 1))
                        edge(ve.scalar_tensor_tensor(impsum[:, :], impsum[:, :], SCALE / G,
                                                     cst["cmaskNEG"][:, :], ALU.mult, ALU.add))
                        ve.sem_inc(sm["a_msk"].sem, 1); bump(sm["a_msk"], 1)
                    if bh >= 1:
                        bhp = bh - 1
                        hh = bhp % 8
                        ve.wait_ge(sm["a_gt"].sem, 16 * (bhp // 8 + 1))
                        for j in range(16):
                            qc, g = divmod(j, 4)
                            c = 16 * bhp + j
                            if c >= 1:
                                ve.wait_ge(sm["a_tr"].sem, 12 * c)
                            ve.wait_ge(sm["a_co"].sem, c + 1)
                            ve.tensor_copy(cnum[0:65, :], pbr[0:65, :]).then_inc(sm["a_bev"].sem, 1); bump(sm["a_bev"], 1)
                            ve.wait_ge(sm["a_fo"].sem, c + 1)
                            ve.wait_ge(sm["a_fv"].sem, 16 * (c + 1))
                            ve.tensor_copy(fnum[0:65, :], pbr[0:65, :]).then_inc(sm["a_bev"].sem, 1); bump(sm["a_bev"], 1)
                            ve.wait_ge(sm["a_so"].sem, 8 * (c + 1))
                            ve.tensor_copy(snum[0:65, :], pbr[0:65, :]).then_inc(sm["a_bev"].sem, 1); bump(sm["a_bev"], 1)
                            if j > 0:
                                tback_evac(c - 1)
                            ptrv = ptr[:, 0:390].rearrange("p (qt br o) -> p qt br o", qt=2, br=3)
                            gv = gtile[:, :].rearrange("p (qt gg) -> p qt gg", qt=16)
                            mv = mbr[:, 0:6].rearrange("p (qt br) -> p qt br", qt=2)
                            for r in range(2):
                                ve.wait_ge(sm["a_tr"].sem, 12 * c + 6 * (r + 1))
                                if r == 0 and c >= 1:
                                    ve.wait_ge(sm["a_tb"].sem, 4 * c)
                                edge(ve.reciprocal(zrec[:, 0:6].rearrange("p (qt br) -> p qt br", qt=2),
                                                   ptrv[:, :, :, 64]))
                                edge(ve.tensor_mul(mv,
                                                   gv[:, 4 * qc + 2 * r:4 * qc + 2 * r + 2,
                                                      12 * hh + 3 * g:12 * hh + 3 * g + 3],
                                                   zrec[:, 0:6].rearrange("p (qt br) -> p qt br", qt=2)))
                                av = acc[:, 128 * r:128 * (r + 1)].rearrange("p (qt d) -> p qt d", qt=2)
                                tv = tmpc[:, 0:128].rearrange("p (qt d) -> p qt d", qt=2)
                                edge(ve.tensor_tensor(av, ptrv[:, :, 0, 0:64],
                                                      mv[:, :, 0].unsqueeze(2).to_broadcast([128, 2, 64]),
                                                      ALU.mult))
                                edge(ve.tensor_tensor(tv, ptrv[:, :, 1, 0:64],
                                                      mv[:, :, 1].unsqueeze(2).to_broadcast([128, 2, 64]),
                                                      ALU.mult))
                                edge(ve.tensor_add(av, av, tv))
                                edge(ve.tensor_tensor(tv, ptrv[:, :, 2, 0:64],
                                                      mv[:, :, 2].unsqueeze(2).to_broadcast([128, 2, 64]),
                                                      ALU.mult))
                                ve.tensor_add(av, av, tv).then_inc(sm["a_cb"].sem, 1); bump(sm["a_cb"], 1)
                    if bh < NBH:
                        for n in range(48):
                            src = pim if n % 2 == 0 else ptr
                            ve.wait_ge(sm["a_tpv"].sem, 48 * bh + n + 1)
                            if n < 16:
                                ve.tensor_copy(vKM[:, n * 65:n * 65 + 64],
                                               src[:, 0:64]).then_inc(sm["a_evv"].sem, 1); bump(sm["a_evv"], 1)
                            else:
                                w = n - 16
                                ve.tensor_copy(vKM2[:, w * 65:w * 65 + 64],
                                               src[0:64, 0:64]).then_inc(sm["a_evv"].sem, 1); bump(sm["a_evv"], 1)
                        for qt in range(16):
                            n = 16 * bh + qt
                            srcp = pim[:, 0:128] if qt % 2 == 0 else ptr[:, 0:128]
                            ve.wait_ge(sm["a_imtp"].sem, n + 1)
                            ve.tensor_copy(impq[:, qt * 128:(qt + 1) * 128],
                                           srcp).then_inc(sm["a_imev"].sem, 1); bump(sm["a_imev"], 1)
                        ve.wait_ge(sm["a_imev"].sem, 16 * (bh + 1))
                        for qt in range(16):
                            edge(ve.max(top8[:, qt * 8:(qt + 1) * 8], impq[:, qt * 128:(qt + 1) * 128]))
                        t8v = top8[:, :].rearrange("p (qt e) -> p qt e", qt=16)
                        zrv = zr[:, :].rearrange("p (k qt) -> p k qt", k=6)
                        iv = impq[:, :].rearrange("p (qt q) -> p qt q", qt=16)
                        ev = esh[:, :].rearrange("p (qt q) -> p qt q", qt=16)
                        edge(ve.tensor_scalar(zrv[:, 0, :], t8v[:, :, 0], -1000.0, None, ALU.max))
                        ve.tensor_scalar(zrv[:, 1, :], zrv[:, 0, :], -1.0, None,
                                         ALU.mult).then_inc(sm["a_mpr"].sem, 1); bump(sm["a_mpr"], 1)
                        ve.wait_ge(sm["a_mpr"].sem, bh + 1)
                        ve.tensor_add(ev, iv, zrv[:, 1, :].unsqueeze(2).to_broadcast([128, 16, 128])
                                      ).then_inc(sm["a_lsh"].sem, 1); bump(sm["a_lsh"], 1)
                        ve.wait_ge(sm["a_es"].sem, bh + 1)
                        edge(ve.tensor_reduce(zrv[:, 3, :].unsqueeze(2), ev, AX.X, ALU.add))
                        ve.wait_ge(sm["a_es2"].sem, bh + 1)
                        ve.tensor_add(zrv[:, 3, :], zrv[:, 3, :], zrv[:, 2, :]).then_inc(sm["a_zv"].sem, 1); bump(sm["a_zv"], 1)
                        ve.wait_ge(sm["a_es3"].sem, bh + 1)
                        edge(ve.tensor_scalar(zrv[:, 5, :], zrv[:, 4, :], float(np.log(1e-10)),
                                              None, ALU.add))
                        edge(ve.tensor_add(zrv[:, 5, :], zrv[:, 5, :], zrv[:, 0, :]))
                        if "bmD" in dbg and bh >= 1:
                            ve.wait_ge(sm["a_bd"].sem, 16 * bh)
                        bv = BMq[:, :].rearrange("p (qt q) -> p qt q", qt=16)
                        sv = ev
                        edge(ve.tensor_tensor(bv, iv, t8v[:, :, 7].unsqueeze(2).to_broadcast(
                            [128, 16, 128]), ALU.is_ge))
                        edge(ve.tensor_tensor(sv, iv, zrv[:, 5, :].unsqueeze(2).to_broadcast(
                            [128, 16, 128]), ALU.is_gt))
                        ve.tensor_mul(BMq[:, :], BMq[:, :], esh[:, :]).then_inc(sm["a_bm"].sem, 1); bump(sm["a_bm"], 1)
                        for qt in range(16):
                            n = 16 * bh + qt
                            srcp = pim[:, 0:128] if qt % 2 == 0 else ptr[:, 0:128]
                            ve.wait_ge(sm["a_bmtp"].sem, n + 1)
                            if bh >= 1 and qt == 0:
                                ve.wait_ge(sm["a_fs"].sem, 256 * bh)
                            ve.tensor_scalar(BMT1[:, qt * 128:(qt + 1) * 128], srcp, 1.0,
                                             None, ALU.subtract).then_inc(sm["a_bmev"].sem, 1); bump(sm["a_bmev"], 1)
                tback_evac(16 * NBH - 1)

        full_barrier()

    if "wout" in stages:
        with ExitStack() as sc:
            wo = sc.enter_context(nc.sbuf_tensor("wo_w", [128, 16 * 16 * 128], BF16))
            am = [sc.enter_context(nc.sbuf_tensor("wo_a0", [128, 16 * 512], BF16))] * 2
            wb = sc.enter_context(nc.sbuf_tensor("wo_wb", [128, 16 * 512], F32))
            xt = [sc.enter_context(nc.sbuf_tensor(f"wo_x{i}", [128, DM], F32)) for i in range(2)]
            x1r = sc.enter_context(nc.sbuf_tensor("wo_xr", [128, DM], F32))
            x1l = sc.enter_context(nc.sbuf_tensor("wo_xl", [128, DM], F32))
            stl = sc.enter_context(nc.sbuf_tensor("wo_st", [128, 32], F32))
            txb = [sc.enter_context(nc.sbuf_tensor("wo_t0", [128, DM], BF16))] * 2
            pw = [P0, P1]
            pt = [PIM, PTR]
            pl = [P2, P3]
            w_w = rsem("wo_w2")
            w_am = [rsem("qk_x0"), rsem("qk_x1")]
            w_xl = [rsem("ln_ld0"), rsem("ln_ld1")]
            w_pe = rsem("qk_pe")
            w_ev = rsem("qk_ev")
            w_tp = rsem("ln_tp")
            w_te = rsem("ln_ev")
            w_m = rsem("ln_m")
            w_var = rsem("ln_var")
            w_xlr = rsem("ln_xl")
            w_ltp = rsem("qk_tg")
            w_lev = rsem("qk_eg")
            w_st = [rsem("ln_st0"), rsem("ln_st1")]
            w_e = rsem("ln_e")
            NI = NB * 4

            @blk.gpsimd
            def _(g_):
                for oc in range(16):
                    dma(g_, wo[:, oc * 2048:(oc + 1) * 2048].rearrange("p (ic m) -> p ic m", ic=16),
                        w_out[:, oc * 128:(oc + 1) * 128].rearrange("(ic p) m -> p ic m", p=128), w_w)

            @blk.sync
            def _(sy):
                for it in range(NI):
                    seq, tq = divmod(it, 4)
                    if it >= 1:
                        sy.wait_ge(w_pe.sem, w_pe.base + 16 * it)
                    for hf in range(2):
                        dma(sy, am[it % 2][hf * 64:hf * 64 + 64, :].rearrange("p (ic t) -> p ic t", ic=16),
                            attnFM[seq, hf:32:2, :, tq * 512:(tq + 1) * 512].transpose([1, 0, 2]),
                            w_am[0])
                    for tt in range(4):
                        n = 4 * it + tt
                        if n >= 2:
                            sy.wait_ge(w_xlr.sem, w_xlr.base + n - 1)
                        dma(sy, xt[n % 2][:, :],
                            x_in[seq, tq * 512 + tt * 128:tq * 512 + (tt + 1) * 128, :],
                            w_xl[n % 2])
                        sy.wait_ge(w_lev.sem, w_lev.base + 16 * (n + 1))
                        sy.wait_ge(w_xlr.sem, w_xlr.base + n + 1)
                        tg = tq * 512 + tt * 128
                        dma(sy, xln2T[seq, :, :, tg:tg + 128].transpose([1, 0, 2]),
                            txb[n % 2][:, :].rearrange("p (dc t) -> p dc t", dc=16), w_st[n % 2])
                        dma(sy, x1D[seq, tg:tg + 128, :], x1r[:, :], w_st[n % 2])
                for i in range(2):
                    sy.wait_ge(w_st[i].sem, w_st[i].val)

            @blk.tensor
            def _(t):
                t.wait_ge(w_w.sem, w_w.val)
                for it in range(NI):
                    seq, tq = divmod(it, 4)
                    t.wait_ge(w_am[0].sem, w_am[0].base + 32 * (it + 1))
                    amv = am[it % 2][:, :].rearrange("p (ic t) -> p ic t", ic=16)
                    for oc in range(16):
                        n = 16 * it + oc
                        if n >= 2:
                            t.wait_ge(w_ev.sem, w_ev.base + n - 1)
                        for ic in range(16):
                            mm = t.matmul(pw[n % 2][:, :],
                                          wo[:, (oc * 16 + ic) * 128:(oc * 16 + ic + 1) * 128],
                                          amv[:, ic, :], start=(ic == 0), stop=(ic == 15))
                        mm.then_inc(w_pe.sem, 1); bump(w_pe, 1)
                    for tt in range(4):
                        n = 4 * it + tt
                        t.wait_ge(w_ev.sem, w_ev.base + 16 * (it + 1))
                        for oc in range(16):
                            k = 16 * n + oc
                            if k >= 2:
                                t.wait_ge(w_te.sem, w_te.base + k - 1)
                            t.transpose(pt[k % 2][:, 0:128],
                                        wb[:, (oc * 4 + tt) * 128:(oc * 4 + tt + 1) * 128],
                                        identF_sb[:, :]).then_inc(w_tp.sem, 1); bump(w_tp, 1)
                        t.wait_ge(w_xlr.sem, w_xlr.base + n + 1)
                        for dc in range(16):
                            k = 16 * n + dc
                            if k >= 2:
                                t.wait_ge(w_lev.sem, w_lev.base + k - 1)
                            t.transpose(pl[k % 2][:, 0:128], x1l[:, dc * 128:(dc + 1) * 128],
                                        identF_sb[:, :]).then_inc(w_ltp.sem, 1); bump(w_ltp, 1)

            @blk.vector
            def _(ve):
                ec = [0]

                def edge(ins):
                    ins.then_inc(w_e.sem, 1); bump(w_e, 1)
                    ec[0] += 1
                    ve.wait_ge(w_e.sem, w_e.base + ec[0])

                edge(ve.memset(stl[:, :], 0.0))
                for it in range(NI):
                    for oc in range(16):
                        n = 16 * it + oc
                        ve.wait_ge(w_pe.sem, w_pe.base + n + 1)
                        ve.tensor_copy(wb[:, oc * 512:(oc + 1) * 512],
                                       pw[n % 2][:, :]).then_inc(w_ev.sem, 1); bump(w_ev, 1)
                    for tt in range(4):
                        n = 4 * it + tt
                        ve.wait_ge(w_xl[n % 2].sem, w_xl[n % 2].base + 16 * (n // 2 + 1))
                        if n >= 1:
                            ve.wait_ge(w_xlr.sem, w_xlr.base + n)
                        for oc in range(16):
                            k = 16 * n + oc
                            ve.wait_ge(w_tp.sem, w_tp.base + k + 1)
                            if n >= 1 and oc == 0:
                                ve.wait_ge(w_st[(n - 1) % 2].sem, w_st[(n - 1) % 2].base + 32 * ((n - 1) // 2 + 1))
                            ve.tensor_add(x1r[:, oc * 128:(oc + 1) * 128], pt[k % 2][:, 0:128],
                                          xt[n % 2][:, oc * 128:(oc + 1) * 128]
                                          ).then_inc(w_te.sem, 1); bump(w_te, 1)
                        for cch in range(4):
                            edge(ve.bn_stats(stl[:, 6 * cch:6 * cch + 6],
                                             x1r[:, 512 * cch:512 * (cch + 1)]))
                        edge(ve.bn_aggr(stl[:, 24:26], stl[:, 0:12]))
                        edge(ve.bn_aggr(stl[:, 26:28], stl[:, 12:24]))
                        edge(ve.tensor_tensor(stl[:, 2:3], stl[:, 24:25], stl[:, 26:27], ALU.add))
                        edge(ve.tensor_scalar(stl[:, 2:3], stl[:, 2:3], 0.5, None, ALU.mult))
                        edge(ve.tensor_tensor(stl[:, 3:4], stl[:, 24:25], stl[:, 26:27],
                                              ALU.subtract))
                        edge(ve.tensor_scalar(stl[:, 3:4], stl[:, 3:4], 0.5, None, ALU.mult))
                        edge(ve.tensor_scalar(stl[:, 3:4], stl[:, 3:4], stl[:, 3:4], None, ALU.mult))
                        edge(ve.tensor_tensor(stl[:, 4:5], stl[:, 25:26], stl[:, 27:28], ALU.add))
                        ve.tensor_scalar(stl[:, 4:5], stl[:, 4:5], 0.5, stl[:, 3:4],
                                          ALU.mult, ALU.add).then_inc(w_m.sem, 1); bump(w_m, 1)
                        ve.wait_ge(w_var.sem, w_var.base + n + 1)
                        edge(ve.reciprocal(stl[:, 6:7], stl[:, 5:6]))
                        ve.tensor_scalar(x1l[:, :], x1r[:, :], stl[:, 2:3], stl[:, 6:7],
                                         ALU.subtract, ALU.mult).then_inc(w_xlr.sem, 1); bump(w_xlr, 1)
                        for dc in range(16):
                            k = 16 * n + dc
                            ve.wait_ge(w_ltp.sem, w_ltp.base + k + 1)
                            if n >= 1 and dc == 0:
                                ve.wait_ge(w_st[(n - 1) % 2].sem, w_st[(n - 1) % 2].base + 32 * ((n - 1) // 2 + 1))
                            ve.tensor_copy(txb[n % 2][:, dc * 128:(dc + 1) * 128],
                                           pl[k % 2][:, 0:128]).then_inc(w_lev.sem, 1); bump(w_lev, 1)

            @blk.scalar
            def _(s):
                s.wait_ge(w_e.sem, w_e.base + 1)
                for it in range(NI):
                    for tt in range(4):
                        n = 4 * it + tt
                        s.wait_ge(w_m.sem, w_m.base + n + 1)
                        s.activation(stl[:, 5:6], stl[:, 4:5], AF.Sqrt, bias=smallc[:, 0:1],
                                     scale=1.0).then_inc(w_var.sem, 1); bump(w_var, 1)

        full_barrier()

    if "ffna" in stages:
        with ExitStack() as sc:
            xr = sc.enter_context(nc.sbuf_tensor("fa_x", [128, 16 * NB * S], BF16))
            wgs = [sc.enter_context(nc.sbuf_tensor(f"fa_wg{i}", [128, 16 * 128], BF16)) for i in range(2)]
            wus = [sc.enter_context(nc.sbuf_tensor(f"fa_wu{i}", [128, 16 * 128], BF16)) for i in range(2)]
            gsil = sc.enter_context(nc.sbuf_tensor("fa_gs", [128, 512], F32))
            gbuf = [sc.enter_context(nc.sbuf_tensor("fa_gb0", [128, NB * S], BF16))] * 2
            pg = [P0, P1]
            pu = [P2, P3]
            f_x = rsem("cm_ld")
            f_w = [rsem("fa_w0"), rsem("fa_w1")]
            f_pg = rsem("ro_pe")
            f_pu = rsem("ro_t2")
            f_si = rsem("ro_dv")
            f_gt = rsem("cm_mm")
            f_st = [rsem("fa_st0"), rsem("fa_st1")]

            @blk.sync
            def _(sy):
                for seq in range(NB):
                    for dc in range(16):
                        dma(sy, xr[:, (dc * NB + seq) * S:(dc * NB + seq + 1) * S],
                            xln2T[seq, dc, :, :], f_x)

            @blk.gpsimd
            def _(g_):
                for hc in range(65):
                    if hc < 64:
                        if hc >= 2:
                            g_.wait_ge(f_pu.sem, f_pu.base + 8 * (hc - 1))
                        dma(g_, wgs[hc % 2][:, :].rearrange("p (ic m) -> p ic m", ic=16),
                            ffn_wgT[:, hc * 128:(hc + 1) * 128].rearrange("(ic p) m -> p ic m", p=128),
                            f_w[hc % 2])
                        dma(g_, wus[hc % 2][:, :].rearrange("p (ic m) -> p ic m", ic=16),
                            ffn_wuT[:, hc * 128:(hc + 1) * 128].rearrange("(ic p) m -> p ic m", p=128),
                            f_w[hc % 2])
                    if hc >= 1:
                        g_.wait_ge(f_gt.sem, f_gt.base + 8 * hc)
                        dma(g_, gatedD[hc - 1, :, :], gbuf[(hc - 1) % 2][:, :], f_st[(hc - 1) % 2])
                for i in range(2):
                    g_.wait_ge(f_st[i].sem, f_st[i].val)

            @blk.tensor
            def _(t):
                t.wait_ge(f_x.sem, f_x.val)
                for hc in range(64):
                    t.wait_ge(f_w[hc % 2].sem, f_w[hc % 2].base + 32 * (hc // 2 + 1))
                    for tb in range(8):
                        n = 8 * hc + tb
                        if n >= 2:
                            t.wait_ge(f_si.sem, f_si.base + n - 1)
                        for ic in range(16):
                            mm = t.matmul(pg[n % 2][:, :], wgs[hc % 2][:, ic * 128:(ic + 1) * 128],
                                          xr[:, :].rearrange("p (ic t) -> p ic t", ic=16)[:, ic,
                                          tb * 512:(tb + 1) * 512],
                                          start=(ic == 0), stop=(ic == 15))
                        mm.then_inc(f_pg.sem, 1); bump(f_pg, 1)
                        if n >= 2:
                            t.wait_ge(f_gt.sem, f_gt.base + n - 1)
                        for ic in range(16):
                            mm = t.matmul(pu[n % 2][:, :], wus[hc % 2][:, ic * 128:(ic + 1) * 128],
                                          xr[:, :].rearrange("p (ic t) -> p ic t", ic=16)[:, ic,
                                          tb * 512:(tb + 1) * 512],
                                          start=(ic == 0), stop=(ic == 15))
                        mm.then_inc(f_pu.sem, 1); bump(f_pu, 1)

            @blk.scalar
            def _(s):
                for hc in range(64):
                    for tb in range(8):
                        n = 8 * hc + tb
                        s.wait_ge(f_pg.sem, f_pg.base + n + 1)
                        if n >= 1:
                            s.wait_ge(f_gt.sem, f_gt.base + n)
                        s.activation(gsil[:, :], pg[n % 2][:, :], AF.Silu).then_inc(f_si.sem, 1); bump(f_si, 1)

            @blk.vector
            def _(ve):
                for hc in range(64):
                    for tb in range(8):
                        n = 8 * hc + tb
                        ve.wait_ge(f_si.sem, f_si.base + n + 1)
                        ve.wait_ge(f_pu.sem, f_pu.base + n + 1)
                        if hc >= 1 and tb == 0:
                            ve.wait_ge(f_st[(hc - 1) % 2].sem,
                                       f_st[(hc - 1) % 2].base + 16 * ((hc - 1) // 2 + 1))
                        ve.tensor_mul(gbuf[hc % 2][:, tb * 512:(tb + 1) * 512], gsil[:, :],
                                      pu[n % 2][:, :]).then_inc(f_gt.sem, 1); bump(f_gt, 1)

        full_barrier()

    if "ffnb" in stages:
        with ExitStack() as sc:
            gr = sc.enter_context(nc.sbuf_tensor("fb_g", [128, 64 * 512], BF16))
            wds = [sc.enter_context(nc.sbuf_tensor(f"fb_w{i}", [128, 64 * 128], BF16)) for i in range(2)]
            ob = sc.enter_context(nc.sbuf_tensor("fb_o", [128, 16 * 512], F32))
            x1t = [sc.enter_context(nc.sbuf_tensor("fb_x0", [128, DM], F32))] * 2
            orow = [sc.enter_context(nc.sbuf_tensor(f"fb_or{i}", [128, DM], F32)) for i in range(2)]
            pd = [P0, P1]
            pt2 = [PIM, PTR]
            b_g = rsem("cm_ld")
            b_w = [rsem("fb_w0"), rsem("fb_w1")]
            b_x = [rsem("ln_ld0"), rsem("ln_ld1")]
            b_pe = rsem("qk_pe")
            b_ev = rsem("qk_ev")
            b_tp = rsem("ln_tp")
            b_te = rsem("ln_ev")
            b_st = [rsem("ln_st0"), rsem("ln_st1")]
            NSH = 8

            @blk.sync
            def _(sy):
                for sh in range(NSH):
                    if sh >= 1:
                        sy.wait_ge(b_pe.sem, b_pe.base + 16 * sh)
                    for q4_ in range(2):
                        hc0 = q4_ * 16
                        dma(sy, gr[:, hc0 * 512:(hc0 + 16) * 512].rearrange(
                                "p (hc t) -> p hc t", hc=16),
                            gatedD[hc0:hc0 + 16, :, sh * 512:(sh + 1) * 512].transpose([1, 0, 2]),
                            b_g)
                    for tt in range(4):
                        n = 4 * sh + tt
                        tg = sh * 512 + tt * 128
                        seq, trow = divmod(tg, S)
                        if n >= 1:
                            sy.wait_ge(b_te.sem, b_te.base + 16 * n)
                        dma(sy, x1t[n % 2][:, :], x1D[seq, trow:trow + 128, :], b_x[n % 2])
                        sy.wait_ge(b_te.sem, b_te.base + 16 * (n + 1))
                        dma(sy, out[seq, trow:trow + 128, :], orow[n % 2][:, :], b_st[n % 2])
                for i in range(2):
                    sy.wait_ge(b_st[i].sem, b_st[i].val)

            @blk.scalar
            def _(s):
                for sh in range(NSH):
                    if sh >= 1:
                        s.wait_ge(b_pe.sem, b_pe.base + 16 * sh)
                    for q4_ in range(2):
                        hc0 = 32 + q4_ * 16
                        dma(s, gr[:, hc0 * 512:(hc0 + 16) * 512].rearrange(
                                "p (hc t) -> p hc t", hc=16),
                            gatedD[hc0:hc0 + 16, :, sh * 512:(sh + 1) * 512].transpose([1, 0, 2]),
                            b_g)

            @blk.gpsimd
            def _(g_):
                g_.wait_ge(sem("a_wdb").sem, 128)
                for sh in range(NSH):
                    for oc in range(16):
                        n = 16 * sh + oc
                        if n >= 2:
                            g_.wait_ge(b_pe.sem, b_pe.base + n - 1)
                        dma(g_, wds[n % 2][:, :].rearrange("p (hc m) -> p hc m", hc=64),
                            wdB[:, oc * 128:(oc + 1) * 128].rearrange("(hc p) m -> p hc m", p=128),
                            b_w[n % 2])

            @blk.tensor
            def _(t):
                for sh in range(NSH):
                    t.wait_ge(b_g.sem, b_g.base + 4 * 16 * (sh + 1))
                    for oc in range(16):
                        n = 16 * sh + oc
                        t.wait_ge(b_w[n % 2].sem, b_w[n % 2].base + 16 * (n // 2 + 1))
                        if n >= 2:
                            t.wait_ge(b_ev.sem, b_ev.base + n - 1)
                        for hc in range(64):
                            mm = t.matmul(pd[n % 2][:, :],
                                          wds[n % 2][:, hc * 128:(hc + 1) * 128],
                                          gr[:, :].rearrange("p (hc t) -> p hc t", hc=64)[:, hc, :],
                                          start=(hc == 0), stop=(hc == 63))
                        mm.then_inc(b_pe.sem, 1); bump(b_pe, 1)
                    for tt in range(4):
                        n = 4 * sh + tt
                        t.wait_ge(b_ev.sem, b_ev.base + 16 * (sh + 1))
                        for oc in range(16):
                            k = 16 * n + oc
                            if k >= 2:
                                t.wait_ge(b_te.sem, b_te.base + k - 1)
                            t.transpose(pt2[k % 2][:, 0:128],
                                        ob[:, (oc * 4 + tt) * 128:(oc * 4 + tt) * 128 + 128],
                                        identF_sb[:, :]).then_inc(b_tp.sem, 1); bump(b_tp, 1)

            @blk.vector
            def _(ve):
                for sh in range(NSH):
                    for oc in range(16):
                        n = 16 * sh + oc
                        ve.wait_ge(b_pe.sem, b_pe.base + n + 1)
                        if sh >= 1 and oc == 0:
                            ve.wait_ge(b_tp.sem, b_tp.base + 64 * sh)
                        ve.tensor_copy(ob[:, oc * 512:(oc + 1) * 512],
                                       pd[n % 2][:, :]).then_inc(b_ev.sem, 1); bump(b_ev, 1)
                    for tt in range(4):
                        n = 4 * sh + tt
                        ve.wait_ge(b_x[n % 2].sem, b_x[n % 2].base + 16 * (n // 2 + 1))
                        for oc in range(16):
                            k = 16 * n + oc
                            ve.wait_ge(b_tp.sem, b_tp.base + k + 1)
                            if n >= 2 and oc == 0:
                                ve.wait_ge(b_st[n % 2].sem, b_st[n % 2].base + 16 * (n // 2))
                            ve.tensor_add(orow[n % 2][:, oc * 128:(oc + 1) * 128], pt2[k % 2][:, 0:128],
                                          x1t[n % 2][:, oc * 128:(oc + 1) * 128]
                                          ).then_inc(b_te.sem, 1); bump(b_te, 1)

        full_barrier()

    if "ffnb" not in stages:
        s_fin = sem("fin")
        zb = ctx.enter_context(nc.sbuf_tensor("zb", [128, DM], F32))

        @blk.vector
        def _(ve):
            ve.memset(zb[:, :], 0.0).then_inc(s_fin.sem, 1)
        s_fin.val += 1

        @blk.sync
        def _(sy):
            sy.wait_ge(s_fin.sem, s_fin.val)
            for seq in range(NB):
                for tc in range(16):
                    dma(sy, out[seq, tc * 128:(tc + 1) * 128, :], zb[:, :], s_fin)
            sy.wait_ge(s_fin.sem, s_fin.val)

    ctx.close()
    nc.finalize()
    return nc


def prep_inputs(inputs):
    c = make_consts()
    w = {
        "w_qkv": inputs["L0_attn_w_qkv"],
        "w_gate": inputs["L0_attn_w_gate"],
        "b_gate": np.ascontiguousarray(inputs["L0_attn_b_gate"].reshape(96, 1)),
        "w_out": inputs["L0_attn_w_out"],
        "k_cw1": inputs["L0_attn_k_cw1"],
        "k_cb1": np.ascontiguousarray(inputs["L0_attn_k_cb1"].reshape(CDIM, 1)),
        "k_cw2": inputs["L0_attn_k_cw2"],
        "k_cb2": np.ascontiguousarray(inputs["L0_attn_k_cb2"].reshape(DH, 1)),
        "v_cw1": inputs["L0_attn_v_cw1"],
        "v_cb1": np.ascontiguousarray(inputs["L0_attn_v_cb1"].reshape(CDIM, 1)),
        "v_cw2": inputs["L0_attn_v_cw2"],
        "v_cb2": np.ascontiguousarray(inputs["L0_attn_v_cb2"].reshape(DH, 1)),
        "kposT": np.ascontiguousarray(inputs["L0_attn_k_pos"].transpose(0, 2, 1)),
        "vposT": np.ascontiguousarray(inputs["L0_attn_v_pos"].transpose(0, 2, 1)),
        "memckT": np.ascontiguousarray(inputs["L0_attn_mem_ck"].reshape(KVH, DH).T),
        "memcv": np.ascontiguousarray(inputs["L0_attn_mem_cv"].reshape(KVH, DH)),
        "ffn_wgT": np.ascontiguousarray(inputs["L0_ffn_w_gate"].T),
        "ffn_wuT": np.ascontiguousarray(inputs["L0_ffn_w_up"].T),
        "ffn_wdT": np.ascontiguousarray(inputs["L0_ffn_w_down"].T),
    }
    w.update(c)
    return w


def kernel(**inputs):
    x = inputs["L0_attn_hidden_states"]
    shared = prep_inputs(inputs)
    nc = build()
    in_maps = []
    for core in range(8):
        m = dict(shared)
        m["x"] = np.ascontiguousarray(x[2 * core:2 * core + 2])
        in_maps.append(m)
    res = run_bass_kernel_spmd(nc, in_maps, core_ids=list(range(8)))
    return np.concatenate([r["out"] for r in res.results], axis=0)
